# version 07
# baseline (speedup 1.0000x reference)
import numpy as np
import ml_dtypes
import concourse.bass as bass
import concourse.mybir as mybir
from concourse.bass_utils import run_bass_kernel_spmd

F32 = mybir.dt.float32
BF16 = mybir.dt.bfloat16
AF = mybir.ActivationFunctionType
ALU = mybir.AluOpType
BF = ml_dtypes.bfloat16

NCORES = 8
BSEQ = 4
TFULL = 2048
D = 1024
H, DK, DV = 8, 64, 128
RLOW = 64
FF = 4096
CT = 128
SB = 16
NSB = CT // SB
NT = BSEQ * CT
NTH = NT // 2
EPS = 1e-6
LNEPS = 1e-5

ENGINES = ("tensor", "vector", "scalar", "gpsimd", "sync")

WSHAPES = {
    "wq": (D, H * DK), "wk": (D, H * DK), "wv": (D, H * DV),
    "wsp": (D, H * DK), "wau": (D, RLOW), "wad": (RLOW, H * DK),
    "wbeta": (D, H), "wu1": (D, RLOW), "wu2": (RLOW, D),
    "wout": (H * DV, D), "wff1": (D, FF), "wff3": (D, FF), "wff2": (FF, D),
    "w1t": (D, FF), "w2t": (D, FF), "h1t": (FF, FF), "h2t": (FF, FF),
    "w3t": (FF, D),
}

SMALL_SHAPES = {
    "wni": [128, 8], "convb": [128, 8], "convw": [128, 32],
    "bau": [64, 1], "bad": [128, 4], "bbeta": [8, 1], "bu1": [64, 1],
    "bu2": [128, 8], "wffn": [128, 8], "whn": [128, 8],
    "s1": [128, 32], "s2": [128, 32], "hs1": [128, 32], "hs2": [128, 32],
    "s3": [128, 8],
    "asp": [128, 4], "bseg8": [128, 32],
    "ident": [128, 128], "maskM": [128, 128], "maskG": [128, 128],
    "seg8": [128, 32], "headmap": [8, 512], "onesrow": [1, 128],
}


class Sched:
    def __init__(self, nc):
        self.nc = nc
        self.items = {e: [] for e in ENGINES}
        self.sems = {}
        self.counts = {}
        self.last_wait = {e: {} for e in ENGINES}

    def sem(self, name):
        if name not in self.sems:
            self.sems[name] = self.nc.alloc_semaphore(name)
            self.counts[name] = 0
        return self.sems[name]

    SELF_SEM = {"vector": "V", "scalar": "S", "gpsimd": "G"}

    def op(self, engine, fn, waits=(), inc=None, n=1, tag="", chain=False):
        waits = list(waits)
        ssem = self.SELF_SEM.get(engine)
        if chain and ssem is not None:
            self.sem(ssem)
            waits.insert(0, (ssem, self.counts[ssem]))
        tok = None
        if inc is not None:
            self.sem(inc)
            self.counts[inc] += n
            tok = (inc, self.counts[inc])
        lw = self.last_wait[engine]
        eff = []
        for w in waits:
            if w is None:
                continue
            sname, v = w
            self.sem(sname)
            if v > lw.get(sname, 0):
                eff.append((sname, v))
                lw[sname] = v
        self.items[engine].append((fn, tuple(eff), tok, n, tag))
        return tok

    def replay(self, engine, eng):
        for fn, waits, tok, n, tag in self.items[engine]:
            for (sname, v) in waits:
                eng.wait_ge(self.sems[sname], v)
            inst = fn(eng)
            if tok is not None:
                if inst is None:
                    raise RuntimeError(f"inc-op returned None: {tag}")
                inst.then_inc(self.sems[tok[0]], n)


class Bank:
    def __init__(self, ps, name):
        self.ps = ps
        self.psb = ps[:].bitcast(mybir.dt.bfloat16)
        self.name = name
        self.free = None
        self.ready = None


def ceil_div(a, b):
    return -(-a // b)


def build_program(T=TFULL, dbg=()):
    nc = bass.Bass("TRN2", target_bir_lowering=False)
    ntiles = T // CT

    ntl = T // CT
    x_fm = nc.dram_tensor("x_fm", [8, 128, ntl, BSEQ, CT], F32,
                          kind="ExternalInput").ap()
    out_fm = nc.dram_tensor("out_fm", [8, 128, ntl, BSEQ, CT], F32,
                            kind="ExternalOutput").ap()
    w_in = {k: nc.dram_tensor("w_" + k, list(s), F32, kind="ExternalInput").ap()
            for k, s in WSHAPES.items()}
    stripe_index = {}
    stripe_list = []
    _ns = [0]

    def _plan(wname, group=2, special=None):
        K, O = WSHAPES[wname]
        kchunks = ceil_div(K, 128)
        otiles = ceil_div(O, 128)
        ents = []
        if special == "v":
            for kq in range(0, kchunks, 2):
                ents.append((kq, 2, 0, 1024))
        elif special == "wo":
            for ot in range(otiles):
                ents.append((0, kchunks, ot, 128))
        else:
            for og0 in range(0, otiles, group):
                ow = sum(min(128, O - ot * 128)
                         for ot in range(og0, min(og0 + group, otiles)))
                kstr = max(1, 2048 // ow) if K % 128 == 0 else 1
                for kq in range(0, kchunks, kstr):
                    nk = min(kstr, kchunks - kq)
                    ents.append((kq, nk, og0, ow))
        for (kq, nk, og0, ow) in ents:
            rows = min(128, K - kq * 128)
            si = _ns[0]
            _ns[0] += 1
            stripe_index[(wname, og0, kq)] = (si, nk, ow, rows)
            stripe_list.append((wname, si, kq, nk, og0, ow, rows))

    for _w in ("wq", "wk", "wsp"):
        _plan(_w)
    _plan("wv", special="v")
    for _w in ("wau", "wad", "wbeta", "wu1", "wu2"):
        _plan(_w)
    _plan("wout", special="wo")
    for _w in ("w1t", "w2t", "h1t", "h2t", "w3t", "wff1", "wff3", "wff2"):
        _plan(_w)
    wstripes = nc.dram_tensor("wstripes", [_ns[0], 128, 2048], BF16,
                              kind="Internal").ap()
    _BF_SMALL = ("maskM", "maskG")
    sm_in = {k: nc.dram_tensor("p_" + k, s, BF16 if k in _BF_SMALL else F32,
                               kind="ExternalInput").ap()
             for k, s in SMALL_SHAPES.items()}

    sbuf = lambda n, shape, dt: nc.alloc_sbuf_tensor(n, shape, dt)
    P = {k: sbuf("sp_" + k, SMALL_SHAPES[k],
                 BF16 if k in _BF_SMALL else F32) for k in SMALL_SHAPES}
    P["onescol"] = P["maskG"][:, 127:128]
    ident_bf = sbuf("ident_bf", [128, 128], BF16)
    seg8_bf = sbuf("seg8_bf", [128, 32], BF16)
    bseg8_bf = sbuf("bseg8_bf", [128, 32], BF16)


    NPOOL = 6
    POOLW = 2048
    pool = [sbuf(f"pool_{i}", [128, POOLW], BF16) for i in range(NPOOL)]
    pool_state = [None] * NPOOL
    pool_i = [0]

    OUT = sbuf("OUT", [128, 8, NT], F32)
    xn = sbuf("xn", [128, 8, NT], F32)
    xn_bf = sbuf("xn_bf", [128, 8, NT], BF16)
    scr2 = sbuf("scr2", [128, 2, NT], BF16)
    h1h = sbuf("h1h", [128, 8, BSEQ, CT + 3], BF16)
    convacc = sbuf("convacc", [128, BSEQ, CT], F32)
    gate_sc = convacc[:].rearrange("p s t -> p (s t)")
    hc_bf = sbuf("hc_bf", [128, 8, NT], BF16)
    y_bf = hc_bf
    y2 = sbuf("y2", [128, 8, NT], F32)
    qf = y2[:, 0:4, :]
    kf = y2[:, 4:8, :]
    qn_bf = sbuf("qn_bf", [128, 4, NT], BF16)
    kn_bf = sbuf("kn_bf", [128, 4, NT], BF16)
    v_tm = sbuf("v_tm", [128, BSEQ, H * DV], BF16)
    au_bf = sbuf("au_bf", [64, NT], BF16)
    g1_bf = sbuf("g1_bf", [64, NT], BF16)
    bb8 = sbuf("bb8", [8, NT], F32)
    active8 = sbuf("active8", [8, NT], F32)
    beta8 = bb8
    beta_tok = sbuf("beta_tok", [128, BSEQ, 8], F32)
    nq8 = sbuf("nq8", [8, NT], F32)
    nk8 = nq8
    stat_t = active8[0:1, :]
    stat_v = bb8[0:1, :]
    memst = sbuf("memst", [128, 16], F32)
    sp_tmp = sbuf("sp_tmp", [128, 16], F32)
    apad = sbuf("apad", [128, BSEQ, NSB, SB + 1], F32)
    rho = sbuf("rho", [128, BSEQ, NSB, SB + 1], BF16)
    pcum = sbuf("pcum", [128, BSEQ, NSB, SB + 1], F32)
    rrev = sbuf("rrev", [128, BSEQ, NSB, SB + 1], BF16)
    rhorec = sbuf("rhorec", [128, BSEQ, NSB, SB + 1], BF16)
    CHOFF = [8 * b * b + 8 * b for b in range(NSB + 1)]
    chist = sbuf("chist", [128, BSEQ, CHOFF[NSB]], BF16)
    shA = sbuf("shA", [128, 4096], BF16)
    ab_bf = shA[:, 0:2048].rearrange("p (g t) -> p g t", g=4)
    spike_fm = shA[:, 2048:4096].rearrange("p (g t) -> p g t", g=4)
    kqh = y2[:, 0:4, :].rearrange("p a b -> p (a b)").bitcast(BF16).rearrange(
        "p (g s b j) -> p g s b j", g=4, s=BSEQ, b=NSB)
    shB = sbuf("shB", [128, 4096], BF16)
    sd_scan = shB[:].bitcast(F32).rearrange("p (t x) -> p t x", t=CT)
    khat = shB[:, 0:2048].rearrange("p (g t) -> p g t", g=4)
    qtl = shB[:, 2048:4096].rearrange("p (g t) -> p g t", g=4)
    shC = sbuf("shC", [128, 2048], BF16)
    spike_scan = shC[:].rearrange("p (t x) -> p t x", t=CT)
    kap = shC[:].rearrange("p (g t) -> p g t", g=4)
    S_all = sbuf("S_all", [128, 16, DV], F32)
    S_bf = sbuf("S_bf", [128, DV], BF16)
    Mhat = sbuf("Mhat", [128, 128], BF16)
    Ghat = sbuf("Ghat", [128, 128], BF16)
    Xp = [Mhat] + [sbuf(f"Xp_{i}", [128, 128], BF16) for i in range(1, 7)]
    Yp = [sbuf(f"Yp_{i}", [128, 128], BF16) for i in range(6)]
    Rping = [sbuf(f"Rp_{i}", [128, 128], BF16) for i in range(2)]
    khatT = sbuf("khatT", [128, 64], BF16)
    Ocomb = sbuf("Ocomb", [128, 128], F32)
    On_bf = sbuf("On_bf", [128, 128], BF16)
    rms_col = sbuf("rms_col", [128, 4], F32)
    a_big = sbuf("a_big", [128, 32, NTH], BF16)
    f2_bf = sbuf("f2_bf", [128, 32, NTH], BF16)
    stat_m = nq8[0:1, :]

    banks = [Bank(nc.alloc_psum_tensor(f"bank{i}", [128, 512], F32), f"b{i}")
             for i in range(8)]
    B_MAIN = banks[0:4]
    B_BC = banks[4:6]
    B_BUILD = banks[6]
    B_S1, B_S2 = banks[4], banks[5]
    B_MISC = banks[7]

    s = Sched(nc)

    dbg_out = {}
    dbg_shapes = {
        "xn": [128, 8, NT], "hc": [128, 8, NT], "q": [128, 4, NT],
        "k": [128, 4, NT], "sd": [128, CT, 16], "ab": [128, 4, NT],
        "spike": [128, 4, NT], "alpha": [128, BSEQ, NSB, SB + 1],
        "beta": [8, NT], "vtm": [128, BSEQ, H * DV], "y": [128, 8, NT],
        "y2": [128, 8, NT], "gate": [128, NT], "xbig": [128, 32, NTH],
        "f2": [128, 32, NTH],
    }
    for name in dbg:
        dbg_out[name] = nc.dram_tensor("dbg_" + name, dbg_shapes[name], F32,
                                       kind="ExternalOutput").ap()
    dbg_src = {
        "xn": xn, "hc": hc_bf, "q": qn_bf, "k": kn_bf, "sd": sd_scan,
        "ab": ab_bf, "spike": spike_fm, "alpha": apad, "beta": beta8,
        "vtm": v_tm, "y": y_bf, "y2": y2, "gate": gate_sc, "xbig": a_big,
        "f2": f2_bf,
    }
    dbg_stage = [None, None]
    def emit_dbg(name, waits):
        if name not in dbg_out:
            return
        src = dbg_src[name]
        srcap = src if isinstance(src, bass.AP) else src[:]
        if srcap.dtype != F32:
            if dbg_stage[0] is None:
                dbg_stage[0] = f2_bf[:].rearrange("p a b -> p (a b)").bitcast(F32)
            n = int(np.prod(dbg_shapes[name][1:]))
            stg = dbg_stage[0][:, 0:n]
            w = list(waits) + ([dbg_stage[1]] if dbg_stage[1] else [])
            t = s.op("vector", lambda e, stg=stg, srcap=srcap: e.tensor_copy(
                stg, srcap.rearrange(
                    " ".join(["p"] + [chr(97 + i) for i in range(srcap.ndim - 1)])
                    + " -> p (" + " ".join(chr(97 + i) for i in range(srcap.ndim - 1))
                    + ")") if srcap.ndim > 2 else srcap),
                waits=w, inc="V", tag=f"dbgcast_{name}", chain=True)
            dbg_stage[1] = s.op("sync", lambda e, stg=stg, name=name, _w=None: e.dma_start(
                out=dbg_out[name].rearrange(
                    " ".join(["p"] + [chr(97 + i) for i in range(len(dbg_shapes[name]) - 1)])
                    + " -> p (" + " ".join(chr(97 + i) for i in range(len(dbg_shapes[name]) - 1))
                    + ")") if len(dbg_shapes[name]) > 2 else dbg_out[name],
                in_=stg), waits=[t] + ([dbg_stage[1]] if dbg_stage[1] else []),
                inc="DD", n=16, tag=f"dbg_{name}")
        else:
            dbg_stage[1] = s.op("sync", lambda e, srcap=srcap, name=name: e.dma_start(
                out=dbg_out[name], in_=srcap),
                waits=list(waits) + ([dbg_stage[1]] if dbg_stage[1] else []),
                inc="DD", n=16, tag=f"dbg_{name}")

    def bank_mm(bank, fn, first, last, waits=(), force_inc=False, tag=""):
        w = list(waits)
        if first and bank.free is not None:
            w.append(bank.free)
        tok = s.op("tensor", fn, waits=w,
                   inc="T" if (last or force_inc) else None, tag=tag)
        if last:
            bank.ready = tok

    def consume(engine, bank, fn, last=True, waits=(), tag="", chain=False):
        w = list(waits)
        if bank is not None and bank.ready is not None:
            w.append(bank.ready)
        sem = {"vector": "V", "scalar": "S", "gpsimd": "G"}[engine]
        tok = s.op(engine, fn, waits=w, inc=sem, tag=tag, chain=chain)
        if bank is not None and last:
            bank.free = tok
        return tok

    def vop(fn, waits=(), tag="", chain=False):
        return s.op("vector", fn, waits=waits, inc="V", tag=tag, chain=chain)

    def sop(fn, waits=(), tag="", chain=False):
        return s.op("scalar", fn, waits=waits, inc="S", tag=tag, chain=chain)

    def gop(fn, waits=(), tag="", chain=False):
        return s.op("gpsimd", fn, waits=waits, inc="G", tag=tag, chain=chain)

    cast_done = {}

    def stripe_load(wname, og0, kq, tag=""):
        si, nk, ow, rows = stripe_index[(wname, og0, kq)]
        i = pool_i[0]
        pool_i[0] = (i + 1) % NPOOL
        buf = pool[i]
        w = []
        if pool_state[i] is not None:
            w.append(pool_state[i])
        for t_ in cast_done.get(wname, ()):
            w.append(t_)
        tok = s.op("sync",
                   lambda e, buf=buf, si=si, rows=rows, nk=nk, ow=ow:
                   e.dma_start(out=buf[0:rows, 0:nk * ow],
                               in_=wstripes[si, 0:rows, 0:nk * ow]),
                   waits=w, inc=f"DW{i}", n=16, tag=f"ld_{wname}_{og0}_{kq}" + tag)
        return buf, tok, i

    def stripe_used(i):
        pool_state[i] = ("T", s.counts["T"])

    def proj_gen(wname, mov_view, out_cb, mov_toks=(), group=2, nt=NT, tag=""):
        K, O = WSHAPES[wname]
        kchunks = ceil_div(K, 128)
        otiles = ceil_div(O, 128)
        for ogidx, og0 in enumerate(range(0, otiles, group)):
            ots = list(range(og0, min(og0 + group, otiles)))
            ow = sum(min(128, O - ot * 128) for ot in ots)
            kstr = max(1, 2048 // ow) if K % 128 == 0 else 1
            boff = 2 * (ogidx % 2) if group <= 2 else 0
            for kq in range(0, kchunks, kstr):
                nk = min(kstr, kchunks - kq)
                buf, ltok, bi = stripe_load(wname, og0, kq, tag=tag)
                for kk in range(nk):
                    ki = kq + kk
                    rows = min(128, K - ki * 128)
                    for j, ot in enumerate(ots):
                        bank = B_MAIN[boff + j]
                        osz = min(128, O - ot * 128)
                        mv = mov_view(ki)
                        bank_mm(bank,
                                lambda e, buf=buf, j=j, mv=mv, bank=bank,
                                       kk=kk, ow=ow, osz=osz, rows=rows, nt=nt,
                                       ki=ki:
                                e.matmul(bank.ps[0:osz, 0:nt],
                                         buf[0:rows, kk * ow + j * 128:
                                             kk * ow + j * 128 + osz], mv,
                                         start=(ki == 0),
                                         stop=(ki == kchunks - 1)),
                                first=(ki == 0), last=(ki == kchunks - 1),
                                force_inc=(j == len(ots) - 1 and kk == nk - 1),
                                waits=[ltok] + list(mov_toks),
                                tag=f"{tag}{wname}_mm{ot}_{ki}")
                stripe_used(bi)
                yield
            for j, ot in enumerate(ots):
                out_cb(ot, B_MAIN[boff + j])

    def proj(*a, **kw):
        for _ in proj_gen(*a, **kw):
            pass

    def pump(gen, n):
        for _ in range(n):
            if next(gen, "DONE") == "DONE":
                return False
        return True

    for k in SMALL_SHAPES:
        s.op("sync", lambda e, k=k: e.dma_start(out=P[k][:], in_=sm_in[k]),
             inc="DP", n=16, tag=f"p_{k}")
    allp = ("DP", s.counts["DP"])
    vop(lambda e: e.tensor_copy(ident_bf[:], P["ident"][:]), waits=[allp])
    vop(lambda e: e.tensor_copy(seg8_bf[:], P["seg8"][:]))
    vop(lambda e: e.tensor_copy(bseg8_bf[:], P["bseg8"][:]))
    cst = sbuf("cst", [128, 3], F32)
    vop(lambda e: e.memset(cst[:, 0:1], LNEPS))
    vop(lambda e: e.memset(cst[:, 1:2], EPS))
    vop(lambda e: e.memset(cst[:, 2:3], 1.0 / DV))
    vop(lambda e: e.memset(memst[:], 0.0))
    vop(lambda e: e.memset(h1h[:], 0.0))
    vop(lambda e: e.memset(S_all[:], 0.0))
    vop(lambda e: e.memset(apad[:], 1.0))
    vop(lambda e: e.memset(B_BUILD.ps[:, 0:256], 0.0))
    vop(lambda e: e.memset(rho[:], 1.0))
    vop(lambda e: e.memset(rrev[:], 1.0))
    vop(lambda e: e.memset(pcum[:], 1.0))
    init_tok = vop(lambda e: e.memset(chist[:], 0.0))
    consts_tok = init_tok

    cast_order = ["wq", "wk", "wv", "wsp", "wau", "wad", "wbeta", "wu1",
                  "wu2", "w1t", "w2t", "wff1", "wff3", "h1t", "h2t", "wff2",
                  "w3t", "wout"]
    _by_w = {}
    for ent in stripe_list:
        _by_w.setdefault(ent[0], []).append(ent)
    ci = 0
    cb_state = [init_tok, init_tok]
    for wname in cast_order:
        K, O = WSHAPES[wname]
        tout = None
        for (_, si, kq, nk, og0, ow, rows) in _by_w[wname]:
            if wname == "wv":
                o0 = 0
            elif wname == "wout":
                o0 = og0 * 128
            else:
                o0 = og0 * 128
            par = ci % 2
            if nk == 1:
                tout = s.op("gpsimd",
                            lambda e, si=si, wname=wname, kq=kq, o0=o0, ow=ow,
                                   rows=rows:
                            e.dma_start(out=wstripes[si, 0:rows, 0:ow],
                                        in_=w_in[wname][kq * 128:kq * 128 + rows,
                                                        o0:o0 + ow]),
                            waits=[cb_state[par]], inc=f"DCG{par}", n=16,
                            tag=f"cast_{wname}_{si}")
            else:
                tout = s.op("gpsimd",
                            lambda e, si=si, wname=wname, kq=kq, o0=o0, ow=ow,
                                   nk=nk:
                            e.dma_start(
                                out=wstripes[si, :, 0:nk * ow].rearrange(
                                    "p (n c) -> p n c", n=nk),
                                in_=w_in[wname][kq * 128:(kq + nk) * 128,
                                                o0:o0 + ow].rearrange(
                                    "(n p) c -> p n c", p=128)),
                            waits=[cb_state[par]], inc=f"DCG{par}", n=16,
                            tag=f"cast_{wname}_{si}")
            cb_state[par] = tout
            ci += 1
        cast_done[wname] = (cb_state[0], cb_state[1])
    cast_all = tout

    halo_tok = init_tok
    prev_scan_tok = init_tok
    out_final_toks = []
    for ti in range(ntiles):
        t0 = ti * CT

        xtoks = []
        for c in range(8):
            xtoks.append(s.op("sync",
                lambda e, c=c, ti=ti: e.dma_start(
                    out=OUT[:, c, :].rearrange("p (s t) -> p s t", s=BSEQ),
                    in_=x_fm[c, :, ti]),
                waits=([init_tok] if ti == 0 else [out_final_toks[-1]]),
                inc="DX", n=16, tag=f"x_{ti}_{c}"))
        xtok_all = xtoks[-1]

        xb_toks = []
        for c in range(8):
            xb_toks.append(sop(lambda e, c=c: e.copy(xn_bf[:, c, :], OUT[:, c, :]),
                               waits=[xtok_all], tag="xbf"))
        bs = B_BC[0]
        for c in range(8):
            bank_mm(bs, lambda e, c=c, bs=bs: e.matmul(
                bs.ps[0:1, 0:NT], P["onescol"], xn_bf[:, c, :],
                start=(c == 0), stop=(c == 7)),
                first=(c == 0), last=(c == 7),
                waits=[xb_toks[c], consts_tok], tag="lnsum")
        bs2 = B_BC[1]
        sq_toks = []
        for c in range(8):
            w = [xtok_all]
            if c >= 2:
                w.append(sq_toks[c - 2][1])
            tsq = sop(lambda e, c=c: e.activation(
                out=scr2[:, c % 2, :], in_=OUT[:, c, :], func=AF.Square),
                waits=w, tag="xsq")
            bank_mm(bs2, lambda e, c=c, bs2=bs2: e.matmul(
                bs2.ps[0:1, 0:NT], P["onescol"], scr2[:, c % 2, :],
                start=(c == 0), stop=(c == 7)),
                first=(c == 0), last=(c == 7), force_inc=True,
                waits=[tsq], tag="lnsum2")
            sq_toks.append((tsq, ("T", s.counts["T"])))
        consume("vector", bs, lambda e, bs=bs: e.tensor_scalar(
            out=stat_m[:], in0=bs.ps[0:1, 0:NT], scalar1=1.0 / D, scalar2=None,
            op0=ALU.mult), last=True, tag="ln_m")
        consume("vector", bs2, lambda e, bs2=bs2: e.tensor_scalar(
            out=stat_v[:], in0=bs2.ps[0:1, 0:NT], scalar1=1.0 / D, scalar2=None,
            op0=ALU.mult), last=True, tag="ln_v")
        vop(lambda e: e.tensor_mul(stat_t[:], stat_m[:], stat_m[:]), tag="ln_m2", chain=True)
        tv = vop(lambda e: e.tensor_sub(stat_v[:], stat_v[:], stat_t[:]), tag="ln_var", chain=True)
        tv = sop(lambda e: e.activation(out=stat_v[:], in_=stat_v[:], func=AF.Sqrt,
                                        bias=cst[0:1, 0:1]), waits=[tv], tag="ln_sd")
        tv = vop(lambda e: e.reciprocal(stat_v[:], stat_v[:]), waits=[tv], tag="ln_rs", chain=True)
        tm = ("V", s.counts["V"])
        bm, br = B_BC[0], B_BC[1]
        bank_mm(bm, lambda e, bm=bm: e.matmul(
            bm.ps[0:128, 0:NT], P["onesrow"][0:1, :], stat_m[:],
            start=True, stop=True), first=True, last=True, waits=[tm], tag="bc_m")
        bank_mm(br, lambda e, br=br: e.matmul(
            br.ps[0:128, 0:NT], P["onesrow"][0:1, :], stat_v[:],
            start=True, stop=True), first=True, last=True, waits=[tv], tag="bc_rs")
        xn_toks = []
        for c in range(8):
            consume("vector", bm, lambda e, c=c, bm=bm: e.tensor_sub(
                xn[:, c, :], OUT[:, c, :], bm.ps[0:128, 0:NT]),
                last=(c == 7), tag="xn_sub")
            consume("vector", br, lambda e, c=c, br=br: e.tensor_mul(
                xn[:, c, :], xn[:, c, :], br.ps[0:128, 0:NT]),
                last=(c == 7), tag="xn_mul", chain=True)
            xn_toks.append(("V", s.counts["V"]))
        xnbf_toks = []
        for c in range(8):
            xnbf_toks.append(sop(lambda e, c=c: e.copy(xn_bf[:, c, :], xn[:, c, :]),
                                 waits=[xn_toks[c]], tag="xnbf"))
        xnbf_all = xnbf_toks[-1]
        emit_dbg("xn", [xn_toks[-1]])

        bs = B_BC[0]
        sq_toks = []
        for c in range(8):
            w = [xn_toks[c]]
            if c >= 2:
                w.append(sq_toks[c - 2][1])
            tsq = sop(lambda e, c=c: e.activation(
                out=scr2[:, c % 2, :], in_=xn[:, c, :], func=AF.Square),
                waits=w, tag="h1sq")
            bank_mm(bs, lambda e, c=c, bs=bs: e.matmul(
                bs.ps[0:1, 0:NT], P["onescol"], scr2[:, c % 2, :],
                start=(c == 0), stop=(c == 7)),
                first=(c == 0), last=(c == 7), force_inc=True,
                waits=[tsq], tag="h1sum")
            sq_toks.append((tsq, ("T", s.counts["T"])))
        consume("vector", bs, lambda e, bs=bs: e.tensor_scalar(
            out=stat_m[:], in0=bs.ps[0:1, 0:NT], scalar1=1.0 / D, scalar2=None,
            op0=ALU.mult), tag="h1_ms")
        tv = sop(lambda e: e.activation(out=stat_m[:], in_=stat_m[:], func=AF.Sqrt,
                                        bias=cst[0:1, 1:2]), waits=[("V", s.counts["V"])],
                 tag="h1_sd")
        tv = vop(lambda e: e.reciprocal(stat_m[:], stat_m[:]), waits=[tv], tag="h1_rs", chain=True)
        br = B_BC[1]
        bank_mm(br, lambda e, br=br: e.matmul(
            br.ps[0:128, 0:NT], P["onesrow"][0:1, :], stat_m[:],
            start=True, stop=True), first=True, last=True, waits=[tv], tag="bc_h1")
        h1_toks = []
        for c in range(8):
            consume("vector", br, lambda e, c=c, br=br: e.scalar_tensor_tensor(
                out=h1h[:, c, :, 3:3 + CT],
                in0=xn[:, c, :].rearrange("p (s t) -> p s t", s=BSEQ),
                scalar=P["wni"][:, c:c + 1],
                in1=br.ps[0:128, 0:NT].rearrange("p (s t) -> p s t", s=BSEQ),
                op0=ALU.mult, op1=ALU.mult),
                last=(c == 7), waits=[halo_tok, xn_toks[c]], tag="h1")
            h1_toks.append(("V", s.counts["V"]))
        hc_toks = []
        for c in range(8):
            w = [h1_toks[c]]
            if c > 0:
                w.append(hc_toks[c - 1])
            vop(lambda e, c=c: e.tensor_scalar(
                out=convacc[:], in0=h1h[:, c, :, 0:CT],
                scalar1=P["convw"][:, 4 * c:4 * c + 1],
                scalar2=P["convb"][:, c:c + 1], op0=ALU.mult, op1=ALU.add),
                waits=w, tag="conv0", chain=True)
            for j in range(1, 4):
                vop(lambda e, c=c, j=j: e.scalar_tensor_tensor(
                    out=convacc[:], in0=h1h[:, c, :, j:j + CT],
                    scalar=P["convw"][:, 4 * c + j:4 * c + j + 1],
                    in1=convacc[:], op0=ALU.mult, op1=ALU.add), tag=f"conv{j}", chain=True)
            hc_toks.append(sop(lambda e, c=c: e.activation(
                out=hc_bf[:, c, :], in_=convacc[:].rearrange("p s t -> p (s t)"),
                func=AF.Silu), waits=[("V", s.counts["V"])], tag="hc"))
        hc_all = hc_toks[-1]
        halo_tok = gop(lambda e: e.tensor_copy(
            h1h[:, :, :, 0:3], h1h[:, :, :, CT:CT + 3]),
            waits=[hc_all] + h1_toks, tag="halo", chain=True)
        emit_dbg("hc", [hc_all])

        hcv = lambda ki: hc_bf[:, ki, :]
        def sd_cb(ot, bank):
            consume("vector", bank, lambda e, ot=ot, bank=bank: e.tensor_copy(
                sd_scan[:, :, 4 * ot:4 * ot + 4].transpose([0, 2, 1]),
                bank.ps[0:128, 0:NT].rearrange("p (s t) -> p s t", s=BSEQ)),
                waits=[prev_scan_tok], tag="sd_ev")
        proj("wsp", hcv, sd_cb, mov_toks=[hc_all], tag="sd_")
        sd_tok = ("V", s.counts["V"])
        emit_dbg("sd", [sd_tok])
        def q_cb(ot, bank):
            consume("vector", bank, lambda e, ot=ot, bank=bank: e.tensor_copy(
                qf[:, ot, :], bank.ps[0:128, 0:NT]), tag="q_ev")
        proj("wq", hcv, q_cb, mov_toks=[hc_all], tag="q_")
        q_tok = ("V", s.counts["V"])
        def k_cb(ot, bank):
            consume("vector", bank, lambda e, ot=ot, bank=bank: e.tensor_copy(
                kf[:, ot, :], bank.ps[0:128, 0:NT]), tag="k_ev")
        proj("wk", hcv, k_cb, mov_toks=[hc_all], tag="k_")
        k_tok = ("V", s.counts["V"])
        vtoks = []
        for rnd in range(2):
            tcs = (2 * rnd, 2 * rnd + 1)
            for kq in range(0, 8, 2):
                buf, ltok, bi = stripe_load("wv", 0, kq, tag="v")
                for kk in range(2):
                    ki = kq + kk
                    for j, tc in enumerate(tcs):
                        for oh in range(2):
                            bank = B_MAIN[2 * j + oh]
                            bank_mm(bank,
                                    lambda e, buf=buf, ki=ki, kk=kk, tc=tc,
                                           oh=oh, bank=bank:
                                    e.matmul(bank.ps[0:128, 0:512],
                                             hc_bf[:, ki, tc * 128:(tc + 1) * 128],
                                             buf[:, kk * 1024 + oh * 512:
                                                 kk * 1024 + (oh + 1) * 512],
                                             start=(ki == 0), stop=(ki == 7)),
                                    first=(ki == 0), last=(ki == 7),
                                    force_inc=(j == 1 and oh == 1 and kk == 1),
                                    waits=[ltok, hc_all], tag=f"v_mm{tc}_{oh}_{ki}")
                stripe_used(bi)
            for j, tc in enumerate(tcs):
                for oh in range(2):
                    bank = B_MAIN[2 * j + oh]
                    vtoks.append(consume("scalar", bank,
                        lambda e, tc=tc, oh=oh, bank=bank: e.copy(
                            v_tm[:, tc, oh * 512:(oh + 1) * 512],
                            bank.ps[0:128, 0:512]), tag="v_ev"))
        v_tok = vtoks[-1]
        emit_dbg("vtm", [v_tok])
        def au_cb(ot, bank):
            consume("scalar", bank, lambda e, bank=bank: e.activation(
                out=au_bf[:, :], in_=bank.ps[0:64, 0:NT], func=AF.Silu,
                bias=P["bau"][:, 0:1]), tag="au_ev")
        proj("wau", hcv, au_cb, mov_toks=[hc_all], tag="au_")
        au_tok = ("S", s.counts["S"])
        def ab_cb(ot, bank):
            consume("vector", bank, lambda e, ot=ot, bank=bank: e.tensor_scalar(
                out=ab_bf[:, ot, :], in0=bank.ps[0:128, 0:NT],
                scalar1=P["bad"][:, ot:ot + 1], scalar2=None, op0=ALU.add),
                tag="ab_ev")
        proj("wad", lambda ki: au_bf[:, :], ab_cb, mov_toks=[au_tok], tag="ab_")
        ab_tok = ("V", s.counts["V"])
        emit_dbg("ab", [ab_tok])
        def bb_cb(ot, bank):
            consume("vector", bank, lambda e, bank=bank: e.tensor_scalar(
                out=bb8[:], in0=bank.ps[0:8, 0:NT], scalar1=P["bbeta"][:, 0:1],
                scalar2=None, op0=ALU.add), tag="bb_ev")
        proj("wbeta", hcv, bb_cb, mov_toks=[hc_all], tag="bb_")
        bb_tok = ("V", s.counts["V"])
        def g1_cb(ot, bank):
            consume("scalar", bank, lambda e, bank=bank: e.activation(
                out=g1_bf[:, :], in_=bank.ps[0:64, 0:NT], func=AF.Silu,
                bias=P["bu1"][:, 0:1]), tag="g1_ev")
        proj("wu1", lambda ki: xn_bf[:, ki, :], g1_cb, mov_toks=[xnbf_all], tag="g1_")
        g1_tok = ("S", s.counts["S"])

        sqguard = []
        for (src, dstn, n8, tg) in ((qf, qn_bf, nq8, "q"), (kf, kn_bf, nk8, "k")):
            bs = B_BC[0]
            for g in range(4):
                w = [q_tok if tg == "q" else k_tok]
                if len(sqguard) >= 2:
                    w.append(sqguard[-2])
                tsq = sop(lambda e, g=g, src=src: e.activation(
                    out=scr2[:, g % 2, :], in_=src[:, g, :], func=AF.Square),
                    waits=w, tag=tg + "sq")
                bank_mm(bs, lambda e, g=g, bs=bs: e.matmul(
                    bs.ps[0:8, 0:NT], seg8_bf[:, 8 * g:8 * g + 8],
                    scr2[:, g % 2, :], start=(g == 0), stop=(g == 3)),
                    first=(g == 0), last=(g == 3), force_inc=True,
                    waits=[tsq], tag=tg + "ss")
                sqguard.append(("T", s.counts["T"]))
            consume("scalar", bs, lambda e, bs=bs, n8=n8: e.activation(
                out=n8[:], in_=bs.ps[0:8, 0:NT], func=AF.Sqrt), tag=tg + "_sd")
            tv = vop(lambda e, n8=n8: e.tensor_scalar(
                out=n8[:], in0=n8[:], scalar1=EPS, scalar2=None, op0=ALU.add),
                waits=[("S", s.counts["S"])], tag=tg + "_eps")
            tv = vop(lambda e, n8=n8: e.reciprocal(n8[:], n8[:]), waits=[tv],
                     tag=tg + "_rec", chain=True)
            for g in range(4):
                br = B_BC[1]
                bank_mm(br, lambda e, g=g, br=br, n8=n8: e.matmul(
                    br.ps[0:128, 0:NT], P["headmap"][0:8, g * 128:(g + 1) * 128],
                    n8[:], start=True, stop=True),
                    first=True, last=True, waits=[tv], tag=tg + "_bc")
                consume("vector", br, lambda e, g=g, br=br, src=src, dstn=dstn:
                        e.tensor_mul(dstn[:, g, :], src[:, g, :],
                                     br.ps[0:128, 0:NT]),
                        waits=[q_tok if tg == "q" else k_tok], tag=tg + "_nrm")
        qn_tok = kn_tok = ("V", s.counts["V"])
        emit_dbg("q", [qn_tok])
        emit_dbg("k", [kn_tok])


        tern_state = []
        def ternary_gen(half):
            tsl = slice(half * NTH, (half + 1) * NTH)
            def w1_cb(ot, bank):
                consume("scalar", bank, lambda e, ot=ot, bank=bank: e.activation(
                    out=a_big[:, ot, :], in_=bank.ps[0:128, 0:NTH], func=AF.Silu,
                    scale=P["s1"][:, ot:ot + 1]),
                    waits=([dbg_stage[1]] if dbg_stage[1] else []), tag="t1_ev")
            yield from proj_gen("w1t", lambda ki: xn_bf[:, ki, tsl], w1_cb,
                                mov_toks=[xnbf_all], nt=NTH, tag=f"t1h{half}_")
            t1_tok = ("S", s.counts["S"])
            def w2_cb(ot, bank):
                consume("vector", bank, lambda e, ot=ot, bank=bank:
                        e.scalar_tensor_tensor(
                            out=a_big[:, ot, :], in0=bank.ps[0:128, 0:NTH],
                            scalar=P["s2"][:, ot:ot + 1], in1=a_big[:, ot, :],
                            op0=ALU.mult, op1=ALU.mult),
                        waits=[t1_tok], tag="t2_ev")
            yield from proj_gen("w2t", lambda ki: xn_bf[:, ki, tsl], w2_cb,
                                mov_toks=[xnbf_all], nt=NTH, tag=f"t2h{half}_")
            f1_tok = ("V", s.counts["V"])
            if half == 0:
                emit_dbg("xbig", [f1_tok])
            def h1_cb(ot, bank):
                consume("scalar", bank, lambda e, ot=ot, bank=bank: e.activation(
                    out=f2_bf[:, ot, :], in_=bank.ps[0:128, 0:NTH], func=AF.Gelu,
                    scale=P["hs1"][:, ot:ot + 1]),
                    waits=([dbg_stage[1]] if dbg_stage[1] else []), tag="f2_ev")
            yield from proj_gen("h1t", lambda ki: a_big[:, ki, :], h1_cb,
                                mov_toks=[f1_tok], nt=NTH, tag=f"f2h{half}_")
            f2_tok = ("S", s.counts["S"])
            if half == 0:
                emit_dbg("f2", [f2_tok])
            def h2_cb(ot, bank):
                consume("scalar", bank, lambda e, ot=ot, bank=bank: e.activation(
                    out=a_big[:, ot, :], in_=bank.ps[0:128, 0:NTH], func=AF.Gelu,
                    scale=P["hs2"][:, ot:ot + 1]), tag="f3_ev")
            yield from proj_gen("h2t", lambda ki: f2_bf[:, ki, :], h2_cb,
                                mov_toks=[f2_tok], nt=NTH, tag=f"f3h{half}_")
            f3_tok = ("S", s.counts["S"])
            def w3_cb(ot, bank):
                consume("vector", bank, lambda e, ot=ot, bank=bank:
                        e.scalar_tensor_tensor(
                            out=OUT[:, ot, tsl], in0=bank.ps[0:128, 0:NTH],
                            scalar=P["s3"][:, ot:ot + 1], in1=OUT[:, ot, tsl],
                            op0=ALU.mult, op1=ALU.add),
                        waits=[f3_tok, xtok_all], tag="t3_ev", chain=True)
            yield from proj_gen("w3t", lambda ki: a_big[:, ki, :], w3_cb,
                                mov_toks=[f3_tok], nt=NTH, tag=f"t3h{half}_")
            tern_state.append(("V", s.counts["V"]))

        def _tern_chain():
            yield from ternary_gen(0)
            yield from ternary_gen(1)
        tgen = _tern_chain()
        pump(tgen, 32)

        for t in range(CT):
            w = [sd_tok] if t == 0 else []
            vop(lambda e, t=t: e.scalar_tensor_tensor(
                out=memst[:], in0=memst[:], scalar=0.9, in1=sd_scan[:, t, :],
                op0=ALU.mult, op1=ALU.add), waits=w, tag=f"ms{t}", chain=True)
            vop(lambda e, t=t: e.tensor_single_scalar(
                out=sd_scan[:, t, :], in_=memst[:], scalar=0.5, op=ALU.is_gt),
                tag=f"sp{t}", chain=True)
            vop(lambda e, t=t: e.scalar_tensor_tensor(
                out=memst[:], in0=sd_scan[:, t, :], scalar=-0.5, in1=memst[:],
                op0=ALU.mult, op1=ALU.add), tag=f"mm{t}", chain=True)
        scan_v_tok = ("V", s.counts["V"])
        spf_toks = []
        for g in range(4):
            for sq in range(BSEQ):
                spf_toks.append(gop(lambda e, g=g, sq=sq: e.tensor_copy(
                    spike_fm[:, g, sq * CT:(sq + 1) * CT],
                    sd_scan[:, :, 4 * g + sq]),
                    waits=[scan_v_tok], tag="spfm", chain=True))
        spf_all = spf_toks[-1]
        prev_scan_tok = spf_all
        emit_dbg("spike", [spf_all])
        pump(tgen, 150)

        ap_valid = apad[:, :, :, 0:SB]
        bsp, bbs = B_BC[0], B_BC[1]
        for g in range(4):
            bank_mm(bsp, lambda e, g=g, bsp=bsp: e.matmul(
                bsp.ps[0:8, 0:NT], seg8_bf[:, 8 * g:8 * g + 8], spike_fm[:, g, :],
                start=(g == 0), stop=(g == 3)),
                first=(g == 0), last=(g == 3), waits=[spf_all], tag="segsum")
            bank_mm(bbs, lambda e, g=g, bbs=bbs: e.matmul(
                bbs.ps[0:8, 0:NT], bseg8_bf[:, 8 * g:8 * g + 8], spike_fm[:, g, :],
                start=(g == 0), stop=(g == 3)),
                first=(g == 0), last=(g == 3), waits=[spf_all], tag="bsum")
        consume("vector", bsp, lambda e, bsp=bsp: e.tensor_single_scalar(
            out=active8[:], in_=bsp.ps[0:8, 0:NT], scalar=0.0, op=ALU.is_gt),
            tag="active")
        act_tok = ("V", s.counts["V"])
        consume("vector", bbs, lambda e, bbs=bbs: e.tensor_add(
            beta8[:], bb8[:], bbs.ps[0:8, 0:NT]), waits=[bb_tok], tag="blog")
        tb = sop(lambda e: e.activation(out=beta8[:], in_=beta8[:], func=AF.Sigmoid),
                 waits=[("V", s.counts["V"])], tag="bsig")
        tb = vop(lambda e: e.tensor_mul(beta8[:], beta8[:], active8[:]),
                 waits=[tb, act_tok], tag="bact", chain=True)
        emit_dbg("beta", [tb])
        bt = B_MISC
        for sq in range(BSEQ):
            bank_mm(bt, lambda e, sq=sq, bt=bt: e.transpose(
                bt.ps[0:128, 8 * sq:8 * sq + 8],
                beta8[0:8, sq * CT:(sq + 1) * CT], P["ident"][0:8, 0:8]),
                first=(sq == 0), last=(sq == 3), waits=[tb], tag="btr")
        consume("vector", bt, lambda e, bt=bt: e.tensor_copy(
            beta_tok[:].rearrange("p s h -> p (s h)"), bt.ps[0:128, 0:32]),
            tag="btr_ev")
        btok_tok = ("V", s.counts["V"])



        g_dec_toks = []
        for g in range(4):
            t1 = vop(lambda e, g=g: e.scalar_tensor_tensor(
                out=ap_valid, in0=spike_fm[:, g, :].rearrange(
                    "p (s b j) -> p s b j", s=BSEQ, b=NSB),
                scalar=P["asp"][:, g:g + 1],
                in1=ab_bf[:, g, :].rearrange("p (s b j) -> p s b j", s=BSEQ, b=NSB),
                op0=ALU.mult, op1=ALU.add),
                waits=[spf_all, ab_tok, init_tok], tag="alog")
            t1 = sop(lambda e: e.activation(
                out=ap_valid, in_=ap_valid, func=AF.Sigmoid),
                waits=[t1], tag="asig")
            br = B_BC[1]
            bank_mm(br, lambda e, g=g, br=br: e.matmul(
                br.ps[0:128, 0:NT], P["headmap"][0:8, g * 128:(g + 1) * 128],
                active8[:], start=True, stop=True),
                first=True, last=True, waits=[act_tok], tag="abc")
            vop(lambda e: e.tensor_scalar(
                out=ap_valid, in0=ap_valid, scalar1=-1.0, scalar2=None,
                op0=ALU.add), waits=[t1], tag="am1")
            consume("vector", br, lambda e, br=br: e.tensor_mul(
                ap_valid, ap_valid,
                br.ps[0:128, 0:NT].rearrange("p (s b j) -> p s b j", s=BSEQ, b=NSB)),
                tag="am2", chain=True)
            amk = vop(lambda e: e.tensor_scalar(
                out=ap_valid, in0=ap_valid, scalar1=1.0, scalar2=None,
                op0=ALU.add), tag="am3", chain=True)
            if g == 3:
                emit_dbg("alpha", [amk])
            for sq in range(BSEQ):
                for b in range(NSB):
                    w = [amk] if (sq == 0 and b == 0) else []
                    def _rho(e, sq=sq, b=b):
                        with nc.allow_low_precision(reason="bf16 cumprod store; scan state is fp32"):
                            return e.tensor_tensor_scan(
                                out=rho[:, sq, b, 0:SB], data0=apad[:, sq, b, 0:SB],
                                data1=apad[:, sq, b, 0:SB], initial=1.0,
                                op0=ALU.mult, op1=ALU.bypass)
                    vop(_rho, waits=w, tag="rho")
            FL = NSB * (SB + 1)
            for sq in range(BSEQ):
                vop(lambda e, sq=sq: e.tensor_tensor_scan(
                    out=pcum[:, sq].rearrange("p a b -> p (a b)"),
                    data0=apad[:, sq].rearrange("p a b -> p (a b)"),
                    data1=apad[:, sq].rearrange("p a b -> p (a b)"),
                    initial=1.0, op0=ALU.mult, op1=ALU.bypass), tag="pcum")
            for sq in range(BSEQ):
                def _rrev(e, sq=sq, FL=FL):
                    with nc.allow_low_precision(reason="bf16 cumprod store; scan state is fp32"):
                        return e.tensor_tensor_scan(
                            out=rrev[:, sq].rearrange("p a b -> p (a b)")[:, FL - 2::-1],
                            data0=apad[:, sq].rearrange("p a b -> p (a b)")[:, FL - 1:0:-1],
                            data1=apad[:, sq].rearrange("p a b -> p (a b)")[:, FL - 1:0:-1],
                            initial=1.0, op0=ALU.mult, op1=ALU.bypass)
                vop(_rrev, tag="rrev")
            def _rrc(e):
                with nc.allow_low_precision(reason="bf16 reciprocal store"):
                    return e.reciprocal(
                        rhorec[:].rearrange("p s b j -> p (s b j)"),
                        rho[:].rearrange("p s b j -> p (s b j)"))
            vop(_rrc, tag="rhorec", chain=True)
            scans_tok = ("V", s.counts["V"])
            knv = lambda g: kn_bf[:, g, :].rearrange("p (s b j) -> p s b j",
                                                     s=BSEQ, b=NSB)
            qnv = lambda g: qn_bf[:, g, :].rearrange("p (s b j) -> p s b j",
                                                     s=BSEQ, b=NSB)
            vop(lambda e, g=g: e.tensor_mul(
                kqh[:, g, :, :, 0:SB], knv(g), rho[:, :, :, 0:SB]),
                waits=[kn_tok, scans_tok], tag="krho", chain=True)
            vop(lambda e, g=g: e.tensor_mul(
                kqh[:, g, :, :, SB:2 * SB], qnv(g), rho[:, :, :, 0:SB]),
                waits=[qn_tok], tag="qrho")
            vop(lambda e, g=g: e.tensor_mul(
                kap[:, g, :].rearrange("p (s b j) -> p s b j", s=BSEQ, b=NSB),
                knv(g), pcum[:, :, :, 0:SB]), tag="kap")
            vop(lambda e, g=g: e.tensor_mul(
                qtl[:, g, :].rearrange("p (s b j) -> p s b j", s=BSEQ, b=NSB),
                qnv(g), pcum[:, :, :, 0:SB]), tag="qtl")
            var_tok = vop(lambda e, g=g: e.tensor_mul(
                khat[:, g, :].rearrange("p (s b j) -> p s b j", s=BSEQ, b=NSB),
                knv(g), rrev[:, :, :, 0:SB]), tag="khat")
            gop(lambda e, g=g: e.tensor_mul(
                chist[:, :, 0:SB], knv(g)[:, :, 0, :], rhorec[:, :, 0, 0:SB]),
                waits=[var_tok], tag="chi0", chain=True)
            for b in range(1, NSB):
                ob, opb = CHOFF[b], CHOFF[b - 1]
                if b > 1:
                    gop(lambda e, g=g, b=b, ob=ob, opb=opb: e.tensor_mul(
                        chist[:, :, ob:ob + SB * (b - 1)],
                        chist[:, :, opb:opb + SB * (b - 1)],
                        rho[:, :, b - 1, SB - 1:SB].broadcast_to(
                            [128, BSEQ, SB * (b - 1)])), tag=f"chiA{b}", chain=True)
                gop(lambda e, g=g, b=b, ob=ob: e.tensor_mul(
                    chist[:, :, ob + SB * (b - 1):ob + SB * b],
                    knv(g)[:, :, b - 1, :], rhorec[:, :, b - 1, 0:SB]),
                    tag=f"chiB{b}", chain=True)
                gop(lambda e, g=g, b=b, ob=ob: e.tensor_mul(
                    chist[:, :, ob + SB * (b - 1):ob + SB * b],
                    chist[:, :, ob + SB * (b - 1):ob + SB * b],
                    rho[:, :, b - 1, SB - 1:SB].broadcast_to([128, BSEQ, SB])),
                    tag=f"chiB2{b}", chain=True)
                gop(lambda e, g=g, b=b, ob=ob: e.tensor_mul(
                    chist[:, :, ob + SB * b:ob + SB * (b + 1)],
                    knv(g)[:, :, b, :], rhorec[:, :, b, 0:SB]), tag=f"chiC{b}", chain=True)
            chi_tok = ("G", s.counts["G"])
            g_dec_toks.append((("V", s.counts["V"]), chi_tok))
            pump(tgen, 8)

            for half in range(2):
                for sq in range(BSEQ):
                    h = 2 * g + half
                    P0, P1 = 64 * half, 64 * half + 64
                    slot = g * 4 + sq
                    tsl = slice(sq * CT, (sq + 1) * CT)
                    vt, ct = g_dec_toks[g]
                    sbf = vop(lambda e, P0=P0, P1=P1, slot=slot: e.tensor_copy(
                        S_bf[P0:P1, :], S_all[P0:P1, slot, :]), tag="sbf", chain=True)
                    bank_mm(B_BUILD, lambda e, P0=P0, P1=P1, tsl=tsl, g=g:
                            e.matmul(B_BUILD.ps[0:128, 256:384],
                                     kap[P0:P1, g, tsl], S_bf[P0:P1, :],
                                     start=True, stop=True),
                            first=True, last=False, waits=[sbf, vt], tag="rt_mm")
                    for b in range(NSB):
                        bank_mm(B_BUILD,
                                lambda e, P0=P0, P1=P1, sq=sq, b=b, g=g:
                                e.matmul(B_BUILD.ps[0:SB * b + SB, 32 * b:32 * b + 32],
                                         chist[P0:P1, sq, CHOFF[b]:CHOFF[b] + SB * b + SB],
                                         kqh[P0:P1, g, sq, b, :],
                                         start=True, stop=True),
                                first=False, last=(b == NSB - 1),
                                waits=[ct], tag=f"bld{b}")
                    consume("vector", B_BUILD, lambda e, sq=sq, h=h:
                            e.scalar_tensor_tensor(
                                out=Mhat[:].rearrange("p (b j) -> p b j", b=NSB),
                                in0=B_BUILD.ps[0:128, 0:256].rearrange(
                                    "p (b x j) -> p b x j", b=NSB, x=2)[:, :, 0, :],
                                scalar=beta_tok[:, sq, h:h + 1],
                                in1=P["maskM"][:].rearrange("p (b j) -> p b j", b=NSB),
                                op0=ALU.mult, op1=ALU.mult),
                            last=False, waits=[btok_tok], tag="mhat")
                    consume("vector", B_BUILD, lambda e, sq=sq, h=h:
                            e.scalar_tensor_tensor(
                                out=Ghat[:].rearrange("p (b j) -> p b j", b=NSB),
                                in0=B_BUILD.ps[0:128, 0:256].rearrange(
                                    "p (b x j) -> p b x j", b=NSB, x=2)[:, :, 1, :],
                                scalar=beta_tok[:, sq, h:h + 1],
                                in1=P["maskG"][:].rearrange("p (b j) -> p b j", b=NSB),
                                op0=ALU.mult, op1=ALU.mult),
                            last=False, tag="ghat")
                    mh_tok = ("V", s.counts["V"])
                    consume("vector", B_BUILD, lambda e, sq=sq, h=h:
                            e.tensor_sub(Rping[0][:],
                                         v_tm[:, sq, h * 128:(h + 1) * 128],
                                         B_BUILD.ps[0:128, 256:384]),
                            last=True, waits=[v_tok], tag="r0")
                    r_tok = ("V", s.counts["V"])
                    bank_mm(B_S1, lambda e: e.transpose(
                        B_S1.psb[0:128, 0:128], Mhat[:], ident_bf[:]),
                        first=True, last=True, waits=[mh_tok], tag="trN")
                    ev = consume("scalar", B_S1, lambda e: e.copy(
                        Yp[0][:], B_S1.psb[0:128, 0:128]), tag="y1_ev")
                    xy_tok = {(0, 'x'): mh_tok, (0, 'y'): ev}
                    sb_banks = [B_S2, B_S1]
                    bi_ = 0
                    for kk in range(1, 7):
                        bank = sb_banks[bi_ % 2]; bi_ += 1
                        bank_mm(bank, lambda e, kk=kk, bank=bank: e.matmul(
                            bank.ps[0:128, 0:128], Yp[kk - 1][:], Xp[kk - 1][:],
                            start=True, stop=True),
                            first=True, last=True,
                            waits=[xy_tok[(kk - 1, 'x')], xy_tok[(kk - 1, 'y')]],
                            tag=f"sqX{kk}")
                        xy_tok[(kk, 'x')] = consume("scalar", bank,
                            lambda e, kk=kk, bank=bank: e.copy(
                                Xp[kk][:], bank.ps[0:128, 0:128]), tag=f"sqXe{kk}")
                        if kk < 6:
                            bank = sb_banks[bi_ % 2]; bi_ += 1
                            bank_mm(bank, lambda e, kk=kk, bank=bank: e.matmul(
                                bank.ps[0:128, 0:128], Xp[kk - 1][:], Yp[kk - 1][:],
                                start=True, stop=True),
                                first=True, last=True, tag=f"sqY{kk}")
                            xy_tok[(kk, 'y')] = consume("scalar", bank,
                                lambda e, kk=kk, bank=bank: e.copy(
                                    Yp[kk][:], bank.ps[0:128, 0:128]),
                                tag=f"sqYe{kk}")
                    rp = 0
                    for kk in range(7, 0, -1):
                        bank = sb_banks[bi_ % 2]; bi_ += 1
                        bank_mm(bank, lambda e, kk=kk, bank=bank, rp=rp: e.matmul(
                            bank.ps[0:128, 0:128], Xp[kk - 1][:], Rping[rp][:],
                            start=True, stop=True),
                            first=True, last=True,
                            waits=[xy_tok[(kk - 1, 'x')], r_tok], tag=f"app{kk}")
                        op = ALU.subtract if kk == 1 else ALU.add
                        r_tok = consume("vector", bank,
                            lambda e, bank=bank, rp=rp, op=op: e.tensor_tensor(
                                Rping[1 - rp][:], Rping[rp][:],
                                bank.ps[0:128, 0:128], op), tag=f"appe{kk}", chain=True)
                        rp = 1 - rp
                    Wt = Rping[rp]
                    bank_mm(B_MISC, lambda e, P0=P0, P1=P1, g=g, tsl=tsl: e.matmul(
                        B_MISC.ps[0:128, 0:128], qtl[P0:P1, g, tsl], S_bf[P0:P1, :],
                        start=True, stop=True),
                        first=True, last=False, waits=[sbf], tag="o1")
                    bank_mm(B_MISC, lambda e, rp=rp: e.matmul(
                        B_MISC.ps[0:128, 128:256], Ghat[:], Rping[rp][:],
                        start=True, stop=True),
                        first=False, last=False, waits=[r_tok], tag="o2")
                    bank_mm(B_MISC, lambda e, P0=P0, P1=P1, g=g, tsl=tsl:
                            e.transpose(B_MISC.psb[0:128, 768:832],
                                        khat[P0:P1, g, tsl],
                                        ident_bf[P0:P1, P0:P1]),
                            first=False, last=True, tag="trk")
                    consume("scalar", B_MISC, lambda e: e.copy(
                        Ocomb[:], B_MISC.ps[0:128, 0:128]), last=False, tag="oc1")
                    consume("vector", B_MISC, lambda e: e.tensor_add(
                        Ocomb[:], Ocomb[:], B_MISC.ps[0:128, 128:256]),
                        last=False, waits=[("S", s.counts["S"])], tag="oc2", chain=True)
                    kt_tok = consume("vector", B_MISC,
                        lambda e, sq=sq, h=h: e.tensor_scalar(
                            out=khatT[:], in0=B_MISC.psb[0:128, 768:832],
                            scalar1=beta_tok[:, sq, h:h + 1], scalar2=None,
                            op0=ALU.mult), last=True, tag="kt_ev", chain=True)
                    oc_tok = vop(lambda e: e.scalar_tensor_tensor(
                        out=On_bf[:], in0=Ocomb[:], scalar=1.0, in1=Ocomb[:],
                        op0=ALU.mult, op1=ALU.mult,
                        accum_out=rms_col[:, 0:1]), tag="osq", chain=True)
                    tq = sop(lambda e: e.activation(
                        out=rms_col[:, 1:2], in_=rms_col[:, 0:1], func=AF.Sqrt,
                        scale=cst[:, 2:3], bias=cst[:, 1:2]),
                        waits=[oc_tok], tag="orms")
                    tq = vop(lambda e: e.reciprocal(rms_col[:, 2:3], rms_col[:, 1:2]),
                             waits=[tq], tag="orec")
                    on_tok = vop(lambda e: e.tensor_scalar(
                        out=On_bf[:], in0=Ocomb[:], scalar1=rms_col[:, 2:3],
                        scalar2=None, op0=ALU.mult), waits=[tq], tag="onorm")
                    bank_mm(B_S1, lambda e: e.transpose(
                        B_S1.psb[0:128, 0:128], On_bf[:], ident_bf[:]),
                        first=True, last=False, waits=[on_tok], tag="trO")
                    bank_mm(B_S1, lambda e, P0=P0, P1=P1, rp=rp: e.matmul(
                        B_S1.ps[P0:P1, 128:256], khatT[:], Rping[rp][:],
                        start=True, stop=True),
                        first=False, last=True, waits=[kt_tok], tag="smm")
                    consume("vector", B_S1, lambda e, h=h, sq=sq, tsl=tsl:
                            e.tensor_scalar(
                                out=y_bf[:, h, tsl], in0=B_S1.psb[0:128, 0:128],
                                scalar1=P["whn"][:, h:h + 1], scalar2=None,
                                op0=ALU.mult), last=False, tag="y_ev")
                    consume("vector", B_S1, lambda e, P0=P0, P1=P1, slot=slot, sq=sq:
                            e.scalar_tensor_tensor(
                                out=S_all[P0:P1, slot, :],
                                in0=S_all[P0:P1, slot, :],
                                scalar=pcum[P0:P1, sq, NSB - 1, SB - 1:SB],
                                in1=B_S1.ps[P0:P1, 128:256],
                                op0=ALU.mult, op1=ALU.add),
                            last=True, tag="s_upd")
                    pump(tgen, 4)
        y_tok = ("V", s.counts["V"])
        emit_dbg("y", [y_tok])

        y2_toks = []
        last_gate_read = None
        for ot in range(8):
            bank = banks[4 + ot % 2]
            buf, ltok, bi = stripe_load("wout", ot, 0, tag="wo")
            for ki in range(8):
                bank_mm(bank, lambda e, buf=buf, ki=ki, bank=bank: e.matmul(
                    bank.ps[0:128, 0:NT], buf[:, ki * 128:(ki + 1) * 128],
                    y_bf[:, ki, :], start=(ki == 0), stop=(ki == 7)),
                    first=(ki == 0), last=(ki == 7), force_inc=(ki == 7),
                    waits=[ltok, y_tok], tag=f"wo_mm{ot}_{ki}")
            stripe_used(bi)
            gbank = banks[6 + ot % 2]
            bufg, ltokg, big = stripe_load("wu2", (ot // 2) * 2, 0, tag="wu2")
            bank_mm(gbank, lambda e, bufg=bufg, gbank=gbank, ot=ot: e.matmul(
                gbank.ps[0:128, 0:NT],
                bufg[0:64, (ot % 2) * 128:(ot % 2) * 128 + 128], g1_bf[:, :],
                start=True, stop=True),
                first=True, last=True, waits=[ltokg, g1_tok], tag=f"gt_mm{ot}")
            stripe_used(big)
            consume("scalar", gbank, lambda e, ot=ot, gbank=gbank: e.activation(
                out=gate_sc[:], in_=gbank.ps[0:128, 0:NT], func=AF.Sigmoid,
                bias=P["bu2"][:, ot:ot + 1]),
                waits=[last_gate_read], tag="gate")
            gate_tok = ("S", s.counts["S"])
            last_gate_read = consume("vector", bank,
                lambda e, ot=ot, bank=bank: e.tensor_mul(
                    y2[:, ot, :], bank.ps[0:128, 0:NT], gate_sc[:]),
                waits=[gate_tok], tag="y2a", chain=True)
            y2_toks.append(vop(lambda e, ot=ot: e.tensor_add(
                y2[:, ot, :], y2[:, ot, :], xn[:, ot, :]),
                waits=[xn_toks[ot]], tag="y2b", chain=True))
            pump(tgen, 4)
        emit_dbg("y2", [y2_toks[-1]])
        emit_dbg("gate", [("S", s.counts["S"])])

        pump(tgen, 10 ** 6)
        bs = B_BC[0]
        sqts = []
        for c in range(8):
            w = [y2_toks[c]]
            if c >= 2:
                w.append(sqts[c - 2][1])
            tsq = sop(lambda e, c=c: e.activation(
                out=scr2[:, c % 2, :], in_=y2[:, c, :], func=AF.Square),
                waits=w, tag="h2sq")
            bank_mm(bs, lambda e, c=c, bs=bs: e.matmul(
                bs.ps[0:1, 0:NT], P["onescol"], scr2[:, c % 2, :],
                start=(c == 0), stop=(c == 7)),
                first=(c == 0), last=(c == 7), force_inc=True,
                waits=[tsq], tag="h2sum")
            sqts.append((tsq, ("T", s.counts["T"])))
        consume("vector", bs, lambda e, bs=bs: e.tensor_scalar(
            out=stat_m[:], in0=bs.ps[0:1, 0:NT], scalar1=1.0 / D, scalar2=None,
            op0=ALU.mult), tag="h2_ms")
        tv = sop(lambda e: e.activation(out=stat_m[:], in_=stat_m[:], func=AF.Sqrt,
                                        bias=cst[0:1, 1:2]), waits=[("V", s.counts["V"])],
                 tag="h2_sd")
        tv = vop(lambda e: e.reciprocal(stat_m[:], stat_m[:]), waits=[tv],
                 tag="h2_rs", chain=True)
        br = B_BC[1]
        bank_mm(br, lambda e, br=br: e.matmul(
            br.ps[0:128, 0:NT], P["onesrow"][0:1, :], stat_m[:],
            start=True, stop=True), first=True, last=True, waits=[tv], tag="bc_h2")
        h2_toks = []
        for c in range(8):
            h2_toks.append(consume("vector", br,
                lambda e, c=c, br=br: e.scalar_tensor_tensor(
                    out=xn_bf[:, c, :], in0=y2[:, c, :],
                    scalar=P["wffn"][:, c:c + 1], in1=br.ps[0:128, 0:NT],
                    op0=ALU.mult, op1=ALU.mult),
                last=(c == 7), waits=[y2_toks[c]], tag="h2"))
        h2_all = h2_toks[-1]

        out_toks = [None] * 8
        def swiglu(half):
            tsl = slice(half * NTH, (half + 1) * NTH)
            def ff1_cb(ot, bank):
                consume("scalar", bank, lambda e, ot=ot, bank=bank: e.activation(
                    out=a_big[:, ot, :], in_=bank.ps[0:128, 0:NTH],
                    func=AF.Silu),
                    waits=[dbg_stage[1]] if dbg_stage[1] else [], tag="ff1_ev")
            proj("wff1", lambda ki: xn_bf[:, ki, tsl], ff1_cb,
                 mov_toks=[h2_all] + tern_state, nt=NTH, tag=f"ff1h{half}_")
            s1_tok = ("S", s.counts["S"])
            def ff3_cb(ot, bank):
                consume("vector", bank, lambda e, ot=ot, bank=bank: e.tensor_mul(
                    a_big[:, ot, :], a_big[:, ot, :], bank.ps[0:128, 0:NTH]),
                    waits=[s1_tok], tag="ff3_ev")
            proj("wff3", lambda ki: xn_bf[:, ki, tsl], ff3_cb, nt=NTH,
                 tag=f"ff3h{half}_")
            a1_tok = ("V", s.counts["V"])
            def ff2_cb(ot, bank):
                consume("vector", bank, lambda e, ot=ot, bank=bank, tsl=tsl:
                        e.tensor_add(OUT[:, ot, tsl], OUT[:, ot, tsl],
                                     bank.ps[0:128, 0:NTH]),
                        waits=[a1_tok], tag="ff2_ev", chain=True)
                out_toks[ot] = vop(lambda e, ot=ot, tsl=tsl: e.tensor_add(
                    OUT[:, ot, tsl], OUT[:, ot, tsl], y2[:, ot, tsl]),
                    tag="outy2", chain=True)
            proj("wff2", lambda ki: a_big[:, ki, :], ff2_cb,
                 mov_toks=[a1_tok], nt=NTH, tag=f"ff2h{half}_")

        swiglu(0)
        swiglu(1)

        otoks = []
        for c in range(8):
            otoks.append(s.op("sync",
                lambda e, c=c, ti=ti: e.dma_start(
                    out=out_fm[c, :, ti],
                    in_=OUT[:, c, :].rearrange("p (s t) -> p s t", s=BSEQ)),
                waits=[out_toks[c]], inc="DO", n=16, tag=f"out_{ti}_{c}"))
        out_final_toks.append(otoks[-1])

    fw = [("DO", s.counts["DO"])]
    if s.counts.get("DD"):
        fw.append(("DD", s.counts["DD"]))
    s.op("sync", lambda e: e.nop(), waits=fw, tag="fin")

    with nc.Block() as block:
        @block.tensor
        def _(eng):
            s.replay("tensor", eng)

        @block.vector
        def _(eng):
            s.replay("vector", eng)

        @block.scalar
        def _(eng):
            s.replay("scalar", eng)

        @block.gpsimd
        def _(eng):
            s.replay("gpsimd", eng)

        @block.sync
        def _(eng):
            s.replay("sync", eng)

    return nc, dbg_out



def _consts():
    i128 = np.arange(128)
    maskM = (i128[:, None] < i128[None, :]).astype(np.float32)
    maskG = (i128[:, None] <= i128[None, :]).astype(np.float32)
    seg8 = np.zeros((128, 32), np.float32)
    headmap = np.zeros((8, 512), np.float32)
    for g in range(4):
        for p in range(128):
            h = 2 * g + p // 64
            seg8[p, 8 * g + h] = 1.0
            headmap[h, g * 128 + p] = 1.0
    return {
        "ident": np.eye(128, dtype=np.float32),
        "maskM": maskM.astype(BF), "maskG": maskG.astype(BF),
        "seg8": seg8, "headmap": headmap,
        "onesrow": np.ones((1, 128), np.float32),
    }


def _smalls(inp):
    f = np.float32
    fm8 = lambda v: np.ascontiguousarray(np.asarray(v, f).reshape(8, 128).T)
    out = {
        "wni": fm8(inp["L0_attn_w_norm_in"]),
        "convb": fm8(inp["L0_attn_conv_b"]),
        "convw": np.ascontiguousarray(
            np.asarray(inp["L0_attn_conv_w"], f).reshape(8, 128, 4)
            .transpose(1, 0, 2).reshape(128, 32)),
        "bau": np.asarray(inp["L0_attn_b_au"], f).reshape(64, 1),
        "bad": np.ascontiguousarray(np.asarray(inp["L0_attn_b_ad"], f).reshape(4, 128).T),
        "bbeta": np.asarray(inp["L0_attn_b_beta"], f).reshape(8, 1),
        "bu1": np.asarray(inp["L0_attn_b_u1"], f).reshape(64, 1),
        "bu2": fm8(inp["L0_attn_b_u2"]),
        "wffn": fm8(inp["L0_attn_w_ffn"]),
        "whn": np.ascontiguousarray(np.asarray(inp["L0_attn_w_hn"], f).T),
        "s1": np.ascontiguousarray(np.asarray(inp["L0_ffn_w1_s"], f).reshape(32, 128).T),
        "s2": np.ascontiguousarray(np.asarray(inp["L0_ffn_w2_s"], f).reshape(32, 128).T),
        "hs1": np.ascontiguousarray(np.asarray(inp["L0_ffn_h1_s"], f).reshape(32, 128).T),
        "hs2": np.ascontiguousarray(np.asarray(inp["L0_ffn_h2_s"], f).reshape(32, 128).T),
        "s3": np.ascontiguousarray(np.asarray(inp["L0_ffn_w3_s"], f).reshape(8, 128).T),
        "asp": np.ascontiguousarray(
            np.asarray(inp["L0_attn_alpha_spike"], f).reshape(4, 128).T),
    }
    bs = np.asarray(inp["L0_attn_beta_spike"], f).reshape(8, 64)
    bseg8 = np.zeros((128, 32), np.float32)
    for g in range(4):
        bseg8[0:64, 8 * g + 2 * g] = bs[2 * g]
        bseg8[64:128, 8 * g + 2 * g + 1] = bs[2 * g + 1]
    out["bseg8"] = bseg8
    out.update(_consts())
    return out


def _weights(inp):
    f = np.float32
    a = lambda k: np.asarray(inp[k], f)
    t = lambda k: np.ascontiguousarray(np.asarray(inp[k], f).T)
    return {
        "wq": a("L0_attn_wq"), "wk": a("L0_attn_wk"), "wv": a("L0_attn_wv"),
        "wsp": a("L0_attn_w_spike"), "wau": a("L0_attn_w_au"),
        "wad": a("L0_attn_w_ad"), "wbeta": a("L0_attn_w_beta"),
        "wu1": a("L0_attn_w_u1"), "wu2": a("L0_attn_w_u2"),
        "wout": a("L0_attn_w_out"), "wff1": a("L0_attn_w_ff1"),
        "wff3": a("L0_attn_w_ff3"), "wff2": a("L0_attn_w_ff2"),
        "w1t": t("L0_ffn_w1_w"), "w2t": t("L0_ffn_w2_w"),
        "h1t": t("L0_ffn_h1_w"), "h2t": t("L0_ffn_h2_w"),
        "w3t": t("L0_ffn_w3_w"),
    }


def x_layout(xs):
    B, T, Dm = xs.shape
    ntl = T // CT
    v = xs.transpose(2, 0, 1).reshape(8, 128, B, ntl, CT)
    return np.ascontiguousarray(v.transpose(0, 1, 3, 2, 4))


def out_delayout(o):
    _, _, ntl, B, _ = o.shape
    return o.transpose(3, 2, 4, 0, 1).reshape(B, ntl * CT, 8 * 128)


def kernel(**inputs):
    x = np.asarray(inputs["L0_attn_x"], np.float32)
    B, T, Dm = x.shape
    smalls = {"p_" + k: v for k, v in _smalls(inputs).items()}
    weights = {"w_" + k: v for k, v in _weights(inputs).items()}
    nc, _ = build_program(T=T)
    in_maps = []
    for c in range(NCORES):
        m = dict(smalls)
        m.update(weights)
        m["x_fm"] = x_layout(x[BSEQ * c:BSEQ * (c + 1)])
        in_maps.append(m)
    res = run_bass_kernel_spmd(nc, in_maps, core_ids=list(range(NCORES)))
    outs = [out_delayout(r["out_fm"]) for r in res.results]
    return np.ascontiguousarray(np.concatenate(outs, axis=0))


# version 08
# speedup vs baseline: 1.0463x; 1.0463x over previous
import numpy as np
import ml_dtypes
import concourse.bass as bass
import concourse.mybir as mybir
from concourse.bass_utils import run_bass_kernel_spmd

F32 = mybir.dt.float32
BF16 = mybir.dt.bfloat16
AF = mybir.ActivationFunctionType
ALU = mybir.AluOpType
BF = ml_dtypes.bfloat16

NCORES = 8
BSEQ = 4
TFULL = 2048
D = 1024
H, DK, DV = 8, 64, 128
RLOW = 64
FF = 4096
CT = 128
SB = 16
NSB = CT // SB
NT = BSEQ * CT
NTH = NT // 2
EPS = 1e-6
LNEPS = 1e-5

ENGINES = ("tensor", "vector", "scalar", "gpsimd", "sync")

WSHAPES = {
    "wq": (D, H * DK), "wk": (D, H * DK), "wv": (D, H * DV),
    "wsp": (D, H * DK), "wau": (D, RLOW), "wad": (RLOW, H * DK),
    "wbeta": (D, H), "wu1": (D, RLOW), "wu2": (RLOW, D),
    "wout": (H * DV, D), "wff1": (D, FF), "wff3": (D, FF), "wff2": (FF, D),
    "w1t": (D, FF), "w2t": (D, FF), "h1t": (FF, FF), "h2t": (FF, FF),
    "w3t": (FF, D),
}

SMALL_SHAPES = {
    "wni": [128, 8], "convb": [128, 8], "convw": [128, 32],
    "bau": [64, 1], "bad": [128, 4], "bbeta": [8, 1], "bu1": [64, 1],
    "bu2": [128, 8], "wffn": [128, 8], "whn": [128, 8],
    "s1": [128, 32], "s2": [128, 32], "hs1": [128, 32], "hs2": [128, 32],
    "s3": [128, 8],
    "asp": [128, 4], "bseg8": [128, 32],
    "ident": [128, 128], "maskM": [128, 128], "maskG": [128, 128],
    "seg8": [128, 32], "headmap": [8, 512], "onesrow": [1, 128],
}


class Sched:
    def __init__(self, nc):
        self.nc = nc
        self.items = {e: [] for e in ENGINES}
        self.sems = {}
        self.counts = {}
        self.last_wait = {e: {} for e in ENGINES}

    def sem(self, name):
        if name not in self.sems:
            self.sems[name] = self.nc.alloc_semaphore(name)
            self.counts[name] = 0
        return self.sems[name]

    SELF_SEM = {"vector": "V", "scalar": "S", "gpsimd": "G"}

    def op(self, engine, fn, waits=(), inc=None, n=1, tag="", chain=False):
        waits = list(waits)
        ssem = self.SELF_SEM.get(engine)
        if chain and ssem is not None:
            self.sem(ssem)
            waits.insert(0, (ssem, self.counts[ssem]))
        tok = None
        if inc is not None:
            self.sem(inc)
            self.counts[inc] += n
            tok = (inc, self.counts[inc])
        lw = self.last_wait[engine]
        eff = []
        for w in waits:
            if w is None:
                continue
            sname, v = w
            self.sem(sname)
            if v > lw.get(sname, 0):
                eff.append((sname, v))
                lw[sname] = v
        self.items[engine].append((fn, tuple(eff), tok, n, tag))
        return tok

    def replay(self, engine, eng):
        for fn, waits, tok, n, tag in self.items[engine]:
            for (sname, v) in waits:
                eng.wait_ge(self.sems[sname], v)
            inst = fn(eng)
            if tok is not None:
                if inst is None:
                    raise RuntimeError(f"inc-op returned None: {tag}")
                inst.then_inc(self.sems[tok[0]], n)


class Bank:
    def __init__(self, ps, name):
        self.ps = ps
        self.psb = ps[:].bitcast(mybir.dt.bfloat16)
        self.name = name
        self.free = None
        self.ready = None


def ceil_div(a, b):
    return -(-a // b)


def build_program(T=TFULL, dbg=()):
    nc = bass.Bass("TRN2", target_bir_lowering=False)
    ntiles = T // CT

    ntl = T // CT
    x_fm = nc.dram_tensor("x_fm", [8, 128, ntl, BSEQ, CT], F32,
                          kind="ExternalInput").ap()
    out_fm = nc.dram_tensor("out_fm", [8, 128, ntl, BSEQ, CT], F32,
                            kind="ExternalOutput").ap()
    w_in = {k: nc.dram_tensor("w_" + k, list(s), F32, kind="ExternalInput").ap()
            for k, s in WSHAPES.items()}
    stripe_index = {}
    stripe_list = []
    _ns = [0]

    def _plan(wname, group=2, special=None):
        K, O = WSHAPES[wname]
        kchunks = ceil_div(K, 128)
        otiles = ceil_div(O, 128)
        ents = []
        if special == "v":
            for kq in range(0, kchunks, 2):
                ents.append((kq, 2, 0, 1024))
        elif special == "wo":
            for ot in range(otiles):
                ents.append((0, kchunks, ot, 128))
        else:
            for og0 in range(0, otiles, group):
                ow = sum(min(128, O - ot * 128)
                         for ot in range(og0, min(og0 + group, otiles)))
                kstr = max(1, 2048 // ow) if K % 128 == 0 else 1
                for kq in range(0, kchunks, kstr):
                    nk = min(kstr, kchunks - kq)
                    ents.append((kq, nk, og0, ow))
        for (kq, nk, og0, ow) in ents:
            rows = min(128, K - kq * 128)
            si = _ns[0]
            _ns[0] += 1
            stripe_index[(wname, og0, kq)] = (si, nk, ow, rows)
            stripe_list.append((wname, si, kq, nk, og0, ow, rows))

    for _w in ("wq", "wk", "wsp"):
        _plan(_w)
    _plan("wv", special="v")
    for _w in ("wau", "wad", "wbeta", "wu1", "wu2"):
        _plan(_w)
    _plan("wout", special="wo")
    for _w in ("w1t", "w2t", "h1t", "h2t", "w3t", "wff1", "wff3", "wff2"):
        _plan(_w)
    wstripes = nc.dram_tensor("wstripes", [_ns[0], 128, 2048], BF16,
                              kind="Internal").ap()
    _BF_SMALL = ("maskM", "maskG")
    sm_in = {k: nc.dram_tensor("p_" + k, s, BF16 if k in _BF_SMALL else F32,
                               kind="ExternalInput").ap()
             for k, s in SMALL_SHAPES.items()}

    sbuf = lambda n, shape, dt: nc.alloc_sbuf_tensor(n, shape, dt)
    P = {k: sbuf("sp_" + k, SMALL_SHAPES[k],
                 BF16 if k in _BF_SMALL else F32) for k in SMALL_SHAPES}
    P["onescol"] = P["maskG"][:, 127:128]
    ident_bf = sbuf("ident_bf", [128, 128], BF16)
    seg8_bf = sbuf("seg8_bf", [128, 32], BF16)
    bseg8_bf = sbuf("bseg8_bf", [128, 32], BF16)


    NPOOL = 5
    POOLW = 2048
    pool = [sbuf(f"pool_{i}", [128, POOLW], BF16) for i in range(NPOOL)]
    pool_state = [None] * NPOOL
    pool_i = [0]

    OUT = sbuf("OUT", [128, 8, NT], F32)
    xn = sbuf("xn", [128, 8, NT], F32)
    xn_bf = sbuf("xn_bf", [128, 8, NT], BF16)
    scr2 = sbuf("scr2", [128, 2, NT], BF16)
    h1h = sbuf("h1h", [128, 8, BSEQ, CT + 3], BF16)
    convacc = sbuf("convacc", [128, BSEQ, CT], F32)
    gate_sc = convacc[:].rearrange("p s t -> p (s t)")
    hc_bf = sbuf("hc_bf", [128, 8, NT], BF16)
    y_bf = hc_bf
    y2 = sbuf("y2", [128, 8, NT], F32)
    qf = y2[:, 0:4, :]
    kf = y2[:, 4:8, :]
    qn_bf = sbuf("qn_bf", [128, 4, NT], BF16)
    kn_bf = sbuf("kn_bf", [128, 4, NT], BF16)
    v_tm = sbuf("v_tm", [128, BSEQ, H * DV], BF16)
    au_bf = sbuf("au_bf", [64, NT], BF16)
    g1_bf = sbuf("g1_bf", [64, NT], BF16)
    bb8 = sbuf("bb8", [8, NT], F32)
    active8 = sbuf("active8", [8, NT], F32)
    beta8 = bb8
    beta_tok = sbuf("beta_tok", [128, BSEQ, 8], F32)
    nq8 = sbuf("nq8", [8, NT], F32)
    nk8 = nq8
    stat_t = active8[0:1, :]
    stat_v = sbuf("stat_v", [1, NT], F32)
    memst = sbuf("memst", [128, 16], F32)
    sp_tmp = sbuf("sp_tmp", [128, 16], F32)
    apad = sbuf("apad", [128, BSEQ, NSB, SB + 1], F32)
    rho = sbuf("rho", [128, BSEQ, NSB, SB + 1], BF16)
    pcum = sbuf("pcum", [128, BSEQ, NSB, SB + 1], F32)
    rrev = sbuf("rrev", [128, BSEQ, NSB, SB + 1], BF16)
    rhorec = sbuf("rhorec", [128, BSEQ, NSB, SB + 1], BF16)
    CHOFF = [8 * b * b + 8 * b for b in range(NSB + 1)]
    chist = sbuf("chist", [128, BSEQ, CHOFF[NSB]], BF16)
    shA = sbuf("shA", [128, 4096], BF16)
    ab_bf = shA[:, 0:2048].rearrange("p (g t) -> p g t", g=4)
    spike_fm = shA[:, 2048:4096].rearrange("p (g t) -> p g t", g=4)
    kqh = y2[:, 0:4, :].rearrange("p a b -> p (a b)").bitcast(BF16).rearrange(
        "p (g s b j) -> p g s b j", g=4, s=BSEQ, b=NSB)
    shB = sbuf("shB", [128, 4096], BF16)
    sd_scan = shB[:].bitcast(F32).rearrange("p (t x) -> p t x", t=CT)
    khat = shB[:, 0:2048].rearrange("p (g t) -> p g t", g=4)
    qtl = shB[:, 2048:4096].rearrange("p (g t) -> p g t", g=4)
    shC = sbuf("shC", [128, 2048], BF16)
    spike_scan = shC[:].rearrange("p (t x) -> p t x", t=CT)
    kap = shC[:].rearrange("p (g t) -> p g t", g=4)
    S_all = sbuf("S_all", [128, 16, DV], F32)
    S_bf = sbuf("S_bf", [128, DV], BF16)
    Mhat = sbuf("Mhat", [128, 128], BF16)
    Ghat = sbuf("Ghat", [128, 128], BF16)
    Xp = [Mhat] + [sbuf(f"Xp_{i}", [128, 128], BF16) for i in range(1, 7)]
    Yp = [sbuf(f"Yp_{i}", [128, 128], BF16) for i in range(6)]
    Rping = [sbuf(f"Rp_{i}", [128, 128], BF16) for i in range(2)]
    khatT = sbuf("khatT", [128, 64], BF16)
    Ocomb = sbuf("Ocomb", [128, 128], F32)
    On_bf = sbuf("On_bf", [128, 128], BF16)
    rms_col = sbuf("rms_col", [128, 4], F32)
    a_big = sbuf("a_big", [128, 32, NTH], BF16)
    f2_bf = sbuf("f2_bf", [128, 32, NTH], BF16)
    stat_m = sbuf("stat_m", [1, NT], F32)

    banks = [Bank(nc.alloc_psum_tensor(f"bank{i}", [128, 512], F32), f"b{i}")
             for i in range(8)]
    B_MAIN = banks[0:4]
    B_BC = banks[4:6]
    B_BUILD = banks[6]
    B_S1, B_S2 = banks[4], banks[5]
    B_MISC = banks[7]

    s = Sched(nc)

    dbg_out = {}
    dbg_shapes = {
        "xn": [128, 8, NT], "hc": [128, 8, NT], "q": [128, 4, NT],
        "k": [128, 4, NT], "sd": [128, CT, 16], "ab": [128, 4, NT],
        "spike": [128, 4, NT], "alpha": [128, BSEQ, NSB, SB + 1],
        "beta": [8, NT], "vtm": [128, BSEQ, H * DV], "y": [128, 8, NT],
        "y2": [128, 8, NT], "gate": [128, NT], "xbig": [128, 32, NTH],
        "f2": [128, 32, NTH],
    }
    for name in dbg:
        dbg_out[name] = nc.dram_tensor("dbg_" + name, dbg_shapes[name], F32,
                                       kind="ExternalOutput").ap()
    dbg_src = {
        "xn": xn, "hc": hc_bf, "q": qn_bf, "k": kn_bf, "sd": sd_scan,
        "ab": ab_bf, "spike": spike_fm, "alpha": apad, "beta": beta8,
        "vtm": v_tm, "y": y_bf, "y2": y2, "gate": gate_sc, "xbig": a_big,
        "f2": f2_bf,
    }
    dbg_stage = [None, None]
    def emit_dbg(name, waits):
        if name not in dbg_out:
            return
        src = dbg_src[name]
        srcap = src if isinstance(src, bass.AP) else src[:]
        if srcap.dtype != F32:
            if dbg_stage[0] is None:
                dbg_stage[0] = f2_bf[:].rearrange("p a b -> p (a b)").bitcast(F32)
            n = int(np.prod(dbg_shapes[name][1:]))
            stg = dbg_stage[0][:, 0:n]
            w = list(waits) + ([dbg_stage[1]] if dbg_stage[1] else [])
            t = s.op("vector", lambda e, stg=stg, srcap=srcap: e.tensor_copy(
                stg, srcap.rearrange(
                    " ".join(["p"] + [chr(97 + i) for i in range(srcap.ndim - 1)])
                    + " -> p (" + " ".join(chr(97 + i) for i in range(srcap.ndim - 1))
                    + ")") if srcap.ndim > 2 else srcap),
                waits=w, inc="V", tag=f"dbgcast_{name}", chain=True)
            dbg_stage[1] = s.op("sync", lambda e, stg=stg, name=name, _w=None: e.dma_start(
                out=dbg_out[name].rearrange(
                    " ".join(["p"] + [chr(97 + i) for i in range(len(dbg_shapes[name]) - 1)])
                    + " -> p (" + " ".join(chr(97 + i) for i in range(len(dbg_shapes[name]) - 1))
                    + ")") if len(dbg_shapes[name]) > 2 else dbg_out[name],
                in_=stg), waits=[t] + ([dbg_stage[1]] if dbg_stage[1] else []),
                inc="DD", n=16, tag=f"dbg_{name}")
        else:
            dbg_stage[1] = s.op("sync", lambda e, srcap=srcap, name=name: e.dma_start(
                out=dbg_out[name], in_=srcap),
                waits=list(waits) + ([dbg_stage[1]] if dbg_stage[1] else []),
                inc="DD", n=16, tag=f"dbg_{name}")

    def bank_mm(bank, fn, first, last, waits=(), force_inc=False, tag=""):
        w = list(waits)
        if first and bank.free is not None:
            w.append(bank.free)
        tok = s.op("tensor", fn, waits=w,
                   inc="T" if (last or force_inc) else None, tag=tag)
        if last:
            bank.ready = tok

    def consume(engine, bank, fn, last=True, waits=(), tag="", chain=False):
        w = list(waits)
        if bank is not None and bank.ready is not None:
            w.append(bank.ready)
        sem = {"vector": "V", "scalar": "S", "gpsimd": "G"}[engine]
        tok = s.op(engine, fn, waits=w, inc=sem, tag=tag, chain=chain)
        if bank is not None and last:
            bank.free = tok
        return tok

    def vop(fn, waits=(), tag="", chain=False):
        return s.op("vector", fn, waits=waits, inc="V", tag=tag, chain=chain)

    def sop(fn, waits=(), tag="", chain=False):
        return s.op("scalar", fn, waits=waits, inc="S", tag=tag, chain=chain)

    def gop(fn, waits=(), tag="", chain=False):
        return s.op("gpsimd", fn, waits=waits, inc="G", tag=tag, chain=chain)

    cast_done = {}

    def stripe_load(wname, og0, kq, tag=""):
        si, nk, ow, rows = stripe_index[(wname, og0, kq)]
        i = pool_i[0]
        pool_i[0] = (i + 1) % NPOOL
        buf = pool[i]
        w = []
        if pool_state[i] is not None:
            w.append(pool_state[i])
        for t_ in cast_done.get(wname, ()):
            w.append(t_)
        tok = s.op("sync",
                   lambda e, buf=buf, si=si, rows=rows, nk=nk, ow=ow:
                   e.dma_start(out=buf[0:rows, 0:nk * ow],
                               in_=wstripes[si, 0:rows, 0:nk * ow]),
                   waits=w, inc=f"DW{i}", n=16, tag=f"ld_{wname}_{og0}_{kq}" + tag)
        return buf, tok, i

    def stripe_used(i):
        pool_state[i] = ("T", s.counts["T"])

    def proj_gen(wname, mov_view, out_cb, mov_toks=(), group=2, nt=NT, tag=""):
        K, O = WSHAPES[wname]
        kchunks = ceil_div(K, 128)
        otiles = ceil_div(O, 128)
        for ogidx, og0 in enumerate(range(0, otiles, group)):
            ots = list(range(og0, min(og0 + group, otiles)))
            ow = sum(min(128, O - ot * 128) for ot in ots)
            kstr = max(1, 2048 // ow) if K % 128 == 0 else 1
            boff = 2 * (ogidx % 2) if group <= 2 else 0
            for kq in range(0, kchunks, kstr):
                nk = min(kstr, kchunks - kq)
                buf, ltok, bi = stripe_load(wname, og0, kq, tag=tag)
                for kk in range(nk):
                    ki = kq + kk
                    rows = min(128, K - ki * 128)
                    for j, ot in enumerate(ots):
                        bank = B_MAIN[boff + j]
                        osz = min(128, O - ot * 128)
                        mv = mov_view(ki)
                        bank_mm(bank,
                                lambda e, buf=buf, j=j, mv=mv, bank=bank,
                                       kk=kk, ow=ow, osz=osz, rows=rows, nt=nt,
                                       ki=ki:
                                e.matmul(bank.ps[0:osz, 0:nt],
                                         buf[0:rows, kk * ow + j * 128:
                                             kk * ow + j * 128 + osz], mv,
                                         start=(ki == 0),
                                         stop=(ki == kchunks - 1)),
                                first=(ki == 0), last=(ki == kchunks - 1),
                                force_inc=(j == len(ots) - 1 and kk == nk - 1),
                                waits=[ltok] + list(mov_toks),
                                tag=f"{tag}{wname}_mm{ot}_{ki}")
                stripe_used(bi)
                yield
            for j, ot in enumerate(ots):
                out_cb(ot, B_MAIN[boff + j])

    def proj(*a, **kw):
        for _ in proj_gen(*a, **kw):
            pass

    def pump(gen, n):
        for _ in range(n):
            if next(gen, "DONE") == "DONE":
                return False
        return True

    for k in SMALL_SHAPES:
        s.op("sync", lambda e, k=k: e.dma_start(out=P[k][:], in_=sm_in[k]),
             inc="DP", n=16, tag=f"p_{k}")
    allp = ("DP", s.counts["DP"])
    vop(lambda e: e.tensor_copy(ident_bf[:], P["ident"][:]), waits=[allp])
    vop(lambda e: e.tensor_copy(seg8_bf[:], P["seg8"][:]))
    vop(lambda e: e.tensor_copy(bseg8_bf[:], P["bseg8"][:]))
    cst = sbuf("cst", [128, 3], F32)
    vop(lambda e: e.memset(cst[:, 0:1], LNEPS))
    vop(lambda e: e.memset(cst[:, 1:2], EPS))
    vop(lambda e: e.memset(cst[:, 2:3], 1.0 / DV))
    vop(lambda e: e.memset(memst[:], 0.0))
    vop(lambda e: e.memset(h1h[:], 0.0))
    vop(lambda e: e.memset(S_all[:], 0.0))
    vop(lambda e: e.memset(apad[:], 1.0))
    vop(lambda e: e.memset(B_BUILD.ps[:, 0:256], 0.0))
    vop(lambda e: e.memset(rho[:], 1.0))
    vop(lambda e: e.memset(rrev[:], 1.0))
    vop(lambda e: e.memset(pcum[:], 1.0))
    init_tok = vop(lambda e: e.memset(chist[:], 0.0))
    consts_tok = init_tok

    cast_order = ["wq", "wk", "wv", "wsp", "wau", "wad", "wbeta", "wu1",
                  "wu2", "w1t", "w2t", "wff1", "wff3", "h1t", "h2t", "wff2",
                  "w3t", "wout"]
    _by_w = {}
    for ent in stripe_list:
        _by_w.setdefault(ent[0], []).append(ent)
    ci = 0
    cb_state = [init_tok, init_tok]
    for wname in cast_order:
        K, O = WSHAPES[wname]
        tout = None
        for (_, si, kq, nk, og0, ow, rows) in _by_w[wname]:
            if wname == "wv":
                o0 = 0
            elif wname == "wout":
                o0 = og0 * 128
            else:
                o0 = og0 * 128
            par = ci % 2
            if nk == 1:
                tout = s.op("gpsimd",
                            lambda e, si=si, wname=wname, kq=kq, o0=o0, ow=ow,
                                   rows=rows:
                            e.dma_start(out=wstripes[si, 0:rows, 0:ow],
                                        in_=w_in[wname][kq * 128:kq * 128 + rows,
                                                        o0:o0 + ow]),
                            waits=[cb_state[par]], inc=f"DCG{par}", n=16,
                            tag=f"cast_{wname}_{si}")
            else:
                tout = s.op("gpsimd",
                            lambda e, si=si, wname=wname, kq=kq, o0=o0, ow=ow,
                                   nk=nk:
                            e.dma_start(
                                out=wstripes[si, :, 0:nk * ow].rearrange(
                                    "p (n c) -> p n c", n=nk),
                                in_=w_in[wname][kq * 128:(kq + nk) * 128,
                                                o0:o0 + ow].rearrange(
                                    "(n p) c -> p n c", p=128)),
                            waits=[cb_state[par]], inc=f"DCG{par}", n=16,
                            tag=f"cast_{wname}_{si}")
            cb_state[par] = tout
            ci += 1
        cast_done[wname] = (cb_state[0], cb_state[1])
    cast_all = tout

    halo_tok = init_tok
    prev_scan_tok = init_tok
    out_final_toks = []
    for ti in range(ntiles):
        t0 = ti * CT

        xtoks = []
        for c in range(8):
            xtoks.append(s.op("sync",
                lambda e, c=c, ti=ti: e.dma_start(
                    out=OUT[:, c, :].rearrange("p (s t) -> p s t", s=BSEQ),
                    in_=x_fm[c, :, ti]),
                waits=([init_tok] if ti == 0 else [out_final_toks[-1]]),
                inc="DX", n=16, tag=f"x_{ti}_{c}"))
        xtok_all = xtoks[-1]

        xb_toks = []
        for c in range(8):
            xb_toks.append(sop(lambda e, c=c: e.copy(xn_bf[:, c, :], OUT[:, c, :]),
                               waits=[xtok_all], tag="xbf"))
        bs = B_BC[0]
        for c in range(8):
            bank_mm(bs, lambda e, c=c, bs=bs: e.matmul(
                bs.ps[0:1, 0:NT], P["onescol"], xn_bf[:, c, :],
                start=(c == 0), stop=(c == 7)),
                first=(c == 0), last=(c == 7),
                waits=[xb_toks[c], consts_tok], tag="lnsum")
        bs2 = B_BC[1]
        sq_toks = []
        for c in range(8):
            w = [xtok_all]
            if c >= 2:
                w.append(sq_toks[c - 2][1])
            tsq = sop(lambda e, c=c: e.activation(
                out=scr2[:, c % 2, :], in_=OUT[:, c, :], func=AF.Square),
                waits=w, tag="xsq")
            bank_mm(bs2, lambda e, c=c, bs2=bs2: e.matmul(
                bs2.ps[0:1, 0:NT], P["onescol"], scr2[:, c % 2, :],
                start=(c == 0), stop=(c == 7)),
                first=(c == 0), last=(c == 7), force_inc=True,
                waits=[tsq], tag="lnsum2")
            sq_toks.append((tsq, ("T", s.counts["T"])))
        consume("vector", bs, lambda e, bs=bs: e.tensor_scalar(
            out=stat_m[:], in0=bs.ps[0:1, 0:NT], scalar1=1.0 / D, scalar2=None,
            op0=ALU.mult), last=True, tag="ln_m")
        consume("vector", bs2, lambda e, bs2=bs2: e.tensor_scalar(
            out=stat_v[:], in0=bs2.ps[0:1, 0:NT], scalar1=1.0 / D, scalar2=None,
            op0=ALU.mult), last=True, tag="ln_v")
        vop(lambda e: e.tensor_mul(stat_t[:], stat_m[:], stat_m[:]), tag="ln_m2", chain=True)
        tv = vop(lambda e: e.tensor_sub(stat_v[:], stat_v[:], stat_t[:]), tag="ln_var", chain=True)
        tv = sop(lambda e: e.activation(out=stat_v[:], in_=stat_v[:], func=AF.Sqrt,
                                        bias=cst[0:1, 0:1]), waits=[tv], tag="ln_sd")
        tv = vop(lambda e: e.reciprocal(stat_v[:], stat_v[:]), waits=[tv], tag="ln_rs", chain=True)
        tm = ("V", s.counts["V"])
        bm, br = B_BC[0], B_BC[1]
        bank_mm(bm, lambda e, bm=bm: e.matmul(
            bm.ps[0:128, 0:NT], P["onesrow"][0:1, :], stat_m[:],
            start=True, stop=True), first=True, last=True, waits=[tm], tag="bc_m")
        bank_mm(br, lambda e, br=br: e.matmul(
            br.ps[0:128, 0:NT], P["onesrow"][0:1, :], stat_v[:],
            start=True, stop=True), first=True, last=True, waits=[tv], tag="bc_rs")
        xn_toks = []
        for c in range(8):
            consume("vector", bm, lambda e, c=c, bm=bm: e.tensor_sub(
                xn[:, c, :], OUT[:, c, :], bm.ps[0:128, 0:NT]),
                last=(c == 7), tag="xn_sub")
            consume("vector", br, lambda e, c=c, br=br: e.tensor_mul(
                xn[:, c, :], xn[:, c, :], br.ps[0:128, 0:NT]),
                last=(c == 7), tag="xn_mul", chain=True)
            xn_toks.append(("V", s.counts["V"]))
        xnbf_toks = []
        for c in range(8):
            xnbf_toks.append(sop(lambda e, c=c: e.copy(xn_bf[:, c, :], xn[:, c, :]),
                                 waits=[xn_toks[c]], tag="xnbf"))
        xnbf_all = xnbf_toks[-1]
        emit_dbg("xn", [xn_toks[-1]])

        bs = B_BC[0]
        sq_toks = []
        for c in range(8):
            w = [xn_toks[c]]
            if c >= 2:
                w.append(sq_toks[c - 2][1])
            tsq = sop(lambda e, c=c: e.activation(
                out=scr2[:, c % 2, :], in_=xn[:, c, :], func=AF.Square),
                waits=w, tag="h1sq")
            bank_mm(bs, lambda e, c=c, bs=bs: e.matmul(
                bs.ps[0:1, 0:NT], P["onescol"], scr2[:, c % 2, :],
                start=(c == 0), stop=(c == 7)),
                first=(c == 0), last=(c == 7), force_inc=True,
                waits=[tsq], tag="h1sum")
            sq_toks.append((tsq, ("T", s.counts["T"])))
        consume("vector", bs, lambda e, bs=bs: e.tensor_scalar(
            out=stat_m[:], in0=bs.ps[0:1, 0:NT], scalar1=1.0 / D, scalar2=None,
            op0=ALU.mult), tag="h1_ms")
        tv = sop(lambda e: e.activation(out=stat_m[:], in_=stat_m[:], func=AF.Sqrt,
                                        bias=cst[0:1, 1:2]), waits=[("V", s.counts["V"])],
                 tag="h1_sd")
        tv = vop(lambda e: e.reciprocal(stat_m[:], stat_m[:]), waits=[tv], tag="h1_rs", chain=True)
        br = B_BC[1]
        bank_mm(br, lambda e, br=br: e.matmul(
            br.ps[0:128, 0:NT], P["onesrow"][0:1, :], stat_m[:],
            start=True, stop=True), first=True, last=True, waits=[tv], tag="bc_h1")
        h1_toks = []
        for c in range(8):
            consume("vector", br, lambda e, c=c, br=br: e.scalar_tensor_tensor(
                out=h1h[:, c, :, 3:3 + CT],
                in0=xn[:, c, :].rearrange("p (s t) -> p s t", s=BSEQ),
                scalar=P["wni"][:, c:c + 1],
                in1=br.ps[0:128, 0:NT].rearrange("p (s t) -> p s t", s=BSEQ),
                op0=ALU.mult, op1=ALU.mult),
                last=(c == 7), waits=[halo_tok, xn_toks[c]], tag="h1")
            h1_toks.append(("V", s.counts["V"]))
        hc_toks = []
        for c in range(8):
            w = [h1_toks[c]]
            if c > 0:
                w.append(hc_toks[c - 1])
            vop(lambda e, c=c: e.tensor_scalar(
                out=convacc[:], in0=h1h[:, c, :, 0:CT],
                scalar1=P["convw"][:, 4 * c:4 * c + 1],
                scalar2=P["convb"][:, c:c + 1], op0=ALU.mult, op1=ALU.add),
                waits=w, tag="conv0", chain=True)
            for j in range(1, 4):
                vop(lambda e, c=c, j=j: e.scalar_tensor_tensor(
                    out=convacc[:], in0=h1h[:, c, :, j:j + CT],
                    scalar=P["convw"][:, 4 * c + j:4 * c + j + 1],
                    in1=convacc[:], op0=ALU.mult, op1=ALU.add), tag=f"conv{j}", chain=True)
            hc_toks.append(sop(lambda e, c=c: e.activation(
                out=hc_bf[:, c, :], in_=convacc[:].rearrange("p s t -> p (s t)"),
                func=AF.Silu), waits=[("V", s.counts["V"])], tag="hc"))
        hc_all = hc_toks[-1]
        halo_tok = gop(lambda e: e.tensor_copy(
            h1h[:, :, :, 0:3], h1h[:, :, :, CT:CT + 3]),
            waits=[hc_all] + h1_toks, tag="halo", chain=True)
        emit_dbg("hc", [hc_all])

        hcv = lambda ki: hc_bf[:, ki, :]
        def sd_cb(ot, bank):
            consume("vector", bank, lambda e, ot=ot, bank=bank: e.tensor_copy(
                sd_scan[:, :, 4 * ot:4 * ot + 4].transpose([0, 2, 1]),
                bank.ps[0:128, 0:NT].rearrange("p (s t) -> p s t", s=BSEQ)),
                waits=[prev_scan_tok], tag="sd_ev")
        proj("wsp", hcv, sd_cb, mov_toks=[hc_all], tag="sd_")
        sd_tok = ("V", s.counts["V"])
        emit_dbg("sd", [sd_tok])
        def q_cb(ot, bank):
            consume("vector", bank, lambda e, ot=ot, bank=bank: e.tensor_copy(
                qf[:, ot, :], bank.ps[0:128, 0:NT]), tag="q_ev")
        proj("wq", hcv, q_cb, mov_toks=[hc_all], tag="q_")
        q_tok = ("V", s.counts["V"])
        def k_cb(ot, bank):
            consume("vector", bank, lambda e, ot=ot, bank=bank: e.tensor_copy(
                kf[:, ot, :], bank.ps[0:128, 0:NT]), tag="k_ev")
        proj("wk", hcv, k_cb, mov_toks=[hc_all], tag="k_")
        k_tok = ("V", s.counts["V"])
        vtoks = []
        for rnd in range(2):
            tcs = (2 * rnd, 2 * rnd + 1)
            for kq in range(0, 8, 2):
                buf, ltok, bi = stripe_load("wv", 0, kq, tag="v")
                for kk in range(2):
                    ki = kq + kk
                    for j, tc in enumerate(tcs):
                        for oh in range(2):
                            bank = B_MAIN[2 * j + oh]
                            bank_mm(bank,
                                    lambda e, buf=buf, ki=ki, kk=kk, tc=tc,
                                           oh=oh, bank=bank:
                                    e.matmul(bank.ps[0:128, 0:512],
                                             hc_bf[:, ki, tc * 128:(tc + 1) * 128],
                                             buf[:, kk * 1024 + oh * 512:
                                                 kk * 1024 + (oh + 1) * 512],
                                             start=(ki == 0), stop=(ki == 7)),
                                    first=(ki == 0), last=(ki == 7),
                                    force_inc=(j == 1 and oh == 1 and kk == 1),
                                    waits=[ltok, hc_all], tag=f"v_mm{tc}_{oh}_{ki}")
                stripe_used(bi)
            for j, tc in enumerate(tcs):
                for oh in range(2):
                    bank = B_MAIN[2 * j + oh]
                    vtoks.append(consume("scalar", bank,
                        lambda e, tc=tc, oh=oh, bank=bank: e.copy(
                            v_tm[:, tc, oh * 512:(oh + 1) * 512],
                            bank.ps[0:128, 0:512]), tag="v_ev"))
        v_tok = vtoks[-1]
        emit_dbg("vtm", [v_tok])
        def au_cb(ot, bank):
            consume("scalar", bank, lambda e, bank=bank: e.activation(
                out=au_bf[:, :], in_=bank.ps[0:64, 0:NT], func=AF.Silu,
                bias=P["bau"][:, 0:1]), tag="au_ev")
        proj("wau", hcv, au_cb, mov_toks=[hc_all], tag="au_")
        au_tok = ("S", s.counts["S"])
        def ab_cb(ot, bank):
            consume("vector", bank, lambda e, ot=ot, bank=bank: e.tensor_scalar(
                out=ab_bf[:, ot, :], in0=bank.ps[0:128, 0:NT],
                scalar1=P["bad"][:, ot:ot + 1], scalar2=None, op0=ALU.add),
                tag="ab_ev")
        proj("wad", lambda ki: au_bf[:, :], ab_cb, mov_toks=[au_tok], tag="ab_")
        ab_tok = ("V", s.counts["V"])
        emit_dbg("ab", [ab_tok])
        def bb_cb(ot, bank):
            consume("vector", bank, lambda e, bank=bank: e.tensor_scalar(
                out=bb8[:], in0=bank.ps[0:8, 0:NT], scalar1=P["bbeta"][:, 0:1],
                scalar2=None, op0=ALU.add), tag="bb_ev")
        proj("wbeta", hcv, bb_cb, mov_toks=[hc_all], tag="bb_")
        bb_tok = ("V", s.counts["V"])
        def g1_cb(ot, bank):
            consume("scalar", bank, lambda e, bank=bank: e.activation(
                out=g1_bf[:, :], in_=bank.ps[0:64, 0:NT], func=AF.Silu,
                bias=P["bu1"][:, 0:1]), tag="g1_ev")
        proj("wu1", lambda ki: xn_bf[:, ki, :], g1_cb, mov_toks=[xnbf_all], tag="g1_")
        g1_tok = ("S", s.counts["S"])

        sqguard = []
        for (src, dstn, n8, tg) in ((qf, qn_bf, nq8, "q"), (kf, kn_bf, nk8, "k")):
            bs = B_BC[0]
            for g in range(4):
                w = [q_tok if tg == "q" else k_tok]
                if len(sqguard) >= 2:
                    w.append(sqguard[-2])
                tsq = sop(lambda e, g=g, src=src: e.activation(
                    out=scr2[:, g % 2, :], in_=src[:, g, :], func=AF.Square),
                    waits=w, tag=tg + "sq")
                bank_mm(bs, lambda e, g=g, bs=bs: e.matmul(
                    bs.ps[0:8, 0:NT], seg8_bf[:, 8 * g:8 * g + 8],
                    scr2[:, g % 2, :], start=(g == 0), stop=(g == 3)),
                    first=(g == 0), last=(g == 3), force_inc=True,
                    waits=[tsq], tag=tg + "ss")
                sqguard.append(("T", s.counts["T"]))
            consume("scalar", bs, lambda e, bs=bs, n8=n8: e.activation(
                out=n8[:], in_=bs.ps[0:8, 0:NT], func=AF.Sqrt), tag=tg + "_sd")
            tv = vop(lambda e, n8=n8: e.tensor_scalar(
                out=n8[:], in0=n8[:], scalar1=EPS, scalar2=None, op0=ALU.add),
                waits=[("S", s.counts["S"])], tag=tg + "_eps")
            tv = vop(lambda e, n8=n8: e.reciprocal(n8[:], n8[:]), waits=[tv],
                     tag=tg + "_rec", chain=True)
            for g in range(4):
                br = B_BC[1]
                bank_mm(br, lambda e, g=g, br=br, n8=n8: e.matmul(
                    br.ps[0:128, 0:NT], P["headmap"][0:8, g * 128:(g + 1) * 128],
                    n8[:], start=True, stop=True),
                    first=True, last=True, waits=[tv], tag=tg + "_bc")
                consume("vector", br, lambda e, g=g, br=br, src=src, dstn=dstn:
                        e.tensor_mul(dstn[:, g, :], src[:, g, :],
                                     br.ps[0:128, 0:NT]),
                        waits=[q_tok if tg == "q" else k_tok], tag=tg + "_nrm")
        qn_tok = kn_tok = ("V", s.counts["V"])
        emit_dbg("q", [qn_tok])
        emit_dbg("k", [kn_tok])


        tern_state = []
        def ternary_gen(half):
            tsl = slice(half * NTH, (half + 1) * NTH)
            def w1_cb(ot, bank):
                consume("scalar", bank, lambda e, ot=ot, bank=bank: e.activation(
                    out=a_big[:, ot, :], in_=bank.ps[0:128, 0:NTH], func=AF.Silu,
                    scale=P["s1"][:, ot:ot + 1]),
                    waits=([dbg_stage[1]] if dbg_stage[1] else []), tag="t1_ev")
            yield from proj_gen("w1t", lambda ki: xn_bf[:, ki, tsl], w1_cb,
                                mov_toks=[xnbf_all], nt=NTH, tag=f"t1h{half}_")
            t1_tok = ("S", s.counts["S"])
            def w2_cb(ot, bank):
                consume("vector", bank, lambda e, ot=ot, bank=bank:
                        e.scalar_tensor_tensor(
                            out=a_big[:, ot, :], in0=bank.ps[0:128, 0:NTH],
                            scalar=P["s2"][:, ot:ot + 1], in1=a_big[:, ot, :],
                            op0=ALU.mult, op1=ALU.mult),
                        waits=[t1_tok], tag="t2_ev")
            yield from proj_gen("w2t", lambda ki: xn_bf[:, ki, tsl], w2_cb,
                                mov_toks=[xnbf_all], nt=NTH, tag=f"t2h{half}_")
            f1_tok = ("V", s.counts["V"])
            if half == 0:
                emit_dbg("xbig", [f1_tok])
            def h1_cb(ot, bank):
                consume("scalar", bank, lambda e, ot=ot, bank=bank: e.activation(
                    out=f2_bf[:, ot, :], in_=bank.ps[0:128, 0:NTH], func=AF.Gelu,
                    scale=P["hs1"][:, ot:ot + 1]),
                    waits=([dbg_stage[1]] if dbg_stage[1] else []), tag="f2_ev")
            yield from proj_gen("h1t", lambda ki: a_big[:, ki, :], h1_cb,
                                mov_toks=[f1_tok], nt=NTH, tag=f"f2h{half}_")
            f2_tok = ("S", s.counts["S"])
            if half == 0:
                emit_dbg("f2", [f2_tok])
            def h2_cb(ot, bank):
                consume("scalar", bank, lambda e, ot=ot, bank=bank: e.activation(
                    out=a_big[:, ot, :], in_=bank.ps[0:128, 0:NTH], func=AF.Gelu,
                    scale=P["hs2"][:, ot:ot + 1]), tag="f3_ev")
            yield from proj_gen("h2t", lambda ki: f2_bf[:, ki, :], h2_cb,
                                mov_toks=[f2_tok], nt=NTH, tag=f"f3h{half}_")
            f3_tok = ("S", s.counts["S"])
            def w3_cb(ot, bank):
                consume("vector", bank, lambda e, ot=ot, bank=bank:
                        e.scalar_tensor_tensor(
                            out=OUT[:, ot, tsl], in0=bank.ps[0:128, 0:NTH],
                            scalar=P["s3"][:, ot:ot + 1], in1=OUT[:, ot, tsl],
                            op0=ALU.mult, op1=ALU.add),
                        waits=[f3_tok, xtok_all], tag="t3_ev", chain=True)
            yield from proj_gen("w3t", lambda ki: a_big[:, ki, :], w3_cb,
                                mov_toks=[f3_tok], nt=NTH, tag=f"t3h{half}_")
            tern_state.append(("V", s.counts["V"]))

        def _tern_chain():
            yield from ternary_gen(0)
            yield from ternary_gen(1)
        tgen = _tern_chain()
        pump(tgen, 32)

        for t in range(CT):
            w = [sd_tok] if t == 0 else []
            vop(lambda e, t=t: e.scalar_tensor_tensor(
                out=memst[:], in0=memst[:], scalar=0.9, in1=sd_scan[:, t, :],
                op0=ALU.mult, op1=ALU.add), waits=w, tag=f"ms{t}", chain=True)
            vop(lambda e, t=t: e.tensor_single_scalar(
                out=sd_scan[:, t, :], in_=memst[:], scalar=0.5, op=ALU.is_gt),
                tag=f"sp{t}", chain=True)
            vop(lambda e, t=t: e.scalar_tensor_tensor(
                out=memst[:], in0=sd_scan[:, t, :], scalar=-0.5, in1=memst[:],
                op0=ALU.mult, op1=ALU.add), tag=f"mm{t}", chain=True)
            if t % 16 == 15:
                pump(tgen, 5)
        scan_v_tok = ("V", s.counts["V"])
        spf_toks = []
        for g in range(4):
            for sq in range(BSEQ):
                spf_toks.append(gop(lambda e, g=g, sq=sq: e.tensor_copy(
                    spike_fm[:, g, sq * CT:(sq + 1) * CT],
                    sd_scan[:, :, 4 * g + sq]),
                    waits=[scan_v_tok], tag="spfm", chain=True))
        spf_all = spf_toks[-1]
        prev_scan_tok = spf_all
        emit_dbg("spike", [spf_all])
        pump(tgen, 110)

        ap_valid = apad[:, :, :, 0:SB]
        bsp, bbs = B_BC[0], B_BC[1]
        for g in range(4):
            bank_mm(bsp, lambda e, g=g, bsp=bsp: e.matmul(
                bsp.ps[0:8, 0:NT], seg8_bf[:, 8 * g:8 * g + 8], spike_fm[:, g, :],
                start=(g == 0), stop=(g == 3)),
                first=(g == 0), last=(g == 3), waits=[spf_all], tag="segsum")
            bank_mm(bbs, lambda e, g=g, bbs=bbs: e.matmul(
                bbs.ps[0:8, 0:NT], bseg8_bf[:, 8 * g:8 * g + 8], spike_fm[:, g, :],
                start=(g == 0), stop=(g == 3)),
                first=(g == 0), last=(g == 3), waits=[spf_all], tag="bsum")
        consume("vector", bsp, lambda e, bsp=bsp: e.tensor_single_scalar(
            out=active8[:], in_=bsp.ps[0:8, 0:NT], scalar=0.0, op=ALU.is_gt),
            tag="active")
        act_tok = ("V", s.counts["V"])
        consume("vector", bbs, lambda e, bbs=bbs: e.tensor_add(
            beta8[:], bb8[:], bbs.ps[0:8, 0:NT]), waits=[bb_tok], tag="blog")
        tb = sop(lambda e: e.activation(out=beta8[:], in_=beta8[:], func=AF.Sigmoid),
                 waits=[("V", s.counts["V"])], tag="bsig")
        tb = vop(lambda e: e.tensor_mul(beta8[:], beta8[:], active8[:]),
                 waits=[tb, act_tok], tag="bact", chain=True)
        emit_dbg("beta", [tb])
        bt = B_MISC
        for sq in range(BSEQ):
            bank_mm(bt, lambda e, sq=sq, bt=bt: e.transpose(
                bt.ps[0:128, 8 * sq:8 * sq + 8],
                beta8[0:8, sq * CT:(sq + 1) * CT], P["ident"][0:8, 0:8]),
                first=(sq == 0), last=(sq == 3), waits=[tb], tag="btr")
        consume("vector", bt, lambda e, bt=bt: e.tensor_copy(
            beta_tok[:].rearrange("p s h -> p (s h)"), bt.ps[0:128, 0:32]),
            tag="btr_ev")
        btok_tok = ("V", s.counts["V"])



        g_dec_toks = []
        for g in range(4):
            t1 = vop(lambda e, g=g: e.scalar_tensor_tensor(
                out=ap_valid, in0=spike_fm[:, g, :].rearrange(
                    "p (s b j) -> p s b j", s=BSEQ, b=NSB),
                scalar=P["asp"][:, g:g + 1],
                in1=ab_bf[:, g, :].rearrange("p (s b j) -> p s b j", s=BSEQ, b=NSB),
                op0=ALU.mult, op1=ALU.add),
                waits=[spf_all, ab_tok, init_tok], tag="alog")
            t1 = sop(lambda e: e.activation(
                out=ap_valid, in_=ap_valid, func=AF.Sigmoid),
                waits=[t1], tag="asig")
            br = B_BC[1]
            bank_mm(br, lambda e, g=g, br=br: e.matmul(
                br.ps[0:128, 0:NT], P["headmap"][0:8, g * 128:(g + 1) * 128],
                active8[:], start=True, stop=True),
                first=True, last=True, waits=[act_tok], tag="abc")
            vop(lambda e: e.tensor_scalar(
                out=ap_valid, in0=ap_valid, scalar1=-1.0, scalar2=None,
                op0=ALU.add), waits=[t1], tag="am1")
            consume("vector", br, lambda e, br=br: e.tensor_mul(
                ap_valid, ap_valid,
                br.ps[0:128, 0:NT].rearrange("p (s b j) -> p s b j", s=BSEQ, b=NSB)),
                tag="am2", chain=True)
            amk = vop(lambda e: e.tensor_scalar(
                out=ap_valid, in0=ap_valid, scalar1=1.0, scalar2=None,
                op0=ALU.add), tag="am3", chain=True)
            if g == 3:
                emit_dbg("alpha", [amk])
            for sq in range(BSEQ):
                for b in range(NSB):
                    w = [amk] if (sq == 0 and b == 0) else []
                    def _rho(e, sq=sq, b=b):
                        with nc.allow_low_precision(reason="bf16 cumprod store; scan state is fp32"):
                            return e.tensor_tensor_scan(
                                out=rho[:, sq, b, 0:SB], data0=apad[:, sq, b, 0:SB],
                                data1=apad[:, sq, b, 0:SB], initial=1.0,
                                op0=ALU.mult, op1=ALU.bypass)
                    vop(_rho, waits=w, tag="rho")
            FL = NSB * (SB + 1)
            for sq in range(BSEQ):
                vop(lambda e, sq=sq: e.tensor_tensor_scan(
                    out=pcum[:, sq].rearrange("p a b -> p (a b)"),
                    data0=apad[:, sq].rearrange("p a b -> p (a b)"),
                    data1=apad[:, sq].rearrange("p a b -> p (a b)"),
                    initial=1.0, op0=ALU.mult, op1=ALU.bypass), tag="pcum")
            for sq in range(BSEQ):
                def _rrev(e, sq=sq, FL=FL):
                    with nc.allow_low_precision(reason="bf16 cumprod store; scan state is fp32"):
                        return e.tensor_tensor_scan(
                            out=rrev[:, sq].rearrange("p a b -> p (a b)")[:, FL - 2::-1],
                            data0=apad[:, sq].rearrange("p a b -> p (a b)")[:, FL - 1:0:-1],
                            data1=apad[:, sq].rearrange("p a b -> p (a b)")[:, FL - 1:0:-1],
                            initial=1.0, op0=ALU.mult, op1=ALU.bypass)
                vop(_rrev, tag="rrev")
            def _rrc(e):
                with nc.allow_low_precision(reason="bf16 reciprocal store"):
                    return e.reciprocal(
                        rhorec[:].rearrange("p s b j -> p (s b j)"),
                        rho[:].rearrange("p s b j -> p (s b j)"))
            vop(_rrc, tag="rhorec", chain=True)
            scans_tok = ("V", s.counts["V"])
            knv = lambda g: kn_bf[:, g, :].rearrange("p (s b j) -> p s b j",
                                                     s=BSEQ, b=NSB)
            qnv = lambda g: qn_bf[:, g, :].rearrange("p (s b j) -> p s b j",
                                                     s=BSEQ, b=NSB)
            vop(lambda e, g=g: e.tensor_mul(
                kqh[:, g, :, :, 0:SB], knv(g), rho[:, :, :, 0:SB]),
                waits=[kn_tok, scans_tok], tag="krho", chain=True)
            vop(lambda e, g=g: e.tensor_mul(
                kqh[:, g, :, :, SB:2 * SB], qnv(g), rho[:, :, :, 0:SB]),
                waits=[qn_tok], tag="qrho")
            vop(lambda e, g=g: e.tensor_mul(
                kap[:, g, :].rearrange("p (s b j) -> p s b j", s=BSEQ, b=NSB),
                knv(g), pcum[:, :, :, 0:SB]), tag="kap")
            vop(lambda e, g=g: e.tensor_mul(
                qtl[:, g, :].rearrange("p (s b j) -> p s b j", s=BSEQ, b=NSB),
                qnv(g), pcum[:, :, :, 0:SB]), tag="qtl")
            var_tok = vop(lambda e, g=g: e.tensor_mul(
                khat[:, g, :].rearrange("p (s b j) -> p s b j", s=BSEQ, b=NSB),
                knv(g), rrev[:, :, :, 0:SB]), tag="khat")
            gop(lambda e, g=g: e.tensor_mul(
                chist[:, :, 0:SB], knv(g)[:, :, 0, :], rhorec[:, :, 0, 0:SB]),
                waits=[var_tok], tag="chi0", chain=True)
            for b in range(1, NSB):
                ob, opb = CHOFF[b], CHOFF[b - 1]
                if b > 1:
                    gop(lambda e, g=g, b=b, ob=ob, opb=opb: e.tensor_mul(
                        chist[:, :, ob:ob + SB * (b - 1)],
                        chist[:, :, opb:opb + SB * (b - 1)],
                        rho[:, :, b - 1, SB - 1:SB].broadcast_to(
                            [128, BSEQ, SB * (b - 1)])), tag=f"chiA{b}", chain=True)
                gop(lambda e, g=g, b=b, ob=ob: e.tensor_mul(
                    chist[:, :, ob + SB * (b - 1):ob + SB * b],
                    knv(g)[:, :, b - 1, :], rhorec[:, :, b - 1, 0:SB]),
                    tag=f"chiB{b}", chain=True)
                gop(lambda e, g=g, b=b, ob=ob: e.tensor_mul(
                    chist[:, :, ob + SB * (b - 1):ob + SB * b],
                    chist[:, :, ob + SB * (b - 1):ob + SB * b],
                    rho[:, :, b - 1, SB - 1:SB].broadcast_to([128, BSEQ, SB])),
                    tag=f"chiB2{b}", chain=True)
                gop(lambda e, g=g, b=b, ob=ob: e.tensor_mul(
                    chist[:, :, ob + SB * b:ob + SB * (b + 1)],
                    knv(g)[:, :, b, :], rhorec[:, :, b, 0:SB]), tag=f"chiC{b}", chain=True)
            chi_tok = ("G", s.counts["G"])
            g_dec_toks.append((("V", s.counts["V"]), chi_tok))
            pump(tgen, 8)

            for half in range(2):
                for sq in range(BSEQ):
                    h = 2 * g + half
                    P0, P1 = 64 * half, 64 * half + 64
                    slot = g * 4 + sq
                    tsl = slice(sq * CT, (sq + 1) * CT)
                    vt, ct = g_dec_toks[g]
                    sbf = vop(lambda e, P0=P0, P1=P1, slot=slot: e.tensor_copy(
                        S_bf[P0:P1, :], S_all[P0:P1, slot, :]), tag="sbf", chain=True)
                    bank_mm(B_BUILD, lambda e, P0=P0, P1=P1, tsl=tsl, g=g:
                            e.matmul(B_BUILD.ps[0:128, 256:384],
                                     kap[P0:P1, g, tsl], S_bf[P0:P1, :],
                                     start=True, stop=True),
                            first=True, last=False, waits=[sbf, vt], tag="rt_mm")
                    for b in range(NSB):
                        bank_mm(B_BUILD,
                                lambda e, P0=P0, P1=P1, sq=sq, b=b, g=g:
                                e.matmul(B_BUILD.ps[0:SB * b + SB, 32 * b:32 * b + 32],
                                         chist[P0:P1, sq, CHOFF[b]:CHOFF[b] + SB * b + SB],
                                         kqh[P0:P1, g, sq, b, :],
                                         start=True, stop=True),
                                first=False, last=(b == NSB - 1),
                                waits=[ct], tag=f"bld{b}")
                    consume("vector", B_BUILD, lambda e, sq=sq, h=h:
                            e.scalar_tensor_tensor(
                                out=Mhat[:].rearrange("p (b j) -> p b j", b=NSB),
                                in0=B_BUILD.ps[0:128, 0:256].rearrange(
                                    "p (b x j) -> p b x j", b=NSB, x=2)[:, :, 0, :],
                                scalar=beta_tok[:, sq, h:h + 1],
                                in1=P["maskM"][:].rearrange("p (b j) -> p b j", b=NSB),
                                op0=ALU.mult, op1=ALU.mult),
                            last=False, waits=[btok_tok], tag="mhat")
                    consume("vector", B_BUILD, lambda e, sq=sq, h=h:
                            e.scalar_tensor_tensor(
                                out=Ghat[:].rearrange("p (b j) -> p b j", b=NSB),
                                in0=B_BUILD.ps[0:128, 0:256].rearrange(
                                    "p (b x j) -> p b x j", b=NSB, x=2)[:, :, 1, :],
                                scalar=beta_tok[:, sq, h:h + 1],
                                in1=P["maskG"][:].rearrange("p (b j) -> p b j", b=NSB),
                                op0=ALU.mult, op1=ALU.mult),
                            last=False, tag="ghat")
                    mh_tok = ("V", s.counts["V"])
                    consume("vector", B_BUILD, lambda e, sq=sq, h=h:
                            e.tensor_sub(Rping[0][:],
                                         v_tm[:, sq, h * 128:(h + 1) * 128],
                                         B_BUILD.ps[0:128, 256:384]),
                            last=True, waits=[v_tok], tag="r0")
                    r_tok = ("V", s.counts["V"])
                    bank_mm(B_S1, lambda e: e.transpose(
                        B_S1.psb[0:128, 0:128], Mhat[:], ident_bf[:]),
                        first=True, last=True, waits=[mh_tok], tag="trN")
                    ev = consume("scalar", B_S1, lambda e: e.copy(
                        Yp[0][:], B_S1.psb[0:128, 0:128]), tag="y1_ev")
                    xy_tok = {(0, 'x'): mh_tok, (0, 'y'): ev}
                    sb_banks = [B_S2, B_S1]
                    bi_ = 0
                    for kk in range(1, 7):
                        bank = sb_banks[bi_ % 2]; bi_ += 1
                        bank_mm(bank, lambda e, kk=kk, bank=bank: e.matmul(
                            bank.ps[0:128, 0:128], Yp[kk - 1][:], Xp[kk - 1][:],
                            start=True, stop=True),
                            first=True, last=True,
                            waits=[xy_tok[(kk - 1, 'x')], xy_tok[(kk - 1, 'y')]],
                            tag=f"sqX{kk}")
                        xy_tok[(kk, 'x')] = consume("scalar", bank,
                            lambda e, kk=kk, bank=bank: e.copy(
                                Xp[kk][:], bank.ps[0:128, 0:128]), tag=f"sqXe{kk}")
                        if kk < 6:
                            bank = sb_banks[bi_ % 2]; bi_ += 1
                            bank_mm(bank, lambda e, kk=kk, bank=bank: e.matmul(
                                bank.ps[0:128, 0:128], Xp[kk - 1][:], Yp[kk - 1][:],
                                start=True, stop=True),
                                first=True, last=True, tag=f"sqY{kk}")
                            xy_tok[(kk, 'y')] = consume("scalar", bank,
                                lambda e, kk=kk, bank=bank: e.copy(
                                    Yp[kk][:], bank.ps[0:128, 0:128]),
                                tag=f"sqYe{kk}")
                    rp = 0
                    for kk in range(7, 0, -1):
                        bank = sb_banks[bi_ % 2]; bi_ += 1
                        bank_mm(bank, lambda e, kk=kk, bank=bank, rp=rp: e.matmul(
                            bank.ps[0:128, 0:128], Xp[kk - 1][:], Rping[rp][:],
                            start=True, stop=True),
                            first=True, last=True,
                            waits=[xy_tok[(kk - 1, 'x')], r_tok], tag=f"app{kk}")
                        op = ALU.subtract if kk == 1 else ALU.add
                        r_tok = consume("vector", bank,
                            lambda e, bank=bank, rp=rp, op=op: e.tensor_tensor(
                                Rping[1 - rp][:], Rping[rp][:],
                                bank.ps[0:128, 0:128], op), tag=f"appe{kk}", chain=True)
                        rp = 1 - rp
                    Wt = Rping[rp]
                    bank_mm(B_MISC, lambda e, P0=P0, P1=P1, g=g, tsl=tsl: e.matmul(
                        B_MISC.ps[0:128, 0:128], qtl[P0:P1, g, tsl], S_bf[P0:P1, :],
                        start=True, stop=True),
                        first=True, last=False, waits=[sbf], tag="o1")
                    bank_mm(B_MISC, lambda e, rp=rp: e.matmul(
                        B_MISC.ps[0:128, 128:256], Ghat[:], Rping[rp][:],
                        start=True, stop=True),
                        first=False, last=False, waits=[r_tok], tag="o2")
                    bank_mm(B_MISC, lambda e, P0=P0, P1=P1, g=g, tsl=tsl:
                            e.transpose(B_MISC.psb[0:128, 768:832],
                                        khat[P0:P1, g, tsl],
                                        ident_bf[P0:P1, P0:P1]),
                            first=False, last=True, tag="trk")
                    consume("scalar", B_MISC, lambda e: e.copy(
                        Ocomb[:], B_MISC.ps[0:128, 0:128]), last=False, tag="oc1")
                    consume("vector", B_MISC, lambda e: e.tensor_add(
                        Ocomb[:], Ocomb[:], B_MISC.ps[0:128, 128:256]),
                        last=False, waits=[("S", s.counts["S"])], tag="oc2", chain=True)
                    kt_tok = consume("vector", B_MISC,
                        lambda e, sq=sq, h=h: e.tensor_scalar(
                            out=khatT[:], in0=B_MISC.psb[0:128, 768:832],
                            scalar1=beta_tok[:, sq, h:h + 1], scalar2=None,
                            op0=ALU.mult), last=True, tag="kt_ev", chain=True)
                    oc_tok = vop(lambda e: e.scalar_tensor_tensor(
                        out=On_bf[:], in0=Ocomb[:], scalar=1.0, in1=Ocomb[:],
                        op0=ALU.mult, op1=ALU.mult,
                        accum_out=rms_col[:, 0:1]), tag="osq", chain=True)
                    tq = sop(lambda e: e.activation(
                        out=rms_col[:, 1:2], in_=rms_col[:, 0:1], func=AF.Sqrt,
                        scale=cst[:, 2:3], bias=cst[:, 1:2]),
                        waits=[oc_tok], tag="orms")
                    tq = vop(lambda e: e.reciprocal(rms_col[:, 2:3], rms_col[:, 1:2]),
                             waits=[tq], tag="orec")
                    on_tok = vop(lambda e: e.tensor_scalar(
                        out=On_bf[:], in0=Ocomb[:], scalar1=rms_col[:, 2:3],
                        scalar2=None, op0=ALU.mult), waits=[tq], tag="onorm")
                    bank_mm(B_S1, lambda e: e.transpose(
                        B_S1.psb[0:128, 0:128], On_bf[:], ident_bf[:]),
                        first=True, last=False, waits=[on_tok], tag="trO")
                    bank_mm(B_S1, lambda e, P0=P0, P1=P1, rp=rp: e.matmul(
                        B_S1.ps[P0:P1, 128:256], khatT[:], Rping[rp][:],
                        start=True, stop=True),
                        first=False, last=True, waits=[kt_tok], tag="smm")
                    consume("vector", B_S1, lambda e, h=h, sq=sq, tsl=tsl:
                            e.tensor_scalar(
                                out=y_bf[:, h, tsl], in0=B_S1.psb[0:128, 0:128],
                                scalar1=P["whn"][:, h:h + 1], scalar2=None,
                                op0=ALU.mult), last=False, tag="y_ev")
                    consume("vector", B_S1, lambda e, P0=P0, P1=P1, slot=slot, sq=sq:
                            e.scalar_tensor_tensor(
                                out=S_all[P0:P1, slot, :],
                                in0=S_all[P0:P1, slot, :],
                                scalar=pcum[P0:P1, sq, NSB - 1, SB - 1:SB],
                                in1=B_S1.ps[P0:P1, 128:256],
                                op0=ALU.mult, op1=ALU.add),
                            last=True, tag="s_upd")
                    pump(tgen, 4)
        y_tok = ("V", s.counts["V"])
        emit_dbg("y", [y_tok])

        y2_toks = []
        last_gate_read = None
        for ot in range(8):
            bank = banks[4 + ot % 2]
            buf, ltok, bi = stripe_load("wout", ot, 0, tag="wo")
            for ki in range(8):
                bank_mm(bank, lambda e, buf=buf, ki=ki, bank=bank: e.matmul(
                    bank.ps[0:128, 0:NT], buf[:, ki * 128:(ki + 1) * 128],
                    y_bf[:, ki, :], start=(ki == 0), stop=(ki == 7)),
                    first=(ki == 0), last=(ki == 7), force_inc=(ki == 7),
                    waits=[ltok, y_tok], tag=f"wo_mm{ot}_{ki}")
            stripe_used(bi)
            gbank = banks[6 + ot % 2]
            bufg, ltokg, big = stripe_load("wu2", (ot // 2) * 2, 0, tag="wu2")
            bank_mm(gbank, lambda e, bufg=bufg, gbank=gbank, ot=ot: e.matmul(
                gbank.ps[0:128, 0:NT],
                bufg[0:64, (ot % 2) * 128:(ot % 2) * 128 + 128], g1_bf[:, :],
                start=True, stop=True),
                first=True, last=True, waits=[ltokg, g1_tok], tag=f"gt_mm{ot}")
            stripe_used(big)
            consume("scalar", gbank, lambda e, ot=ot, gbank=gbank: e.activation(
                out=gate_sc[:], in_=gbank.ps[0:128, 0:NT], func=AF.Sigmoid,
                bias=P["bu2"][:, ot:ot + 1]),
                waits=[last_gate_read], tag="gate")
            gate_tok = ("S", s.counts["S"])
            last_gate_read = consume("vector", bank,
                lambda e, ot=ot, bank=bank: e.tensor_mul(
                    y2[:, ot, :], bank.ps[0:128, 0:NT], gate_sc[:]),
                waits=[gate_tok], tag="y2a", chain=True)
            y2_toks.append(vop(lambda e, ot=ot: e.tensor_add(
                y2[:, ot, :], y2[:, ot, :], xn[:, ot, :]),
                waits=[xn_toks[ot]], tag="y2b", chain=True))
            pump(tgen, 4)
        emit_dbg("y2", [y2_toks[-1]])
        emit_dbg("gate", [("S", s.counts["S"])])

        pump(tgen, 10 ** 6)
        bs = B_BC[0]
        sqts = []
        for c in range(8):
            w = [y2_toks[c]]
            if c >= 2:
                w.append(sqts[c - 2][1])
            tsq = sop(lambda e, c=c: e.activation(
                out=scr2[:, c % 2, :], in_=y2[:, c, :], func=AF.Square),
                waits=w, tag="h2sq")
            bank_mm(bs, lambda e, c=c, bs=bs: e.matmul(
                bs.ps[0:1, 0:NT], P["onescol"], scr2[:, c % 2, :],
                start=(c == 0), stop=(c == 7)),
                first=(c == 0), last=(c == 7), force_inc=True,
                waits=[tsq], tag="h2sum")
            sqts.append((tsq, ("T", s.counts["T"])))
        consume("vector", bs, lambda e, bs=bs: e.tensor_scalar(
            out=stat_m[:], in0=bs.ps[0:1, 0:NT], scalar1=1.0 / D, scalar2=None,
            op0=ALU.mult), tag="h2_ms")
        tv = sop(lambda e: e.activation(out=stat_m[:], in_=stat_m[:], func=AF.Sqrt,
                                        bias=cst[0:1, 1:2]), waits=[("V", s.counts["V"])],
                 tag="h2_sd")
        tv = vop(lambda e: e.reciprocal(stat_m[:], stat_m[:]), waits=[tv],
                 tag="h2_rs", chain=True)
        br = B_BC[1]
        bank_mm(br, lambda e, br=br: e.matmul(
            br.ps[0:128, 0:NT], P["onesrow"][0:1, :], stat_m[:],
            start=True, stop=True), first=True, last=True, waits=[tv], tag="bc_h2")
        h2_toks = []
        for c in range(8):
            h2_toks.append(consume("vector", br,
                lambda e, c=c, br=br: e.scalar_tensor_tensor(
                    out=xn_bf[:, c, :], in0=y2[:, c, :],
                    scalar=P["wffn"][:, c:c + 1], in1=br.ps[0:128, 0:NT],
                    op0=ALU.mult, op1=ALU.mult),
                last=(c == 7), waits=[y2_toks[c]], tag="h2"))
        h2_all = h2_toks[-1]

        out_toks = [None] * 8
        def swiglu(half):
            tsl = slice(half * NTH, (half + 1) * NTH)
            def ff1_cb(ot, bank):
                consume("scalar", bank, lambda e, ot=ot, bank=bank: e.activation(
                    out=a_big[:, ot, :], in_=bank.ps[0:128, 0:NTH],
                    func=AF.Silu),
                    waits=[dbg_stage[1]] if dbg_stage[1] else [], tag="ff1_ev")
            proj("wff1", lambda ki: xn_bf[:, ki, tsl], ff1_cb,
                 mov_toks=[h2_all] + tern_state, nt=NTH, tag=f"ff1h{half}_")
            s1_tok = ("S", s.counts["S"])
            def ff3_cb(ot, bank):
                consume("vector", bank, lambda e, ot=ot, bank=bank: e.tensor_mul(
                    a_big[:, ot, :], a_big[:, ot, :], bank.ps[0:128, 0:NTH]),
                    waits=[s1_tok], tag="ff3_ev")
            proj("wff3", lambda ki: xn_bf[:, ki, tsl], ff3_cb, nt=NTH,
                 tag=f"ff3h{half}_")
            a1_tok = ("V", s.counts["V"])
            def ff2_cb(ot, bank):
                consume("vector", bank, lambda e, ot=ot, bank=bank, tsl=tsl:
                        e.tensor_add(OUT[:, ot, tsl], OUT[:, ot, tsl],
                                     bank.ps[0:128, 0:NTH]),
                        waits=[a1_tok], tag="ff2_ev", chain=True)
                out_toks[ot] = vop(lambda e, ot=ot, tsl=tsl: e.tensor_add(
                    OUT[:, ot, tsl], OUT[:, ot, tsl], y2[:, ot, tsl]),
                    tag="outy2", chain=True)
            proj("wff2", lambda ki: a_big[:, ki, :], ff2_cb,
                 mov_toks=[a1_tok], nt=NTH, tag=f"ff2h{half}_")

        swiglu(0)
        swiglu(1)

        otoks = []
        for c in range(8):
            otoks.append(s.op("sync",
                lambda e, c=c, ti=ti: e.dma_start(
                    out=out_fm[c, :, ti],
                    in_=OUT[:, c, :].rearrange("p (s t) -> p s t", s=BSEQ)),
                waits=[out_toks[c]], inc="DO", n=16, tag=f"out_{ti}_{c}"))
        out_final_toks.append(otoks[-1])

    fw = [("DO", s.counts["DO"])]
    if s.counts.get("DD"):
        fw.append(("DD", s.counts["DD"]))
    s.op("sync", lambda e: e.nop(), waits=fw, tag="fin")

    with nc.Block() as block:
        @block.tensor
        def _(eng):
            s.replay("tensor", eng)

        @block.vector
        def _(eng):
            s.replay("vector", eng)

        @block.scalar
        def _(eng):
            s.replay("scalar", eng)

        @block.gpsimd
        def _(eng):
            s.replay("gpsimd", eng)

        @block.sync
        def _(eng):
            s.replay("sync", eng)

    return nc, dbg_out



def _consts():
    i128 = np.arange(128)
    maskM = (i128[:, None] < i128[None, :]).astype(np.float32)
    maskG = (i128[:, None] <= i128[None, :]).astype(np.float32)
    seg8 = np.zeros((128, 32), np.float32)
    headmap = np.zeros((8, 512), np.float32)
    for g in range(4):
        for p in range(128):
            h = 2 * g + p // 64
            seg8[p, 8 * g + h] = 1.0
            headmap[h, g * 128 + p] = 1.0
    return {
        "ident": np.eye(128, dtype=np.float32),
        "maskM": maskM.astype(BF), "maskG": maskG.astype(BF),
        "seg8": seg8, "headmap": headmap,
        "onesrow": np.ones((1, 128), np.float32),
    }


def _smalls(inp):
    f = np.float32
    fm8 = lambda v: np.ascontiguousarray(np.asarray(v, f).reshape(8, 128).T)
    out = {
        "wni": fm8(inp["L0_attn_w_norm_in"]),
        "convb": fm8(inp["L0_attn_conv_b"]),
        "convw": np.ascontiguousarray(
            np.asarray(inp["L0_attn_conv_w"], f).reshape(8, 128, 4)
            .transpose(1, 0, 2).reshape(128, 32)),
        "bau": np.asarray(inp["L0_attn_b_au"], f).reshape(64, 1),
        "bad": np.ascontiguousarray(np.asarray(inp["L0_attn_b_ad"], f).reshape(4, 128).T),
        "bbeta": np.asarray(inp["L0_attn_b_beta"], f).reshape(8, 1),
        "bu1": np.asarray(inp["L0_attn_b_u1"], f).reshape(64, 1),
        "bu2": fm8(inp["L0_attn_b_u2"]),
        "wffn": fm8(inp["L0_attn_w_ffn"]),
        "whn": np.ascontiguousarray(np.asarray(inp["L0_attn_w_hn"], f).T),
        "s1": np.ascontiguousarray(np.asarray(inp["L0_ffn_w1_s"], f).reshape(32, 128).T),
        "s2": np.ascontiguousarray(np.asarray(inp["L0_ffn_w2_s"], f).reshape(32, 128).T),
        "hs1": np.ascontiguousarray(np.asarray(inp["L0_ffn_h1_s"], f).reshape(32, 128).T),
        "hs2": np.ascontiguousarray(np.asarray(inp["L0_ffn_h2_s"], f).reshape(32, 128).T),
        "s3": np.ascontiguousarray(np.asarray(inp["L0_ffn_w3_s"], f).reshape(8, 128).T),
        "asp": np.ascontiguousarray(
            np.asarray(inp["L0_attn_alpha_spike"], f).reshape(4, 128).T),
    }
    bs = np.asarray(inp["L0_attn_beta_spike"], f).reshape(8, 64)
    bseg8 = np.zeros((128, 32), np.float32)
    for g in range(4):
        bseg8[0:64, 8 * g + 2 * g] = bs[2 * g]
        bseg8[64:128, 8 * g + 2 * g + 1] = bs[2 * g + 1]
    out["bseg8"] = bseg8
    out.update(_consts())
    return out


def _weights(inp):
    f = np.float32
    a = lambda k: np.asarray(inp[k], f)
    t = lambda k: np.ascontiguousarray(np.asarray(inp[k], f).T)
    return {
        "wq": a("L0_attn_wq"), "wk": a("L0_attn_wk"), "wv": a("L0_attn_wv"),
        "wsp": a("L0_attn_w_spike"), "wau": a("L0_attn_w_au"),
        "wad": a("L0_attn_w_ad"), "wbeta": a("L0_attn_w_beta"),
        "wu1": a("L0_attn_w_u1"), "wu2": a("L0_attn_w_u2"),
        "wout": a("L0_attn_w_out"), "wff1": a("L0_attn_w_ff1"),
        "wff3": a("L0_attn_w_ff3"), "wff2": a("L0_attn_w_ff2"),
        "w1t": t("L0_ffn_w1_w"), "w2t": t("L0_ffn_w2_w"),
        "h1t": t("L0_ffn_h1_w"), "h2t": t("L0_ffn_h2_w"),
        "w3t": t("L0_ffn_w3_w"),
    }


def x_layout(xs):
    B, T, Dm = xs.shape
    ntl = T // CT
    v = xs.transpose(2, 0, 1).reshape(8, 128, B, ntl, CT)
    return np.ascontiguousarray(v.transpose(0, 1, 3, 2, 4))


def out_delayout(o):
    _, _, ntl, B, _ = o.shape
    return o.transpose(3, 2, 4, 0, 1).reshape(B, ntl * CT, 8 * 128)


def kernel(**inputs):
    x = np.asarray(inputs["L0_attn_x"], np.float32)
    B, T, Dm = x.shape
    smalls = {"p_" + k: v for k, v in _smalls(inputs).items()}
    weights = {"w_" + k: v for k, v in _weights(inputs).items()}
    nc, _ = build_program(T=T)
    in_maps = []
    for c in range(NCORES):
        m = dict(smalls)
        m.update(weights)
        m["x_fm"] = x_layout(x[BSEQ * c:BSEQ * (c + 1)])
        in_maps.append(m)
    res = run_bass_kernel_spmd(nc, in_maps, core_ids=list(range(NCORES)))
    outs = [out_delayout(r["out_fm"]) for r in res.results]
    return np.ascontiguousarray(np.concatenate(outs, axis=0))


# version 09
# speedup vs baseline: 1.0518x; 1.0052x over previous
import numpy as np
import ml_dtypes
import concourse.bass as bass
import concourse.mybir as mybir
from concourse.bass_utils import run_bass_kernel_spmd

F32 = mybir.dt.float32
BF16 = mybir.dt.bfloat16
AF = mybir.ActivationFunctionType
ALU = mybir.AluOpType
BF = ml_dtypes.bfloat16

NCORES = 8
BSEQ = 4
TFULL = 2048
D = 1024
H, DK, DV = 8, 64, 128
RLOW = 64
FF = 4096
CT = 128
SB = 16
NSB = CT // SB
NT = BSEQ * CT
NTH = NT // 2
EPS = 1e-6
LNEPS = 1e-5

ENGINES = ("tensor", "vector", "scalar", "gpsimd", "sync")

WSHAPES = {
    "wq": (D, H * DK), "wk": (D, H * DK), "wv": (D, H * DV),
    "wsp": (D, H * DK), "wau": (D, RLOW), "wad": (RLOW, H * DK),
    "wbeta": (D, H), "wu1": (D, RLOW), "wu2": (RLOW, D),
    "wout": (H * DV, D), "wff1": (D, FF), "wff3": (D, FF), "wff2": (FF, D),
    "w1t": (D, FF), "w2t": (D, FF), "h1t": (FF, FF), "h2t": (FF, FF),
    "w3t": (FF, D),
}

SMALL_SHAPES = {
    "wni": [128, 8], "convb": [128, 8], "convw": [128, 32],
    "bau": [64, 1], "bad": [128, 4], "bbeta": [8, 1], "bu1": [64, 1],
    "bu2": [128, 8], "wffn": [128, 8], "whn": [128, 8],
    "s1": [128, 32], "s2": [128, 32], "hs1": [128, 32], "hs2": [128, 32],
    "s3": [128, 8],
    "asp": [128, 4], "bseg8": [128, 32],
    "ident": [128, 128], "maskM": [128, 128], "maskG": [128, 128],
    "seg8": [128, 32], "headmap": [8, 512], "onesrow": [1, 128],
}


class Sched:
    def __init__(self, nc):
        self.nc = nc
        self.items = {e: [] for e in ENGINES}
        self.sems = {}
        self.counts = {}
        self.last_wait = {e: {} for e in ENGINES}

    def sem(self, name):
        if name not in self.sems:
            self.sems[name] = self.nc.alloc_semaphore(name)
            self.counts[name] = 0
        return self.sems[name]

    SELF_SEM = {"vector": "V", "scalar": "S", "gpsimd": "G"}

    def op(self, engine, fn, waits=(), inc=None, n=1, tag="", chain=False):
        waits = list(waits)
        ssem = self.SELF_SEM.get(engine)
        if chain and ssem is not None:
            self.sem(ssem)
            waits.insert(0, (ssem, self.counts[ssem]))
        tok = None
        if inc is not None:
            self.sem(inc)
            self.counts[inc] += n
            tok = (inc, self.counts[inc])
        lw = self.last_wait[engine]
        eff = []
        for w in waits:
            if w is None:
                continue
            sname, v = w
            self.sem(sname)
            if v > lw.get(sname, 0):
                eff.append((sname, v))
                lw[sname] = v
        self.items[engine].append((fn, tuple(eff), tok, n, tag))
        return tok

    def replay(self, engine, eng):
        for fn, waits, tok, n, tag in self.items[engine]:
            for (sname, v) in waits:
                eng.wait_ge(self.sems[sname], v)
            inst = fn(eng)
            if tok is not None:
                if inst is None:
                    raise RuntimeError(f"inc-op returned None: {tag}")
                inst.then_inc(self.sems[tok[0]], n)


class Bank:
    def __init__(self, ps, name):
        self.ps = ps
        self.psb = ps[:].bitcast(mybir.dt.bfloat16)
        self.name = name
        self.free = None
        self.ready = None


def ceil_div(a, b):
    return -(-a // b)


def build_program(T=TFULL, dbg=()):
    nc = bass.Bass("TRN2", target_bir_lowering=False)
    ntiles = T // CT

    ntl = T // CT
    x_fm = nc.dram_tensor("x_fm", [8, 128, ntl, BSEQ, CT], F32,
                          kind="ExternalInput").ap()
    out_fm = nc.dram_tensor("out_fm", [8, 128, ntl, BSEQ, CT], F32,
                            kind="ExternalOutput").ap()
    w_in = {k: nc.dram_tensor("w_" + k, list(s), F32, kind="ExternalInput").ap()
            for k, s in WSHAPES.items()}
    stripe_index = {}
    stripe_list = []
    _ns = [0]

    def _plan(wname, group=2, special=None):
        K, O = WSHAPES[wname]
        kchunks = ceil_div(K, 128)
        otiles = ceil_div(O, 128)
        ents = []
        if special == "v":
            for kq in range(0, kchunks, 2):
                ents.append((kq, 2, 0, 1024))
        elif special == "wo":
            for ot in range(otiles):
                ents.append((0, kchunks, ot, 128))
        else:
            for og0 in range(0, otiles, group):
                ow = sum(min(128, O - ot * 128)
                         for ot in range(og0, min(og0 + group, otiles)))
                kstr = max(1, 2048 // ow) if K % 128 == 0 else 1
                for kq in range(0, kchunks, kstr):
                    nk = min(kstr, kchunks - kq)
                    ents.append((kq, nk, og0, ow))
        for (kq, nk, og0, ow) in ents:
            rows = min(128, K - kq * 128)
            si = _ns[0]
            _ns[0] += 1
            stripe_index[(wname, og0, kq)] = (si, nk, ow, rows)
            stripe_list.append((wname, si, kq, nk, og0, ow, rows))

    for _w in ("wq", "wk", "wsp"):
        _plan(_w)
    _plan("wv", special="v")
    for _w in ("wau", "wad", "wbeta", "wu1", "wu2"):
        _plan(_w)
    _plan("wout", special="wo")
    for _w in ("w1t", "w2t", "h1t", "h2t", "w3t", "wff1", "wff3", "wff2"):
        _plan(_w)
    wstripes = nc.dram_tensor("wstripes", [_ns[0], 128, 2048], BF16,
                              kind="Internal").ap()
    _BF_SMALL = ("maskM", "maskG")
    sm_in = {k: nc.dram_tensor("p_" + k, s, BF16 if k in _BF_SMALL else F32,
                               kind="ExternalInput").ap()
             for k, s in SMALL_SHAPES.items()}

    sbuf = lambda n, shape, dt: nc.alloc_sbuf_tensor(n, shape, dt)
    P = {k: sbuf("sp_" + k, SMALL_SHAPES[k],
                 BF16 if k in _BF_SMALL else F32) for k in SMALL_SHAPES}
    P["onescol"] = P["maskG"][:, 127:128]
    ident_bf = sbuf("ident_bf", [128, 128], BF16)
    seg8_bf = sbuf("seg8_bf", [128, 32], BF16)
    bseg8_bf = sbuf("bseg8_bf", [128, 32], BF16)


    NPOOL = 5
    POOLW = 2048
    pool = [sbuf(f"pool_{i}", [128, POOLW], BF16) for i in range(NPOOL)]
    pool_state = [None] * NPOOL
    pool_i = [0]

    OUT = sbuf("OUT", [128, 8, NT], F32)
    xn = sbuf("xn", [128, 8, NT], F32)
    xn_bf = sbuf("xn_bf", [128, 8, NT], BF16)
    scr2 = sbuf("scr2", [128, 2, NT], BF16)
    h1h = sbuf("h1h", [128, 8, BSEQ, CT + 3], BF16)
    convacc = sbuf("convacc", [128, BSEQ, CT], F32)
    gate_sc = convacc[:].rearrange("p s t -> p (s t)")
    hc_bf = sbuf("hc_bf", [128, 8, NT], BF16)
    y_bf = hc_bf
    y2 = sbuf("y2", [128, 8, NT], F32)
    qf = y2[:, 0:4, :]
    kf = y2[:, 4:8, :]
    qn_bf = sbuf("qn_bf", [128, 4, NT], BF16)
    kn_bf = sbuf("kn_bf", [128, 4, NT], BF16)
    v_tm = sbuf("v_tm", [128, BSEQ, H * DV], BF16)
    au_bf = sbuf("au_bf", [64, NT], BF16)
    g1_bf = sbuf("g1_bf", [64, NT], BF16)
    bb8 = sbuf("bb8", [8, NT], F32)
    active8 = sbuf("active8", [8, NT], F32)
    beta8 = bb8
    beta_tok = sbuf("beta_tok", [128, BSEQ, 8], F32)
    nq8 = sbuf("nq8", [8, NT], F32)
    nk8 = nq8
    stat_t = active8[0:1, :]
    stat_v = sbuf("stat_v", [1, NT], F32)
    memst = sbuf("memst", [128, 16], F32)
    sp_tmp = sbuf("sp_tmp", [128, 16], F32)
    apad = sbuf("apad", [128, BSEQ, NSB, SB + 1], F32)
    rho = sbuf("rho", [128, BSEQ, NSB, SB + 1], BF16)
    pcum = sbuf("pcum", [128, BSEQ, NSB, SB + 1], F32)
    rrev = sbuf("rrev", [128, BSEQ, NSB, SB + 1], BF16)
    rhorec = sbuf("rhorec", [128, BSEQ, NSB, SB + 1], BF16)
    CHOFF = [8 * b * b + 8 * b for b in range(NSB + 1)]
    chist = sbuf("chist", [128, BSEQ, CHOFF[NSB]], BF16)
    shA = sbuf("shA", [128, 4096], BF16)
    ab_bf = shA[:, 0:2048].rearrange("p (g t) -> p g t", g=4)
    spike_fm = shA[:, 2048:4096].rearrange("p (g t) -> p g t", g=4)
    kqh = y2[:, 0:4, :].rearrange("p a b -> p (a b)").bitcast(BF16).rearrange(
        "p (g s b j) -> p g s b j", g=4, s=BSEQ, b=NSB)
    shB = sbuf("shB", [128, 4096], BF16)
    sd_scan = shB[:].bitcast(F32).rearrange("p (t x) -> p t x", t=CT)
    khat = shB[:, 0:2048].rearrange("p (g t) -> p g t", g=4)
    qtl = shB[:, 2048:4096].rearrange("p (g t) -> p g t", g=4)
    shC = sbuf("shC", [128, 2048], BF16)
    spike_scan = shC[:].rearrange("p (t x) -> p t x", t=CT)
    kap = shC[:].rearrange("p (g t) -> p g t", g=4)
    S_all = sbuf("S_all", [128, 16, DV], F32)
    S_bf = sbuf("S_bf", [128, DV], BF16)
    Mhat = sbuf("Mhat", [128, 128], BF16)
    Ghat = sbuf("Ghat", [128, 128], BF16)
    Xp = [Mhat] + [sbuf(f"Xp_{i}", [128, 128], BF16) for i in range(1, 7)]
    Yp = [sbuf(f"Yp_{i}", [128, 128], BF16) for i in range(6)]
    Rping = [sbuf(f"Rp_{i}", [128, 128], BF16) for i in range(2)]
    khatT = sbuf("khatT", [128, 64], BF16)
    Ocomb = sbuf("Ocomb", [128, 128], F32)
    On_bf = sbuf("On_bf", [128, 128], BF16)
    rms_col = sbuf("rms_col", [128, 4], F32)
    a_big = sbuf("a_big", [128, 32, NTH], BF16)
    f2_bf = sbuf("f2_bf", [128, 32, NTH], BF16)
    stat_m = sbuf("stat_m", [1, NT], F32)

    banks = [Bank(nc.alloc_psum_tensor(f"bank{i}", [128, 512], F32), f"b{i}")
             for i in range(8)]
    B_MAIN = banks[0:4]
    B_BC = banks[4:6]
    B_BUILD = banks[6]
    B_S1, B_S2 = banks[4], banks[5]
    B_MISC = banks[7]

    s = Sched(nc)

    dbg_out = {}
    dbg_shapes = {
        "xn": [128, 8, NT], "hc": [128, 8, NT], "q": [128, 4, NT],
        "k": [128, 4, NT], "sd": [128, CT, 16], "ab": [128, 4, NT],
        "spike": [128, 4, NT], "alpha": [128, BSEQ, NSB, SB + 1],
        "beta": [8, NT], "vtm": [128, BSEQ, H * DV], "y": [128, 8, NT],
        "y2": [128, 8, NT], "gate": [128, NT], "xbig": [128, 32, NTH],
        "f2": [128, 32, NTH],
    }
    for name in dbg:
        dbg_out[name] = nc.dram_tensor("dbg_" + name, dbg_shapes[name], F32,
                                       kind="ExternalOutput").ap()
    dbg_src = {
        "xn": xn, "hc": hc_bf, "q": qn_bf, "k": kn_bf, "sd": sd_scan,
        "ab": ab_bf, "spike": spike_fm, "alpha": apad, "beta": beta8,
        "vtm": v_tm, "y": y_bf, "y2": y2, "gate": gate_sc, "xbig": a_big,
        "f2": f2_bf,
    }
    dbg_stage = [None, None]
    def emit_dbg(name, waits):
        if name not in dbg_out:
            return
        src = dbg_src[name]
        srcap = src if isinstance(src, bass.AP) else src[:]
        if srcap.dtype != F32:
            if dbg_stage[0] is None:
                dbg_stage[0] = f2_bf[:].rearrange("p a b -> p (a b)").bitcast(F32)
            n = int(np.prod(dbg_shapes[name][1:]))
            stg = dbg_stage[0][:, 0:n]
            w = list(waits) + ([dbg_stage[1]] if dbg_stage[1] else [])
            t = s.op("vector", lambda e, stg=stg, srcap=srcap: e.tensor_copy(
                stg, srcap.rearrange(
                    " ".join(["p"] + [chr(97 + i) for i in range(srcap.ndim - 1)])
                    + " -> p (" + " ".join(chr(97 + i) for i in range(srcap.ndim - 1))
                    + ")") if srcap.ndim > 2 else srcap),
                waits=w, inc="V", tag=f"dbgcast_{name}", chain=True)
            dbg_stage[1] = s.op("sync", lambda e, stg=stg, name=name, _w=None: e.dma_start(
                out=dbg_out[name].rearrange(
                    " ".join(["p"] + [chr(97 + i) for i in range(len(dbg_shapes[name]) - 1)])
                    + " -> p (" + " ".join(chr(97 + i) for i in range(len(dbg_shapes[name]) - 1))
                    + ")") if len(dbg_shapes[name]) > 2 else dbg_out[name],
                in_=stg), waits=[t] + ([dbg_stage[1]] if dbg_stage[1] else []),
                inc="DD", n=16, tag=f"dbg_{name}")
        else:
            dbg_stage[1] = s.op("sync", lambda e, srcap=srcap, name=name: e.dma_start(
                out=dbg_out[name], in_=srcap),
                waits=list(waits) + ([dbg_stage[1]] if dbg_stage[1] else []),
                inc="DD", n=16, tag=f"dbg_{name}")

    def bank_mm(bank, fn, first, last, waits=(), force_inc=False, tag=""):
        w = list(waits)
        if first and bank.free is not None:
            w.append(bank.free)
        tok = s.op("tensor", fn, waits=w,
                   inc="T" if (last or force_inc) else None, tag=tag)
        if last:
            bank.ready = tok

    def consume(engine, bank, fn, last=True, waits=(), tag="", chain=False):
        w = list(waits)
        if bank is not None and bank.ready is not None:
            w.append(bank.ready)
        sem = {"vector": "V", "scalar": "S", "gpsimd": "G"}[engine]
        tok = s.op(engine, fn, waits=w, inc=sem, tag=tag, chain=chain)
        if bank is not None and last:
            bank.free = tok
        return tok

    def vop(fn, waits=(), tag="", chain=False):
        return s.op("vector", fn, waits=waits, inc="V", tag=tag, chain=chain)

    def sop(fn, waits=(), tag="", chain=False):
        return s.op("scalar", fn, waits=waits, inc="S", tag=tag, chain=chain)

    def gop(fn, waits=(), tag="", chain=False):
        return s.op("gpsimd", fn, waits=waits, inc="G", tag=tag, chain=chain)

    cast_done = {}

    def stripe_load(wname, og0, kq, tag=""):
        si, nk, ow, rows = stripe_index[(wname, og0, kq)]
        i = pool_i[0]
        pool_i[0] = (i + 1) % NPOOL
        buf = pool[i]
        w = []
        if pool_state[i] is not None:
            w.append(pool_state[i])
        for t_ in cast_done.get(wname, ()):
            w.append(t_)
        tok = s.op("sync",
                   lambda e, buf=buf, si=si, rows=rows, nk=nk, ow=ow:
                   e.dma_start(out=buf[0:rows, 0:nk * ow],
                               in_=wstripes[si, 0:rows, 0:nk * ow]),
                   waits=w, inc=f"DW{i}", n=16, tag=f"ld_{wname}_{og0}_{kq}" + tag)
        return buf, tok, i

    def stripe_used(i):
        pool_state[i] = ("T", s.counts["T"])

    def proj_gen(wname, mov_view, out_cb, mov_toks=(), group=2, nt=NT, tag=""):
        K, O = WSHAPES[wname]
        kchunks = ceil_div(K, 128)
        otiles = ceil_div(O, 128)
        for ogidx, og0 in enumerate(range(0, otiles, group)):
            ots = list(range(og0, min(og0 + group, otiles)))
            ow = sum(min(128, O - ot * 128) for ot in ots)
            kstr = max(1, 2048 // ow) if K % 128 == 0 else 1
            boff = 2 * (ogidx % 2) if group <= 2 else 0
            for kq in range(0, kchunks, kstr):
                nk = min(kstr, kchunks - kq)
                buf, ltok, bi = stripe_load(wname, og0, kq, tag=tag)
                for kk in range(nk):
                    ki = kq + kk
                    rows = min(128, K - ki * 128)
                    for j, ot in enumerate(ots):
                        bank = B_MAIN[boff + j]
                        osz = min(128, O - ot * 128)
                        mv = mov_view(ki)
                        bank_mm(bank,
                                lambda e, buf=buf, j=j, mv=mv, bank=bank,
                                       kk=kk, ow=ow, osz=osz, rows=rows, nt=nt,
                                       ki=ki:
                                e.matmul(bank.ps[0:osz, 0:nt],
                                         buf[0:rows, kk * ow + j * 128:
                                             kk * ow + j * 128 + osz], mv,
                                         start=(ki == 0),
                                         stop=(ki == kchunks - 1)),
                                first=(ki == 0), last=(ki == kchunks - 1),
                                force_inc=(j == len(ots) - 1 and kk == nk - 1),
                                waits=[ltok] + list(mov_toks),
                                tag=f"{tag}{wname}_mm{ot}_{ki}")
                stripe_used(bi)
                yield
            for j, ot in enumerate(ots):
                out_cb(ot, B_MAIN[boff + j])

    def proj(*a, **kw):
        for _ in proj_gen(*a, **kw):
            pass

    def pump(gen, n):
        for _ in range(n):
            if next(gen, "DONE") == "DONE":
                return False
        return True

    for k in SMALL_SHAPES:
        s.op("sync", lambda e, k=k: e.dma_start(out=P[k][:], in_=sm_in[k]),
             inc="DP", n=16, tag=f"p_{k}")
    allp = ("DP", s.counts["DP"])
    vop(lambda e: e.tensor_copy(ident_bf[:], P["ident"][:]), waits=[allp])
    vop(lambda e: e.tensor_copy(seg8_bf[:], P["seg8"][:]))
    vop(lambda e: e.tensor_copy(bseg8_bf[:], P["bseg8"][:]))
    cst = sbuf("cst", [128, 3], F32)
    vop(lambda e: e.memset(cst[:, 0:1], LNEPS))
    vop(lambda e: e.memset(cst[:, 1:2], EPS))
    vop(lambda e: e.memset(cst[:, 2:3], 1.0 / DV))
    vop(lambda e: e.memset(memst[:], 0.0))
    vop(lambda e: e.memset(h1h[:], 0.0))
    vop(lambda e: e.memset(S_all[:], 0.0))
    vop(lambda e: e.memset(apad[:], 1.0))
    vop(lambda e: e.memset(B_BUILD.ps[:, 0:256], 0.0))
    vop(lambda e: e.memset(rho[:], 1.0))
    vop(lambda e: e.memset(rrev[:], 1.0))
    vop(lambda e: e.memset(pcum[:], 1.0))
    init_tok = vop(lambda e: e.memset(chist[:], 0.0))
    consts_tok = init_tok

    cast_order = ["wq", "wk", "wv", "wsp", "wau", "wad", "wbeta", "wu1",
                  "wu2", "w1t", "w2t", "wff1", "wff3", "h1t", "h2t", "wff2",
                  "w3t", "wout"]
    _by_w = {}
    for ent in stripe_list:
        _by_w.setdefault(ent[0], []).append(ent)
    ci = 0
    cb_state = [init_tok, init_tok]
    for wname in cast_order:
        K, O = WSHAPES[wname]
        tout = None
        for (_, si, kq, nk, og0, ow, rows) in _by_w[wname]:
            if wname == "wv":
                o0 = 0
            elif wname == "wout":
                o0 = og0 * 128
            else:
                o0 = og0 * 128
            par = ci % 2
            if nk == 1:
                tout = s.op("gpsimd",
                            lambda e, si=si, wname=wname, kq=kq, o0=o0, ow=ow,
                                   rows=rows:
                            e.dma_start(out=wstripes[si, 0:rows, 0:ow],
                                        in_=w_in[wname][kq * 128:kq * 128 + rows,
                                                        o0:o0 + ow]),
                            waits=[cb_state[par]], inc=f"DCG{par}", n=16,
                            tag=f"cast_{wname}_{si}")
            else:
                tout = s.op("gpsimd",
                            lambda e, si=si, wname=wname, kq=kq, o0=o0, ow=ow,
                                   nk=nk:
                            e.dma_start(
                                out=wstripes[si, :, 0:nk * ow].rearrange(
                                    "p (n c) -> p n c", n=nk),
                                in_=w_in[wname][kq * 128:(kq + nk) * 128,
                                                o0:o0 + ow].rearrange(
                                    "(n p) c -> p n c", p=128)),
                            waits=[cb_state[par]], inc=f"DCG{par}", n=16,
                            tag=f"cast_{wname}_{si}")
            cb_state[par] = tout
            ci += 1
        cast_done[wname] = (cb_state[0], cb_state[1])
    cast_all = tout

    halo_tok = init_tok
    prev_scan_tok = init_tok
    out_final_toks = []
    for ti in range(ntiles):
        t0 = ti * CT

        xtoks = []
        for c in range(8):
            xtoks.append(s.op("sync",
                lambda e, c=c, ti=ti: e.dma_start(
                    out=OUT[:, c, :].rearrange("p (s t) -> p s t", s=BSEQ),
                    in_=x_fm[c, :, ti]),
                waits=([init_tok] if ti == 0 else [out_final_toks[-1]]),
                inc="DX", n=16, tag=f"x_{ti}_{c}"))
        xtok_all = xtoks[-1]

        xb_toks = []
        for c in range(8):
            xb_toks.append(sop(lambda e, c=c: e.copy(xn_bf[:, c, :], OUT[:, c, :]),
                               waits=[xtok_all], tag="xbf"))
        bs = B_BC[0]
        for c in range(8):
            bank_mm(bs, lambda e, c=c, bs=bs: e.matmul(
                bs.ps[0:1, 0:NT], P["onescol"], xn_bf[:, c, :],
                start=(c == 0), stop=(c == 7)),
                first=(c == 0), last=(c == 7),
                waits=[xb_toks[c], consts_tok], tag="lnsum")
        bs2 = B_BC[1]
        sq_toks = []
        for c in range(8):
            w = [xtok_all]
            if c >= 2:
                w.append(sq_toks[c - 2][1])
            tsq = sop(lambda e, c=c: e.activation(
                out=scr2[:, c % 2, :], in_=OUT[:, c, :], func=AF.Square),
                waits=w, tag="xsq")
            bank_mm(bs2, lambda e, c=c, bs2=bs2: e.matmul(
                bs2.ps[0:1, 0:NT], P["onescol"], scr2[:, c % 2, :],
                start=(c == 0), stop=(c == 7)),
                first=(c == 0), last=(c == 7), force_inc=True,
                waits=[tsq], tag="lnsum2")
            sq_toks.append((tsq, ("T", s.counts["T"])))
        consume("vector", bs, lambda e, bs=bs: e.tensor_scalar(
            out=stat_m[:], in0=bs.ps[0:1, 0:NT], scalar1=1.0 / D, scalar2=None,
            op0=ALU.mult), last=True, tag="ln_m")
        consume("vector", bs2, lambda e, bs2=bs2: e.tensor_scalar(
            out=stat_v[:], in0=bs2.ps[0:1, 0:NT], scalar1=1.0 / D, scalar2=None,
            op0=ALU.mult), last=True, tag="ln_v")
        vop(lambda e: e.tensor_mul(stat_t[:], stat_m[:], stat_m[:]), tag="ln_m2", chain=True)
        tv = vop(lambda e: e.tensor_sub(stat_v[:], stat_v[:], stat_t[:]), tag="ln_var", chain=True)
        tv = sop(lambda e: e.activation(out=stat_v[:], in_=stat_v[:], func=AF.Sqrt,
                                        bias=cst[0:1, 0:1]), waits=[tv], tag="ln_sd")
        tv = vop(lambda e: e.reciprocal(stat_v[:], stat_v[:]), waits=[tv], tag="ln_rs", chain=True)
        tm = ("V", s.counts["V"])
        bm, br = B_BC[0], B_BC[1]
        bank_mm(bm, lambda e, bm=bm: e.matmul(
            bm.ps[0:128, 0:NT], P["onesrow"][0:1, :], stat_m[:],
            start=True, stop=True), first=True, last=True, waits=[tm], tag="bc_m")
        bank_mm(br, lambda e, br=br: e.matmul(
            br.ps[0:128, 0:NT], P["onesrow"][0:1, :], stat_v[:],
            start=True, stop=True), first=True, last=True, waits=[tv], tag="bc_rs")
        xn_toks = []
        for c in range(8):
            consume("vector", bm, lambda e, c=c, bm=bm: e.tensor_sub(
                xn[:, c, :], OUT[:, c, :], bm.ps[0:128, 0:NT]),
                last=(c == 7), tag="xn_sub")
            consume("vector", br, lambda e, c=c, br=br: e.tensor_mul(
                xn[:, c, :], xn[:, c, :], br.ps[0:128, 0:NT]),
                last=(c == 7), tag="xn_mul", chain=True)
            xn_toks.append(("V", s.counts["V"]))
        xnbf_toks = []
        for c in range(8):
            xnbf_toks.append(sop(lambda e, c=c: e.copy(xn_bf[:, c, :], xn[:, c, :]),
                                 waits=[xn_toks[c]], tag="xnbf"))
        xnbf_all = xnbf_toks[-1]
        emit_dbg("xn", [xn_toks[-1]])

        bs = B_BC[0]
        sq_toks = []
        for c in range(8):
            w = [xn_toks[c]]
            if c >= 2:
                w.append(sq_toks[c - 2][1])
            tsq = sop(lambda e, c=c: e.activation(
                out=scr2[:, c % 2, :], in_=xn[:, c, :], func=AF.Square),
                waits=w, tag="h1sq")
            bank_mm(bs, lambda e, c=c, bs=bs: e.matmul(
                bs.ps[0:1, 0:NT], P["onescol"], scr2[:, c % 2, :],
                start=(c == 0), stop=(c == 7)),
                first=(c == 0), last=(c == 7), force_inc=True,
                waits=[tsq], tag="h1sum")
            sq_toks.append((tsq, ("T", s.counts["T"])))
        consume("vector", bs, lambda e, bs=bs: e.tensor_scalar(
            out=stat_m[:], in0=bs.ps[0:1, 0:NT], scalar1=1.0 / D, scalar2=None,
            op0=ALU.mult), tag="h1_ms")
        tv = sop(lambda e: e.activation(out=stat_m[:], in_=stat_m[:], func=AF.Sqrt,
                                        bias=cst[0:1, 1:2]), waits=[("V", s.counts["V"])],
                 tag="h1_sd")
        tv = vop(lambda e: e.reciprocal(stat_m[:], stat_m[:]), waits=[tv], tag="h1_rs", chain=True)
        br = B_BC[1]
        bank_mm(br, lambda e, br=br: e.matmul(
            br.ps[0:128, 0:NT], P["onesrow"][0:1, :], stat_m[:],
            start=True, stop=True), first=True, last=True, waits=[tv], tag="bc_h1")
        h1_toks = []
        for c in range(8):
            consume("vector", br, lambda e, c=c, br=br: e.scalar_tensor_tensor(
                out=h1h[:, c, :, 3:3 + CT],
                in0=xn[:, c, :].rearrange("p (s t) -> p s t", s=BSEQ),
                scalar=P["wni"][:, c:c + 1],
                in1=br.ps[0:128, 0:NT].rearrange("p (s t) -> p s t", s=BSEQ),
                op0=ALU.mult, op1=ALU.mult),
                last=(c == 7), waits=[halo_tok, xn_toks[c]], tag="h1")
            h1_toks.append(("V", s.counts["V"]))
        hc_toks = []
        for c in range(8):
            w = [h1_toks[c]]
            if c > 0:
                w.append(hc_toks[c - 1])
            vop(lambda e, c=c: e.tensor_scalar(
                out=convacc[:], in0=h1h[:, c, :, 0:CT],
                scalar1=P["convw"][:, 4 * c:4 * c + 1],
                scalar2=P["convb"][:, c:c + 1], op0=ALU.mult, op1=ALU.add),
                waits=w, tag="conv0", chain=True)
            for j in range(1, 4):
                vop(lambda e, c=c, j=j: e.scalar_tensor_tensor(
                    out=convacc[:], in0=h1h[:, c, :, j:j + CT],
                    scalar=P["convw"][:, 4 * c + j:4 * c + j + 1],
                    in1=convacc[:], op0=ALU.mult, op1=ALU.add), tag=f"conv{j}", chain=True)
            hc_toks.append(sop(lambda e, c=c: e.activation(
                out=hc_bf[:, c, :], in_=convacc[:].rearrange("p s t -> p (s t)"),
                func=AF.Silu), waits=[("V", s.counts["V"])], tag="hc"))
        hc_all = hc_toks[-1]
        halo_tok = gop(lambda e: e.tensor_copy(
            h1h[:, :, :, 0:3], h1h[:, :, :, CT:CT + 3]),
            waits=[hc_all] + h1_toks, tag="halo", chain=True)
        emit_dbg("hc", [hc_all])

        hcv = lambda ki: hc_bf[:, ki, :]
        def sd_cb(ot, bank):
            consume("vector", bank, lambda e, ot=ot, bank=bank: e.tensor_copy(
                sd_scan[:, :, 4 * ot:4 * ot + 4].transpose([0, 2, 1]),
                bank.ps[0:128, 0:NT].rearrange("p (s t) -> p s t", s=BSEQ)),
                waits=[prev_scan_tok], tag="sd_ev")
        proj("wsp", hcv, sd_cb, mov_toks=[hc_all], tag="sd_")
        sd_tok = ("V", s.counts["V"])
        emit_dbg("sd", [sd_tok])
        def q_cb(ot, bank):
            consume("vector", bank, lambda e, ot=ot, bank=bank: e.tensor_copy(
                qf[:, ot, :], bank.ps[0:128, 0:NT]), tag="q_ev")
        proj("wq", hcv, q_cb, mov_toks=[hc_all], tag="q_")
        q_tok = ("V", s.counts["V"])
        def k_cb(ot, bank):
            consume("vector", bank, lambda e, ot=ot, bank=bank: e.tensor_copy(
                kf[:, ot, :], bank.ps[0:128, 0:NT]), tag="k_ev")
        proj("wk", hcv, k_cb, mov_toks=[hc_all], tag="k_")
        k_tok = ("V", s.counts["V"])
        vtoks = []
        for rnd in range(2):
            tcs = (2 * rnd, 2 * rnd + 1)
            for kq in range(0, 8, 2):
                buf, ltok, bi = stripe_load("wv", 0, kq, tag="v")
                for kk in range(2):
                    ki = kq + kk
                    for j, tc in enumerate(tcs):
                        for oh in range(2):
                            bank = B_MAIN[2 * j + oh]
                            bank_mm(bank,
                                    lambda e, buf=buf, ki=ki, kk=kk, tc=tc,
                                           oh=oh, bank=bank:
                                    e.matmul(bank.ps[0:128, 0:512],
                                             hc_bf[:, ki, tc * 128:(tc + 1) * 128],
                                             buf[:, kk * 1024 + oh * 512:
                                                 kk * 1024 + (oh + 1) * 512],
                                             start=(ki == 0), stop=(ki == 7)),
                                    first=(ki == 0), last=(ki == 7),
                                    force_inc=(j == 1 and oh == 1 and kk == 1),
                                    waits=[ltok, hc_all], tag=f"v_mm{tc}_{oh}_{ki}")
                stripe_used(bi)
            for j, tc in enumerate(tcs):
                for oh in range(2):
                    bank = B_MAIN[2 * j + oh]
                    vtoks.append(consume("scalar", bank,
                        lambda e, tc=tc, oh=oh, bank=bank: e.copy(
                            v_tm[:, tc, oh * 512:(oh + 1) * 512],
                            bank.ps[0:128, 0:512]), tag="v_ev"))
        v_tok = vtoks[-1]
        emit_dbg("vtm", [v_tok])
        def au_cb(ot, bank):
            consume("scalar", bank, lambda e, bank=bank: e.activation(
                out=au_bf[:, :], in_=bank.ps[0:64, 0:NT], func=AF.Silu,
                bias=P["bau"][:, 0:1]), tag="au_ev")
        proj("wau", hcv, au_cb, mov_toks=[hc_all], tag="au_")
        au_tok = ("S", s.counts["S"])
        def ab_cb(ot, bank):
            consume("vector", bank, lambda e, ot=ot, bank=bank: e.tensor_scalar(
                out=ab_bf[:, ot, :], in0=bank.ps[0:128, 0:NT],
                scalar1=P["bad"][:, ot:ot + 1], scalar2=None, op0=ALU.add),
                tag="ab_ev")
        proj("wad", lambda ki: au_bf[:, :], ab_cb, mov_toks=[au_tok], tag="ab_")
        ab_tok = ("V", s.counts["V"])
        emit_dbg("ab", [ab_tok])
        def bb_cb(ot, bank):
            consume("vector", bank, lambda e, bank=bank: e.tensor_scalar(
                out=bb8[:], in0=bank.ps[0:8, 0:NT], scalar1=P["bbeta"][:, 0:1],
                scalar2=None, op0=ALU.add), tag="bb_ev")
        proj("wbeta", hcv, bb_cb, mov_toks=[hc_all], tag="bb_")
        bb_tok = ("V", s.counts["V"])
        def g1_cb(ot, bank):
            consume("scalar", bank, lambda e, bank=bank: e.activation(
                out=g1_bf[:, :], in_=bank.ps[0:64, 0:NT], func=AF.Silu,
                bias=P["bu1"][:, 0:1]), tag="g1_ev")
        proj("wu1", lambda ki: xn_bf[:, ki, :], g1_cb, mov_toks=[xnbf_all], tag="g1_")
        g1_tok = ("S", s.counts["S"])

        sqguard = []
        for (src, dstn, n8, tg) in ((qf, qn_bf, nq8, "q"), (kf, kn_bf, nk8, "k")):
            bs = B_BC[0]
            for g in range(4):
                w = [q_tok if tg == "q" else k_tok]
                if len(sqguard) >= 2:
                    w.append(sqguard[-2])
                tsq = sop(lambda e, g=g, src=src: e.activation(
                    out=scr2[:, g % 2, :], in_=src[:, g, :], func=AF.Square),
                    waits=w, tag=tg + "sq")
                bank_mm(bs, lambda e, g=g, bs=bs: e.matmul(
                    bs.ps[0:8, 0:NT], seg8_bf[:, 8 * g:8 * g + 8],
                    scr2[:, g % 2, :], start=(g == 0), stop=(g == 3)),
                    first=(g == 0), last=(g == 3), force_inc=True,
                    waits=[tsq], tag=tg + "ss")
                sqguard.append(("T", s.counts["T"]))
            consume("scalar", bs, lambda e, bs=bs, n8=n8: e.activation(
                out=n8[:], in_=bs.ps[0:8, 0:NT], func=AF.Sqrt), tag=tg + "_sd")
            tv = vop(lambda e, n8=n8: e.tensor_scalar(
                out=n8[:], in0=n8[:], scalar1=EPS, scalar2=None, op0=ALU.add),
                waits=[("S", s.counts["S"])], tag=tg + "_eps")
            tv = vop(lambda e, n8=n8: e.reciprocal(n8[:], n8[:]), waits=[tv],
                     tag=tg + "_rec", chain=True)
            for g in range(4):
                br = B_BC[1]
                bank_mm(br, lambda e, g=g, br=br, n8=n8: e.matmul(
                    br.ps[0:128, 0:NT], P["headmap"][0:8, g * 128:(g + 1) * 128],
                    n8[:], start=True, stop=True),
                    first=True, last=True, waits=[tv], tag=tg + "_bc")
                consume("vector", br, lambda e, g=g, br=br, src=src, dstn=dstn:
                        e.tensor_mul(dstn[:, g, :], src[:, g, :],
                                     br.ps[0:128, 0:NT]),
                        waits=[q_tok if tg == "q" else k_tok], tag=tg + "_nrm")
        qn_tok = kn_tok = ("V", s.counts["V"])
        emit_dbg("q", [qn_tok])
        emit_dbg("k", [kn_tok])


        tern_state = []
        def ternary_gen(half):
            tsl = slice(half * NTH, (half + 1) * NTH)
            def w1_cb(ot, bank):
                consume("scalar", bank, lambda e, ot=ot, bank=bank: e.activation(
                    out=a_big[:, ot, :], in_=bank.ps[0:128, 0:NTH], func=AF.Silu,
                    scale=P["s1"][:, ot:ot + 1]),
                    waits=([dbg_stage[1]] if dbg_stage[1] else []), tag="t1_ev")
            yield from proj_gen("w1t", lambda ki: xn_bf[:, ki, tsl], w1_cb,
                                mov_toks=[xnbf_all], nt=NTH, tag=f"t1h{half}_")
            t1_tok = ("S", s.counts["S"])
            def w2_cb(ot, bank):
                consume("vector", bank, lambda e, ot=ot, bank=bank:
                        e.scalar_tensor_tensor(
                            out=a_big[:, ot, :], in0=bank.ps[0:128, 0:NTH],
                            scalar=P["s2"][:, ot:ot + 1], in1=a_big[:, ot, :],
                            op0=ALU.mult, op1=ALU.mult),
                        waits=[t1_tok], tag="t2_ev")
            yield from proj_gen("w2t", lambda ki: xn_bf[:, ki, tsl], w2_cb,
                                mov_toks=[xnbf_all], nt=NTH, tag=f"t2h{half}_")
            f1_tok = ("V", s.counts["V"])
            if half == 0:
                emit_dbg("xbig", [f1_tok])
            def h1_cb(ot, bank):
                consume("scalar", bank, lambda e, ot=ot, bank=bank: e.activation(
                    out=f2_bf[:, ot, :], in_=bank.ps[0:128, 0:NTH], func=AF.Gelu,
                    scale=P["hs1"][:, ot:ot + 1]),
                    waits=([dbg_stage[1]] if dbg_stage[1] else []), tag="f2_ev")
            yield from proj_gen("h1t", lambda ki: a_big[:, ki, :], h1_cb,
                                mov_toks=[f1_tok], nt=NTH, tag=f"f2h{half}_")
            f2_tok = ("S", s.counts["S"])
            if half == 0:
                emit_dbg("f2", [f2_tok])
            def h2_cb(ot, bank):
                consume("scalar", bank, lambda e, ot=ot, bank=bank: e.activation(
                    out=a_big[:, ot, :], in_=bank.ps[0:128, 0:NTH], func=AF.Gelu,
                    scale=P["hs2"][:, ot:ot + 1]), tag="f3_ev")
            yield from proj_gen("h2t", lambda ki: f2_bf[:, ki, :], h2_cb,
                                mov_toks=[f2_tok], nt=NTH, tag=f"f3h{half}_")
            f3_tok = ("S", s.counts["S"])
            def w3_cb(ot, bank):
                consume("vector", bank, lambda e, ot=ot, bank=bank:
                        e.scalar_tensor_tensor(
                            out=OUT[:, ot, tsl], in0=bank.ps[0:128, 0:NTH],
                            scalar=P["s3"][:, ot:ot + 1], in1=OUT[:, ot, tsl],
                            op0=ALU.mult, op1=ALU.add),
                        waits=[f3_tok, xtok_all], tag="t3_ev", chain=True)
            yield from proj_gen("w3t", lambda ki: a_big[:, ki, :], w3_cb,
                                mov_toks=[f3_tok], nt=NTH, tag=f"t3h{half}_")
            tern_state.append(("V", s.counts["V"]))

        def _tern_chain():
            yield from ternary_gen(0)
            yield from ternary_gen(1)
        tgen = _tern_chain()
        pump(tgen, 32)

        for t in range(CT):
            w = [sd_tok] if t == 0 else []
            vop(lambda e, t=t: e.scalar_tensor_tensor(
                out=memst[:], in0=memst[:], scalar=0.9, in1=sd_scan[:, t, :],
                op0=ALU.mult, op1=ALU.add), waits=w, tag=f"ms{t}", chain=True)
            vop(lambda e, t=t: e.tensor_single_scalar(
                out=sd_scan[:, t, :], in_=memst[:], scalar=0.5, op=ALU.is_gt),
                tag=f"sp{t}", chain=True)
            vop(lambda e, t=t: e.scalar_tensor_tensor(
                out=memst[:], in0=sd_scan[:, t, :], scalar=-0.5, in1=memst[:],
                op0=ALU.mult, op1=ALU.add), tag=f"mm{t}", chain=True)
            if t % 8 == 7:
                pump(tgen, 3)
        scan_v_tok = ("V", s.counts["V"])
        spf_toks = []
        for g in range(4):
            for sq in range(BSEQ):
                spf_toks.append(gop(lambda e, g=g, sq=sq: e.tensor_copy(
                    spike_fm[:, g, sq * CT:(sq + 1) * CT],
                    sd_scan[:, :, 4 * g + sq]),
                    waits=[scan_v_tok], tag="spfm", chain=True))
        spf_all = spf_toks[-1]
        prev_scan_tok = spf_all
        emit_dbg("spike", [spf_all])
        pump(tgen, 110)

        ap_valid = apad[:, :, :, 0:SB]
        bsp, bbs = B_BC[0], B_BC[1]
        for g in range(4):
            bank_mm(bsp, lambda e, g=g, bsp=bsp: e.matmul(
                bsp.ps[0:8, 0:NT], seg8_bf[:, 8 * g:8 * g + 8], spike_fm[:, g, :],
                start=(g == 0), stop=(g == 3)),
                first=(g == 0), last=(g == 3), waits=[spf_all], tag="segsum")
            bank_mm(bbs, lambda e, g=g, bbs=bbs: e.matmul(
                bbs.ps[0:8, 0:NT], bseg8_bf[:, 8 * g:8 * g + 8], spike_fm[:, g, :],
                start=(g == 0), stop=(g == 3)),
                first=(g == 0), last=(g == 3), waits=[spf_all], tag="bsum")
        consume("vector", bsp, lambda e, bsp=bsp: e.tensor_single_scalar(
            out=active8[:], in_=bsp.ps[0:8, 0:NT], scalar=0.0, op=ALU.is_gt),
            tag="active")
        act_tok = ("V", s.counts["V"])
        consume("vector", bbs, lambda e, bbs=bbs: e.tensor_add(
            beta8[:], bb8[:], bbs.ps[0:8, 0:NT]), waits=[bb_tok], tag="blog")
        tb = sop(lambda e: e.activation(out=beta8[:], in_=beta8[:], func=AF.Sigmoid),
                 waits=[("V", s.counts["V"])], tag="bsig")
        tb = vop(lambda e: e.tensor_mul(beta8[:], beta8[:], active8[:]),
                 waits=[tb, act_tok], tag="bact", chain=True)
        emit_dbg("beta", [tb])
        bt = B_MISC
        for sq in range(BSEQ):
            bank_mm(bt, lambda e, sq=sq, bt=bt: e.transpose(
                bt.ps[0:128, 8 * sq:8 * sq + 8],
                beta8[0:8, sq * CT:(sq + 1) * CT], P["ident"][0:8, 0:8]),
                first=(sq == 0), last=(sq == 3), waits=[tb], tag="btr")
        consume("vector", bt, lambda e, bt=bt: e.tensor_copy(
            beta_tok[:].rearrange("p s h -> p (s h)"), bt.ps[0:128, 0:32]),
            tag="btr_ev")
        btok_tok = ("V", s.counts["V"])



        g_dec_toks = []
        for g in range(4):
            t1 = vop(lambda e, g=g: e.scalar_tensor_tensor(
                out=ap_valid, in0=spike_fm[:, g, :].rearrange(
                    "p (s b j) -> p s b j", s=BSEQ, b=NSB),
                scalar=P["asp"][:, g:g + 1],
                in1=ab_bf[:, g, :].rearrange("p (s b j) -> p s b j", s=BSEQ, b=NSB),
                op0=ALU.mult, op1=ALU.add),
                waits=[spf_all, ab_tok, init_tok], tag="alog")
            t1 = sop(lambda e: e.activation(
                out=ap_valid, in_=ap_valid, func=AF.Sigmoid),
                waits=[t1], tag="asig")
            br = B_BC[1]
            bank_mm(br, lambda e, g=g, br=br: e.matmul(
                br.ps[0:128, 0:NT], P["headmap"][0:8, g * 128:(g + 1) * 128],
                active8[:], start=True, stop=True),
                first=True, last=True, waits=[act_tok], tag="abc")
            vop(lambda e: e.tensor_scalar(
                out=ap_valid, in0=ap_valid, scalar1=-1.0, scalar2=None,
                op0=ALU.add), waits=[t1], tag="am1")
            consume("vector", br, lambda e, br=br: e.tensor_mul(
                ap_valid, ap_valid,
                br.ps[0:128, 0:NT].rearrange("p (s b j) -> p s b j", s=BSEQ, b=NSB)),
                tag="am2", chain=True)
            amk = vop(lambda e: e.tensor_scalar(
                out=ap_valid, in0=ap_valid, scalar1=1.0, scalar2=None,
                op0=ALU.add), tag="am3", chain=True)
            if g == 3:
                emit_dbg("alpha", [amk])
            for sq in range(BSEQ):
                for b in range(NSB):
                    w = [amk] if (sq == 0 and b == 0) else []
                    def _rho(e, sq=sq, b=b):
                        with nc.allow_low_precision(reason="bf16 cumprod store; scan state is fp32"):
                            return e.tensor_tensor_scan(
                                out=rho[:, sq, b, 0:SB], data0=apad[:, sq, b, 0:SB],
                                data1=apad[:, sq, b, 0:SB], initial=1.0,
                                op0=ALU.mult, op1=ALU.bypass)
                    vop(_rho, waits=w, tag="rho")
            FL = NSB * (SB + 1)
            for sq in range(BSEQ):
                vop(lambda e, sq=sq: e.tensor_tensor_scan(
                    out=pcum[:, sq].rearrange("p a b -> p (a b)"),
                    data0=apad[:, sq].rearrange("p a b -> p (a b)"),
                    data1=apad[:, sq].rearrange("p a b -> p (a b)"),
                    initial=1.0, op0=ALU.mult, op1=ALU.bypass), tag="pcum")
            for sq in range(BSEQ):
                def _rrev(e, sq=sq, FL=FL):
                    with nc.allow_low_precision(reason="bf16 cumprod store; scan state is fp32"):
                        return e.tensor_tensor_scan(
                            out=rrev[:, sq].rearrange("p a b -> p (a b)")[:, FL - 2::-1],
                            data0=apad[:, sq].rearrange("p a b -> p (a b)")[:, FL - 1:0:-1],
                            data1=apad[:, sq].rearrange("p a b -> p (a b)")[:, FL - 1:0:-1],
                            initial=1.0, op0=ALU.mult, op1=ALU.bypass)
                vop(_rrev, tag="rrev")
            def _rrc(e):
                with nc.allow_low_precision(reason="bf16 reciprocal store"):
                    return e.reciprocal(
                        rhorec[:].rearrange("p s b j -> p (s b j)"),
                        rho[:].rearrange("p s b j -> p (s b j)"))
            vop(_rrc, tag="rhorec", chain=True)
            scans_tok = ("V", s.counts["V"])
            knv = lambda g: kn_bf[:, g, :].rearrange("p (s b j) -> p s b j",
                                                     s=BSEQ, b=NSB)
            qnv = lambda g: qn_bf[:, g, :].rearrange("p (s b j) -> p s b j",
                                                     s=BSEQ, b=NSB)
            vop(lambda e, g=g: e.tensor_mul(
                kqh[:, g, :, :, 0:SB], knv(g), rho[:, :, :, 0:SB]),
                waits=[kn_tok, scans_tok], tag="krho", chain=True)
            vop(lambda e, g=g: e.tensor_mul(
                kqh[:, g, :, :, SB:2 * SB], qnv(g), rho[:, :, :, 0:SB]),
                waits=[qn_tok], tag="qrho")
            vop(lambda e, g=g: e.tensor_mul(
                kap[:, g, :].rearrange("p (s b j) -> p s b j", s=BSEQ, b=NSB),
                knv(g), pcum[:, :, :, 0:SB]), tag="kap")
            vop(lambda e, g=g: e.tensor_mul(
                qtl[:, g, :].rearrange("p (s b j) -> p s b j", s=BSEQ, b=NSB),
                qnv(g), pcum[:, :, :, 0:SB]), tag="qtl")
            var_tok = vop(lambda e, g=g: e.tensor_mul(
                khat[:, g, :].rearrange("p (s b j) -> p s b j", s=BSEQ, b=NSB),
                knv(g), rrev[:, :, :, 0:SB]), tag="khat")
            gop(lambda e, g=g: e.tensor_mul(
                chist[:, :, 0:SB], knv(g)[:, :, 0, :], rhorec[:, :, 0, 0:SB]),
                waits=[var_tok], tag="chi0", chain=True)
            for b in range(1, NSB):
                ob, opb = CHOFF[b], CHOFF[b - 1]
                if b > 1:
                    gop(lambda e, g=g, b=b, ob=ob, opb=opb: e.tensor_mul(
                        chist[:, :, ob:ob + SB * (b - 1)],
                        chist[:, :, opb:opb + SB * (b - 1)],
                        rho[:, :, b - 1, SB - 1:SB].broadcast_to(
                            [128, BSEQ, SB * (b - 1)])), tag=f"chiA{b}", chain=True)
                gop(lambda e, g=g, b=b, ob=ob: e.tensor_mul(
                    chist[:, :, ob + SB * (b - 1):ob + SB * b],
                    knv(g)[:, :, b - 1, :], rhorec[:, :, b - 1, 0:SB]),
                    tag=f"chiB{b}", chain=True)
                gop(lambda e, g=g, b=b, ob=ob: e.tensor_mul(
                    chist[:, :, ob + SB * (b - 1):ob + SB * b],
                    chist[:, :, ob + SB * (b - 1):ob + SB * b],
                    rho[:, :, b - 1, SB - 1:SB].broadcast_to([128, BSEQ, SB])),
                    tag=f"chiB2{b}", chain=True)
                gop(lambda e, g=g, b=b, ob=ob: e.tensor_mul(
                    chist[:, :, ob + SB * b:ob + SB * (b + 1)],
                    knv(g)[:, :, b, :], rhorec[:, :, b, 0:SB]), tag=f"chiC{b}", chain=True)
            chi_tok = ("G", s.counts["G"])
            g_dec_toks.append((("V", s.counts["V"]), chi_tok))
            pump(tgen, 8)

            for half in range(2):
                for sq in range(BSEQ):
                    h = 2 * g + half
                    P0, P1 = 64 * half, 64 * half + 64
                    slot = g * 4 + sq
                    tsl = slice(sq * CT, (sq + 1) * CT)
                    vt, ct = g_dec_toks[g]
                    sbf = vop(lambda e, P0=P0, P1=P1, slot=slot: e.tensor_copy(
                        S_bf[P0:P1, :], S_all[P0:P1, slot, :]), tag="sbf", chain=True)
                    bank_mm(B_BUILD, lambda e, P0=P0, P1=P1, tsl=tsl, g=g:
                            e.matmul(B_BUILD.ps[0:128, 256:384],
                                     kap[P0:P1, g, tsl], S_bf[P0:P1, :],
                                     start=True, stop=True),
                            first=True, last=False, waits=[sbf, vt], tag="rt_mm")
                    for b in range(NSB):
                        bank_mm(B_BUILD,
                                lambda e, P0=P0, P1=P1, sq=sq, b=b, g=g:
                                e.matmul(B_BUILD.ps[0:SB * b + SB, 32 * b:32 * b + 32],
                                         chist[P0:P1, sq, CHOFF[b]:CHOFF[b] + SB * b + SB],
                                         kqh[P0:P1, g, sq, b, :],
                                         start=True, stop=True),
                                first=False, last=(b == NSB - 1),
                                waits=[ct], tag=f"bld{b}")
                    consume("vector", B_BUILD, lambda e, sq=sq, h=h:
                            e.scalar_tensor_tensor(
                                out=Mhat[:].rearrange("p (b j) -> p b j", b=NSB),
                                in0=B_BUILD.ps[0:128, 0:256].rearrange(
                                    "p (b x j) -> p b x j", b=NSB, x=2)[:, :, 0, :],
                                scalar=beta_tok[:, sq, h:h + 1],
                                in1=P["maskM"][:].rearrange("p (b j) -> p b j", b=NSB),
                                op0=ALU.mult, op1=ALU.mult),
                            last=False, waits=[btok_tok], tag="mhat")
                    consume("vector", B_BUILD, lambda e, sq=sq, h=h:
                            e.scalar_tensor_tensor(
                                out=Ghat[:].rearrange("p (b j) -> p b j", b=NSB),
                                in0=B_BUILD.ps[0:128, 0:256].rearrange(
                                    "p (b x j) -> p b x j", b=NSB, x=2)[:, :, 1, :],
                                scalar=beta_tok[:, sq, h:h + 1],
                                in1=P["maskG"][:].rearrange("p (b j) -> p b j", b=NSB),
                                op0=ALU.mult, op1=ALU.mult),
                            last=False, tag="ghat")
                    mh_tok = ("V", s.counts["V"])
                    consume("vector", B_BUILD, lambda e, sq=sq, h=h:
                            e.tensor_sub(Rping[0][:],
                                         v_tm[:, sq, h * 128:(h + 1) * 128],
                                         B_BUILD.ps[0:128, 256:384]),
                            last=True, waits=[v_tok], tag="r0")
                    r_tok = ("V", s.counts["V"])
                    bank_mm(B_S1, lambda e: e.transpose(
                        B_S1.psb[0:128, 0:128], Mhat[:], ident_bf[:]),
                        first=True, last=True, waits=[mh_tok], tag="trN")
                    ev = consume("scalar", B_S1, lambda e: e.copy(
                        Yp[0][:], B_S1.psb[0:128, 0:128]), tag="y1_ev")
                    xy_tok = {(0, 'x'): mh_tok, (0, 'y'): ev}
                    sb_banks = [B_S2, B_S1]
                    bi_ = 0
                    for kk in range(1, 7):
                        bank = sb_banks[bi_ % 2]; bi_ += 1
                        bank_mm(bank, lambda e, kk=kk, bank=bank: e.matmul(
                            bank.ps[0:128, 0:128], Yp[kk - 1][:], Xp[kk - 1][:],
                            start=True, stop=True),
                            first=True, last=True,
                            waits=[xy_tok[(kk - 1, 'x')], xy_tok[(kk - 1, 'y')]],
                            tag=f"sqX{kk}")
                        xy_tok[(kk, 'x')] = consume("scalar", bank,
                            lambda e, kk=kk, bank=bank: e.copy(
                                Xp[kk][:], bank.ps[0:128, 0:128]), tag=f"sqXe{kk}")
                        if kk < 6:
                            bank = sb_banks[bi_ % 2]; bi_ += 1
                            bank_mm(bank, lambda e, kk=kk, bank=bank: e.matmul(
                                bank.ps[0:128, 0:128], Xp[kk - 1][:], Yp[kk - 1][:],
                                start=True, stop=True),
                                first=True, last=True, tag=f"sqY{kk}")
                            xy_tok[(kk, 'y')] = consume("scalar", bank,
                                lambda e, kk=kk, bank=bank: e.copy(
                                    Yp[kk][:], bank.ps[0:128, 0:128]),
                                tag=f"sqYe{kk}")
                    rp = 0
                    for kk in range(7, 0, -1):
                        bank = sb_banks[bi_ % 2]; bi_ += 1
                        bank_mm(bank, lambda e, kk=kk, bank=bank, rp=rp: e.matmul(
                            bank.ps[0:128, 0:128], Xp[kk - 1][:], Rping[rp][:],
                            start=True, stop=True),
                            first=True, last=True,
                            waits=[xy_tok[(kk - 1, 'x')], r_tok], tag=f"app{kk}")
                        op = ALU.subtract if kk == 1 else ALU.add
                        r_tok = consume("vector", bank,
                            lambda e, bank=bank, rp=rp, op=op: e.tensor_tensor(
                                Rping[1 - rp][:], Rping[rp][:],
                                bank.ps[0:128, 0:128], op), tag=f"appe{kk}", chain=True)
                        rp = 1 - rp
                    Wt = Rping[rp]
                    bank_mm(B_MISC, lambda e, P0=P0, P1=P1, g=g, tsl=tsl: e.matmul(
                        B_MISC.ps[0:128, 0:128], qtl[P0:P1, g, tsl], S_bf[P0:P1, :],
                        start=True, stop=True),
                        first=True, last=False, waits=[sbf], tag="o1")
                    bank_mm(B_MISC, lambda e, rp=rp: e.matmul(
                        B_MISC.ps[0:128, 128:256], Ghat[:], Rping[rp][:],
                        start=True, stop=True),
                        first=False, last=False, waits=[r_tok], tag="o2")
                    bank_mm(B_MISC, lambda e, P0=P0, P1=P1, g=g, tsl=tsl:
                            e.transpose(B_MISC.psb[0:128, 768:832],
                                        khat[P0:P1, g, tsl],
                                        ident_bf[P0:P1, P0:P1]),
                            first=False, last=True, tag="trk")
                    consume("scalar", B_MISC, lambda e: e.copy(
                        Ocomb[:], B_MISC.ps[0:128, 0:128]), last=False, tag="oc1")
                    consume("vector", B_MISC, lambda e: e.tensor_add(
                        Ocomb[:], Ocomb[:], B_MISC.ps[0:128, 128:256]),
                        last=False, waits=[("S", s.counts["S"])], tag="oc2", chain=True)
                    kt_tok = consume("vector", B_MISC,
                        lambda e, sq=sq, h=h: e.tensor_scalar(
                            out=khatT[:], in0=B_MISC.psb[0:128, 768:832],
                            scalar1=beta_tok[:, sq, h:h + 1], scalar2=None,
                            op0=ALU.mult), last=True, tag="kt_ev", chain=True)
                    oc_tok = vop(lambda e: e.scalar_tensor_tensor(
                        out=On_bf[:], in0=Ocomb[:], scalar=1.0, in1=Ocomb[:],
                        op0=ALU.mult, op1=ALU.mult,
                        accum_out=rms_col[:, 0:1]), tag="osq", chain=True)
                    tq = sop(lambda e: e.activation(
                        out=rms_col[:, 1:2], in_=rms_col[:, 0:1], func=AF.Sqrt,
                        scale=cst[:, 2:3], bias=cst[:, 1:2]),
                        waits=[oc_tok], tag="orms")
                    tq = vop(lambda e: e.reciprocal(rms_col[:, 2:3], rms_col[:, 1:2]),
                             waits=[tq], tag="orec")
                    on_tok = vop(lambda e: e.tensor_scalar(
                        out=On_bf[:], in0=Ocomb[:], scalar1=rms_col[:, 2:3],
                        scalar2=None, op0=ALU.mult), waits=[tq], tag="onorm")
                    bank_mm(B_S1, lambda e: e.transpose(
                        B_S1.psb[0:128, 0:128], On_bf[:], ident_bf[:]),
                        first=True, last=False, waits=[on_tok], tag="trO")
                    bank_mm(B_S1, lambda e, P0=P0, P1=P1, rp=rp: e.matmul(
                        B_S1.ps[P0:P1, 128:256], khatT[:], Rping[rp][:],
                        start=True, stop=True),
                        first=False, last=True, waits=[kt_tok], tag="smm")
                    consume("vector", B_S1, lambda e, h=h, sq=sq, tsl=tsl:
                            e.tensor_scalar(
                                out=y_bf[:, h, tsl], in0=B_S1.psb[0:128, 0:128],
                                scalar1=P["whn"][:, h:h + 1], scalar2=None,
                                op0=ALU.mult), last=False, tag="y_ev")
                    consume("vector", B_S1, lambda e, P0=P0, P1=P1, slot=slot, sq=sq:
                            e.scalar_tensor_tensor(
                                out=S_all[P0:P1, slot, :],
                                in0=S_all[P0:P1, slot, :],
                                scalar=pcum[P0:P1, sq, NSB - 1, SB - 1:SB],
                                in1=B_S1.ps[P0:P1, 128:256],
                                op0=ALU.mult, op1=ALU.add),
                            last=True, tag="s_upd")
                    pump(tgen, 4)
        y_tok = ("V", s.counts["V"])
        emit_dbg("y", [y_tok])

        y2_toks = []
        last_gate_read = None
        for ot in range(8):
            bank = banks[4 + ot % 2]
            buf, ltok, bi = stripe_load("wout", ot, 0, tag="wo")
            for ki in range(8):
                bank_mm(bank, lambda e, buf=buf, ki=ki, bank=bank: e.matmul(
                    bank.ps[0:128, 0:NT], buf[:, ki * 128:(ki + 1) * 128],
                    y_bf[:, ki, :], start=(ki == 0), stop=(ki == 7)),
                    first=(ki == 0), last=(ki == 7), force_inc=(ki == 7),
                    waits=[ltok, y_tok], tag=f"wo_mm{ot}_{ki}")
            stripe_used(bi)
            gbank = banks[6 + ot % 2]
            bufg, ltokg, big = stripe_load("wu2", (ot // 2) * 2, 0, tag="wu2")
            bank_mm(gbank, lambda e, bufg=bufg, gbank=gbank, ot=ot: e.matmul(
                gbank.ps[0:128, 0:NT],
                bufg[0:64, (ot % 2) * 128:(ot % 2) * 128 + 128], g1_bf[:, :],
                start=True, stop=True),
                first=True, last=True, waits=[ltokg, g1_tok], tag=f"gt_mm{ot}")
            stripe_used(big)
            consume("scalar", gbank, lambda e, ot=ot, gbank=gbank: e.activation(
                out=gate_sc[:], in_=gbank.ps[0:128, 0:NT], func=AF.Sigmoid,
                bias=P["bu2"][:, ot:ot + 1]),
                waits=[last_gate_read], tag="gate")
            gate_tok = ("S", s.counts["S"])
            last_gate_read = consume("vector", bank,
                lambda e, ot=ot, bank=bank: e.tensor_mul(
                    y2[:, ot, :], bank.ps[0:128, 0:NT], gate_sc[:]),
                waits=[gate_tok], tag="y2a", chain=True)
            y2_toks.append(vop(lambda e, ot=ot: e.tensor_add(
                y2[:, ot, :], y2[:, ot, :], xn[:, ot, :]),
                waits=[xn_toks[ot]], tag="y2b", chain=True))
            pump(tgen, 4)
        emit_dbg("y2", [y2_toks[-1]])
        emit_dbg("gate", [("S", s.counts["S"])])

        pump(tgen, 10 ** 6)
        bs = B_BC[0]
        sqts = []
        for c in range(8):
            w = [y2_toks[c]]
            if c >= 2:
                w.append(sqts[c - 2][1])
            tsq = sop(lambda e, c=c: e.activation(
                out=scr2[:, c % 2, :], in_=y2[:, c, :], func=AF.Square),
                waits=w, tag="h2sq")
            bank_mm(bs, lambda e, c=c, bs=bs: e.matmul(
                bs.ps[0:1, 0:NT], P["onescol"], scr2[:, c % 2, :],
                start=(c == 0), stop=(c == 7)),
                first=(c == 0), last=(c == 7), force_inc=True,
                waits=[tsq], tag="h2sum")
            sqts.append((tsq, ("T", s.counts["T"])))
        consume("vector", bs, lambda e, bs=bs: e.tensor_scalar(
            out=stat_m[:], in0=bs.ps[0:1, 0:NT], scalar1=1.0 / D, scalar2=None,
            op0=ALU.mult), tag="h2_ms")
        tv = sop(lambda e: e.activation(out=stat_m[:], in_=stat_m[:], func=AF.Sqrt,
                                        bias=cst[0:1, 1:2]), waits=[("V", s.counts["V"])],
                 tag="h2_sd")
        tv = vop(lambda e: e.reciprocal(stat_m[:], stat_m[:]), waits=[tv],
                 tag="h2_rs", chain=True)
        br = B_BC[1]
        bank_mm(br, lambda e, br=br: e.matmul(
            br.ps[0:128, 0:NT], P["onesrow"][0:1, :], stat_m[:],
            start=True, stop=True), first=True, last=True, waits=[tv], tag="bc_h2")
        h2_toks = []
        for c in range(8):
            h2_toks.append(consume("vector", br,
                lambda e, c=c, br=br: e.scalar_tensor_tensor(
                    out=xn_bf[:, c, :], in0=y2[:, c, :],
                    scalar=P["wffn"][:, c:c + 1], in1=br.ps[0:128, 0:NT],
                    op0=ALU.mult, op1=ALU.mult),
                last=(c == 7), waits=[y2_toks[c]], tag="h2"))
        h2_all = h2_toks[-1]

        out_toks = [None] * 8
        def swiglu(half):
            tsl = slice(half * NTH, (half + 1) * NTH)
            def ff1_cb(ot, bank):
                consume("scalar", bank, lambda e, ot=ot, bank=bank: e.activation(
                    out=a_big[:, ot, :], in_=bank.ps[0:128, 0:NTH],
                    func=AF.Silu),
                    waits=[dbg_stage[1]] if dbg_stage[1] else [], tag="ff1_ev")
            proj("wff1", lambda ki: xn_bf[:, ki, tsl], ff1_cb,
                 mov_toks=[h2_all] + tern_state, nt=NTH, tag=f"ff1h{half}_")
            s1_tok = ("S", s.counts["S"])
            def ff3_cb(ot, bank):
                consume("vector", bank, lambda e, ot=ot, bank=bank: e.tensor_mul(
                    a_big[:, ot, :], a_big[:, ot, :], bank.ps[0:128, 0:NTH]),
                    waits=[s1_tok], tag="ff3_ev")
            proj("wff3", lambda ki: xn_bf[:, ki, tsl], ff3_cb, nt=NTH,
                 tag=f"ff3h{half}_")
            a1_tok = ("V", s.counts["V"])
            def ff2_cb(ot, bank):
                consume("vector", bank, lambda e, ot=ot, bank=bank, tsl=tsl:
                        e.tensor_add(OUT[:, ot, tsl], OUT[:, ot, tsl],
                                     bank.ps[0:128, 0:NTH]),
                        waits=[a1_tok], tag="ff2_ev", chain=True)
                out_toks[ot] = vop(lambda e, ot=ot, tsl=tsl: e.tensor_add(
                    OUT[:, ot, tsl], OUT[:, ot, tsl], y2[:, ot, tsl]),
                    tag="outy2", chain=True)
            proj("wff2", lambda ki: a_big[:, ki, :], ff2_cb,
                 mov_toks=[a1_tok], nt=NTH, tag=f"ff2h{half}_")

        swiglu(0)
        swiglu(1)

        otoks = []
        for c in range(8):
            otoks.append(s.op("sync",
                lambda e, c=c, ti=ti: e.dma_start(
                    out=out_fm[c, :, ti],
                    in_=OUT[:, c, :].rearrange("p (s t) -> p s t", s=BSEQ)),
                waits=[out_toks[c]], inc="DO", n=16, tag=f"out_{ti}_{c}"))
        out_final_toks.append(otoks[-1])

    fw = [("DO", s.counts["DO"])]
    if s.counts.get("DD"):
        fw.append(("DD", s.counts["DD"]))
    s.op("sync", lambda e: e.nop(), waits=fw, tag="fin")

    with nc.Block() as block:
        @block.tensor
        def _(eng):
            s.replay("tensor", eng)

        @block.vector
        def _(eng):
            s.replay("vector", eng)

        @block.scalar
        def _(eng):
            s.replay("scalar", eng)

        @block.gpsimd
        def _(eng):
            s.replay("gpsimd", eng)

        @block.sync
        def _(eng):
            s.replay("sync", eng)

    return nc, dbg_out



def _consts():
    i128 = np.arange(128)
    maskM = (i128[:, None] < i128[None, :]).astype(np.float32)
    maskG = (i128[:, None] <= i128[None, :]).astype(np.float32)
    seg8 = np.zeros((128, 32), np.float32)
    headmap = np.zeros((8, 512), np.float32)
    for g in range(4):
        for p in range(128):
            h = 2 * g + p // 64
            seg8[p, 8 * g + h] = 1.0
            headmap[h, g * 128 + p] = 1.0
    return {
        "ident": np.eye(128, dtype=np.float32),
        "maskM": maskM.astype(BF), "maskG": maskG.astype(BF),
        "seg8": seg8, "headmap": headmap,
        "onesrow": np.ones((1, 128), np.float32),
    }


def _smalls(inp):
    f = np.float32
    fm8 = lambda v: np.ascontiguousarray(np.asarray(v, f).reshape(8, 128).T)
    out = {
        "wni": fm8(inp["L0_attn_w_norm_in"]),
        "convb": fm8(inp["L0_attn_conv_b"]),
        "convw": np.ascontiguousarray(
            np.asarray(inp["L0_attn_conv_w"], f).reshape(8, 128, 4)
            .transpose(1, 0, 2).reshape(128, 32)),
        "bau": np.asarray(inp["L0_attn_b_au"], f).reshape(64, 1),
        "bad": np.ascontiguousarray(np.asarray(inp["L0_attn_b_ad"], f).reshape(4, 128).T),
        "bbeta": np.asarray(inp["L0_attn_b_beta"], f).reshape(8, 1),
        "bu1": np.asarray(inp["L0_attn_b_u1"], f).reshape(64, 1),
        "bu2": fm8(inp["L0_attn_b_u2"]),
        "wffn": fm8(inp["L0_attn_w_ffn"]),
        "whn": np.ascontiguousarray(np.asarray(inp["L0_attn_w_hn"], f).T),
        "s1": np.ascontiguousarray(np.asarray(inp["L0_ffn_w1_s"], f).reshape(32, 128).T),
        "s2": np.ascontiguousarray(np.asarray(inp["L0_ffn_w2_s"], f).reshape(32, 128).T),
        "hs1": np.ascontiguousarray(np.asarray(inp["L0_ffn_h1_s"], f).reshape(32, 128).T),
        "hs2": np.ascontiguousarray(np.asarray(inp["L0_ffn_h2_s"], f).reshape(32, 128).T),
        "s3": np.ascontiguousarray(np.asarray(inp["L0_ffn_w3_s"], f).reshape(8, 128).T),
        "asp": np.ascontiguousarray(
            np.asarray(inp["L0_attn_alpha_spike"], f).reshape(4, 128).T),
    }
    bs = np.asarray(inp["L0_attn_beta_spike"], f).reshape(8, 64)
    bseg8 = np.zeros((128, 32), np.float32)
    for g in range(4):
        bseg8[0:64, 8 * g + 2 * g] = bs[2 * g]
        bseg8[64:128, 8 * g + 2 * g + 1] = bs[2 * g + 1]
    out["bseg8"] = bseg8
    out.update(_consts())
    return out


def _weights(inp):
    f = np.float32
    a = lambda k: np.asarray(inp[k], f)
    t = lambda k: np.ascontiguousarray(np.asarray(inp[k], f).T)
    return {
        "wq": a("L0_attn_wq"), "wk": a("L0_attn_wk"), "wv": a("L0_attn_wv"),
        "wsp": a("L0_attn_w_spike"), "wau": a("L0_attn_w_au"),
        "wad": a("L0_attn_w_ad"), "wbeta": a("L0_attn_w_beta"),
        "wu1": a("L0_attn_w_u1"), "wu2": a("L0_attn_w_u2"),
        "wout": a("L0_attn_w_out"), "wff1": a("L0_attn_w_ff1"),
        "wff3": a("L0_attn_w_ff3"), "wff2": a("L0_attn_w_ff2"),
        "w1t": t("L0_ffn_w1_w"), "w2t": t("L0_ffn_w2_w"),
        "h1t": t("L0_ffn_h1_w"), "h2t": t("L0_ffn_h2_w"),
        "w3t": t("L0_ffn_w3_w"),
    }


def x_layout(xs):
    B, T, Dm = xs.shape
    ntl = T // CT
    v = xs.transpose(2, 0, 1).reshape(8, 128, B, ntl, CT)
    return np.ascontiguousarray(v.transpose(0, 1, 3, 2, 4))


def out_delayout(o):
    _, _, ntl, B, _ = o.shape
    return o.transpose(3, 2, 4, 0, 1).reshape(B, ntl * CT, 8 * 128)


def kernel(**inputs):
    x = np.asarray(inputs["L0_attn_x"], np.float32)
    B, T, Dm = x.shape
    smalls = {"p_" + k: v for k, v in _smalls(inputs).items()}
    weights = {"w_" + k: v for k, v in _weights(inputs).items()}
    nc, _ = build_program(T=T)
    in_maps = []
    for c in range(NCORES):
        m = dict(smalls)
        m.update(weights)
        m["x_fm"] = x_layout(x[BSEQ * c:BSEQ * (c + 1)])
        in_maps.append(m)
    res = run_bass_kernel_spmd(nc, in_maps, core_ids=list(range(NCORES)))
    outs = [out_delayout(r["out_fm"]) for r in res.results]
    return np.ascontiguousarray(np.concatenate(outs, axis=0))
